# version 12
# baseline (speedup 1.0000x reference)
import numpy as np
import ml_dtypes
import concourse.bass as bass
import concourse.mybir as mybir
from concourse.bass_utils import run_bass_kernel_spmd

F32 = mybir.dt.float32
BF16 = mybir.dt.bfloat16
AF = mybir.ActivationFunctionType
ALU = mybir.AluOpType
AX = mybir.AxisListType

SEQ, DM, H, DH, DFF = 4096, 1024, 16, 64, 4096
C = 128
TILE = 512
T2 = 512
EPS_LN = 1e-5
EPS_RMS = float(np.finfo(np.float32).eps)
EPS_TR = 1e-8
BIGNEG = -1e6

ENGS = ("sync", "gpsimd", "tensor", "vector", "scalar")
DMA_ENGS = ("sync", "gpsimd")
DMA_QUEUES = ("sync", "gpsimd", "scalar")


class Plan:

    def __init__(self):
        self.ops = []
        self.last_writer = {}
        self.readers = {}

    def op(self, eng, fn, reads=(), writes=(), kind=None, pad=0):
        idx = len(self.ops)
        if kind is None:
            kind = "dma" if eng in DMA_ENGS else "comp"
        waits = set()
        soft = set()
        for b in reads:
            for e2, w in self.last_writer.get(b, {}).items():
                (waits if e2 != eng else soft).add(w)
            self.readers.setdefault(b, {})[eng] = idx
        for b in writes:
            for e2, w in self.last_writer.get(b, {}).items():
                if w != idx:
                    (waits if e2 != eng else soft).add(w)
            for e2, r in self.readers.get(b, {}).items():
                if r != idx:
                    (waits if e2 != eng else soft).add(r)
            self.readers[b] = {eng: idx}
            self.last_writer[b] = {eng: idx}
        self.ops.append([eng, fn, waits, kind, soft, pad])
        return idx

    def barrier(self):
        lasts = {}
        for i, op_ in enumerate(self.ops):
            e = op_[0]
            lasts[e] = i
        fence = {}
        for e in ENGS:
            if e not in lasts:
                continue
            deps = {lasts[e2] for e2 in lasts if e2 != e}
            fence[e] = len(self.ops)
            self.ops.append([e, lambda eng: eng.nop(), deps, "comp", set(), 0])
        self.last_writer = {}
        self.readers = {}
        for e, i in fence.items():
            self.last_writer.setdefault("__barrier__", {})[e] = i
        return fence

    def emit(self, nc, padbuf):
        plan_ops = self.ops
        marked = set()
        for eng, fn, waits, kind, soft, pad in plan_ops:
            marked |= waits
        for i, (eng, fn, waits, kind, soft, pad) in enumerate(plan_ops):
            if kind == "dma":
                marked.add(i)
        semval = {}
        opval = {}
        for i, (eng, fn, waits, kind, soft, pad) in enumerate(plan_ops):
            if i in marked:
                key = (eng, kind)
                semval[key] = semval.get(key, 0) + (16 if kind == "dma" else 1)
                opval[i] = semval[key]
        sems = {}
        for (eng, kind) in semval:
            sems[(eng, kind)] = nc.semaphore(f"sem_{eng}_{kind}").__enter__()
        dma_prefix = {e: [] for e in DMA_QUEUES}
        for i, (eng, fn, waits, kind, soft, pad) in enumerate(plan_ops):
            if kind == "dma" and i in marked:
                dma_prefix[eng].append((i, opval[i]))

        def dma_wait_val(eng, before_idx):
            import bisect
            lst = dma_prefix[eng]
            pos = bisect.bisect_left(lst, (before_idx, -1))
            return lst[pos - 1][1] if pos > 0 else 0

        thresholds = {e: set() for e in DMA_QUEUES}
        for i, (eng, fn, waits, kind, soft, pad) in enumerate(plan_ops):
            for w in waits:
                if plan_ops[w][3] == "dma" and plan_ops[w][0] != eng:
                    thresholds[plan_ops[w][0]].add(dma_wait_val(plan_ops[w][0], i))
        thr_sorted = {e: sorted(v) for e, v in thresholds.items()}

        def body_for(eng_name):
            def body(eng):
                import bisect
                waited = {}
                self_waited = 0
                cum = 0
                for i, (e, fn, waits, kind, soft, pad) in enumerate(plan_ops):
                    if e != eng_name:
                        continue
                    allw = sorted(waits) + sorted(w for w in soft if w in marked)
                    for w in allw:
                        we, wkind = plan_ops[w][0], plan_ops[w][3]
                        if wkind == "dma":
                            v = dma_wait_val(we, i) if we != eng_name else opval[w]
                            skey = (we, "dma")
                        else:
                            v = opval[w]
                            skey = (we, "comp")
                        if v == 0 or waited.get(skey, 0) >= v:
                            continue
                        eng.wait_ge(sems[skey], v)
                        waited[skey] = v
                    if kind == "dma" and i in marked:
                        lst = thr_sorted[eng_name]
                        pos = bisect.bisect_right(lst, cum)
                        tstar = lst[pos - 1] if pos > 0 else 0
                        if tstar > self_waited:
                            eng.wait_ge(sems[(eng_name, "dma")], tstar)
                            self_waited = tstar
                    inst = fn(eng)
                    if pad and eng_name == "vector":
                        inst = eng.tensor_copy(padbuf[:, 1:601:2], padbuf[:, 0:600:2])
                    elif pad and eng_name == "scalar":
                        inst = eng.activation(padbuf[:, 1:601:2], padbuf[:, 0:600:2],
                                              mybir.ActivationFunctionType.Copy)
                    if i in marked:
                        amt = 16 if kind == "dma" else 1
                        inst.then_inc(sems[(eng_name, kind)], amt)
                        if kind == "dma":
                            cum += amt
            return body

        blk = nc.Block().__enter__()
        for e in ENGS:
            if any(o[0] == e for o in plan_ops):
                getattr(blk, e)(body_for(e))
        blk.__exit__(None, None, None)


def _consts():
    bf = ml_dtypes.bfloat16
    id128 = np.eye(128, dtype=bf)
    idf32 = np.eye(128, dtype=np.float32)
    blk = np.where(np.arange(128)[:, None] > np.arange(128)[None, :], BIGNEG, 0.0)
    negbig = np.tile(blk, (1, 4)).astype(bf)
    eg = np.zeros((16, 1024), dtype=bf)
    for g in range(8):
        for pp in range(128):
            eg[2 * g + pp // 64, 128 * g + pp] = 1
    p2 = np.zeros((16, 128), dtype=np.float32)
    for h in range(16):
        for pp in range(128):
            if h % 2 == pp // 64:
                p2[h, pp] = 1
    dg = np.zeros((16, 8), dtype=np.float32)
    for h in range(16):
        dg[h, h // 2] = 1
    onesbf = np.ones((128, 1), dtype=bf)
    onesf11 = np.ones((1, 1), dtype=np.float32)
    onesf128 = np.ones((128, 1), dtype=np.float32)
    onesf1x = np.ones((1, 128), dtype=np.float32)
    cb = np.ones((16, 512), dtype=np.float32)
    cb[:, 0::128] = 0.0
    return dict(id128=id128, idf32=idf32, negbig=negbig, eg=eg, p2=p2, dg=dg,
                onesbf=onesbf, onesf11=onesf11, onesf128=onesf128,
                onesf1x=onesf1x, cb=cb)


def build_program(n_tiles=8, phase2=True, taps=(), scan_chunks=4, scan_mask=240767):
    nc = bass.Bass("TRN2", target_bir_lowering=False, detect_race_conditions=False)
    p = Plan()
    cst = _consts()
    NT = n_tiles
    NT2 = SEQ // T2 if phase2 else 0

    x_h = nc.dram_tensor("x", [SEQ, DM], F32, kind="ExternalInput").ap()
    w_h = {}
    for nm, shp in [("wq", [DM, DM]), ("wk", [DM, DM]), ("wv", [DM, DM]),
                    ("wo", [DM, DM]), ("wg", [DM, DM]), ("wog", [DM, DM]),
                    ("wd", [H, DM]), ("wup", [DFF, DM]), ("wdown", [DM, DFF])]:
        w_h[nm] = nc.dram_tensor(nm, shp, F32, kind="ExternalInput").ap()
    bd_h = nc.dram_tensor("bd", [H, 1], F32, kind="ExternalInput").ap()
    nw_h = nc.dram_tensor("normw", [8, 128], F32, kind="ExternalInput").ap()
    out_h = nc.dram_tensor("out", [SEQ, DM], F32, kind="ExternalOutput").ap()

    dr = {}
    for nm in ["wq", "wk", "wv", "wo", "wg", "wog"]:
        dr[nm] = nc.dram_tensor(f"{nm}_bf", [DM, DM], BF16, kind="Internal").ap()
        dr[f"{nm}T"] = nc.dram_tensor(f"{nm}_bfT", [DM, DM], BF16, kind="Internal").ap()
    dr["wd"] = nc.dram_tensor("wd_bf", [H, DM], BF16, kind="Internal").ap()
    dr["xn"] = nc.dram_tensor("xn_sc", [SEQ, DM], BF16, kind="Internal").ap()
    dr["xnn"] = nc.dram_tensor("xnn_sc", [SEQ, DM], BF16, kind="Internal").ap()
    dr["s"] = nc.dram_tensor("s_sc", [SEQ, DM], F32, kind="Internal").ap()
    dr["ast"] = nc.dram_tensor("ast_sc", [8 * 64, TILE], BF16, kind="Internal").ap()
    dr["wutT"] = nc.dram_tensor("wutT_sc", [DM, DFF], BF16, kind="Internal").ap()
    dr["wdtT"] = nc.dram_tensor("wdtT_sc", [DFF, DM], BF16, kind="Internal").ap()

    cst_h = {k: nc.inline_tensor(v, name=f"c_{k}").ap() for k, v in cst.items()}

    sb = {}

    def alloc(name, shape, dtype):
        sb[name] = nc.alloc_sbuf_tensor(name, shape, dtype)

    for k, v in cst.items():
        alloc(k, list(v.shape), BF16 if v.dtype == ml_dtypes.bfloat16 else F32)
    alloc("nwrows", [8, 128], F32)
    alloc("nwfm", [128, 8], F32)
    alloc("bdcol", [H, 1], F32)
    alloc("wbuf0", [128, 8192], BF16)
    alloc("wbuf1", [128, 8192], BF16)
    alloc("wdT", [128, 128], BF16)
    for nm in ["sums", "ssqc", "mu", "var", "rstd4", "c4", "rc4", "nb1", "nb2",
               "t4a", "t4b", "t4c"]:
        alloc(nm, [128, 4], F32)
    alloc("xntm1", [128, DM], BF16)
    for i in range(2):
        alloc(f"xnntm{i}", [128, DM], BF16)
    pool = nc.alloc_sbuf_tensor("pool", [128, 57344], mybir.dt.uint8)
    pool_base = nc.lookup_mloc(pool).addr

    def alloc_at(name, shape, dtype, off):
        sb[name] = nc.alloc_sbuf_tensor_at(name, shape, dtype, offset=pool_base + off)

    alloc("qfm", [128, 8 * 512], BF16)
    alloc("kgfm", [128, 8 * 512], BF16)
    alloc_at("vtm", [128, 4 * 1024], BF16, 0)
    alloc_at("sigg", [128, 8 * 512], BF16, 8192)
    alloc_at("sigog", [128, 8 * 512], BF16, 16384)
    alloc_at("y1", [128, 8 * 512], BF16, 24576)
    alloc_at("xnfm", [128, 8 * 512], BF16, 32768)
    alloc_at("repb", [128, 2048], BF16, 40960)
    alloc_at("khfm", [128, 8 * 128], BF16, 45056)
    alloc_at("khtm", [128, 8 * 128], BF16, 47104)
    alloc_at("dbuf", [128, 512], F32, 49152)
    alloc_at("pbuf", [128, 512], BF16, 51200)
    alloc_at("ahib", [16, 512], BF16, 52224)
    alloc_at("alob", [16, 512], BF16, 53248)
    alloc_at("nahib", [16, 512], BF16, 54272)
    alloc_at("nalob", [16, 512], BF16, 55296)
    alloc_at("hfm", [128, 32 * 512], BF16, 0)
    alloc_at("xnnt", [128, 8 * 512], BF16, 32768)
    alloc_at("dsub", [128, 4 * 1024], F32, 40960)
    pool2 = nc.alloc_sbuf_tensor("pool2", [128, 40960], mybir.dt.uint8)
    pool2_base = nc.lookup_mloc(pool2).addr

    def alloc_at2(name, shape, dtype, off):
        sb[name] = nc.alloc_sbuf_tensor_at(name, shape, dtype, offset=pool2_base + off)

    alloc_at2("sev0", [128, DM], F32, 0)
    alloc_at2("sev1", [128, DM], F32, 4096)
    alloc_at2("wfp0", [128, DM], F32, 8192)
    alloc_at2("wfp1", [128, DM], F32, 12288)
    alloc_at2("xb0", [128, DM], F32, 16384)
    alloc_at2("xb1", [128, DM], F32, 20480)
    alloc_at2("xb2", [128, DM], F32, 24576)
    alloc("xb3", [128, DM], F32)
    alloc_at2("junk", [128, DM], F32, 28672)
    alloc_at2("terna", [128, DM], BF16, 32768)
    alloc_at2("ternb", [128, DM], BF16, 34816)
    alloc_at2("ternw", [128, DM], BF16, 36864)
    alloc_at2("xntm0", [128, DM], BF16, 38912)
    alloc_at2("tnh", [128, 1024], F32, 0)
    alloc_at2("sgt", [128, 1024], F32, 4096)
    alloc_at2("gt", [128, 1024], F32, 8192)
    alloc_at2("ssb0", [128, DM], F32, 12288)
    alloc_at2("ssb1", [128, DM], F32, 16384)
    alloc_at2("xsb0", [128, DM], F32, 20480)
    alloc_at2("xsb1", [128, DM], F32, 24576)
    alloc_at2("osb0", [128, DM], F32, 28672)
    alloc_at2("osb1", [128, DM], F32, 32768)
    alloc_at2("wdnr0", [128, 512], BF16, 36864)
    alloc_at2("wdnr1", [128, 512], BF16, 37888)
    alloc_at2("wdnr2", [128, 512], BF16, 38912)
    alloc_at2("wdnr3", [128, 512], BF16, 39936)
    for i in range(2):
        alloc(f"tanhg{i}", [128, 512], BF16)
    alloc("zb", [16, 512], F32)
    alloc("spa", [16, 512], F32)
    alloc("spb", [16, 512], F32)
    alloc("gfm", [16, 512], F32)
    alloc("afm", [16, 512], F32)
    alloc("ahi32", [16, 512], F32)
    alloc("stgL", [4, 2048], BF16)
    alloc("stgR", [4, 2048], BF16)
    alloc("tmpw", [16, 512], F32)
    alloc("expaw", [16, 512], F32)
    alloc("emaw", [16, 512], F32)
    alloc("expbf", [16, 1024], BF16)
    alloc("eAdall", [16, 32], F32)
    alloc("eAcol", [128, 32], F32)
    alloc("qtil", [128, 8 * 128], BF16)
    alloc("sqb", [128, 1024], BF16)
    alloc("ssqrow", [1, 512], F32)
    alloc("S32", [128, 8 * 64], F32)
    alloc("Sbf", [128, 8 * 64], BF16)
    alloc("rstdrow", [1, 512], F32)
    alloc("rcol4", [128, 4], F32)
    alloc("acc_up", [128, 1], F32)
    alloc("acc_dn", [128, 1], F32)
    alloc("red1", [128, 1], F32)
    alloc("thrc", [128, 2], F32)
    alloc("nthrc", [128, 2], F32)
    alloc("thr11", [1, 2], F32)
    alloc("tern_t", [128, 512], BF16)
    alloc("zrow", [1, 512], F32)
    alloc("padbuf", [128, 608], BF16)
    alloc("zc4", [128, 4], F32)

    ps = {}
    for nm in ["pbig0", "pbig1", "psc", "pt1", "po", "po2", "psmall"]:
        ps[nm] = nc.alloc_psum_tensor(nm, [128, 512], F32)
    ps["ptr"] = nc.alloc_psum_tensor("ptr", [128, 512], BF16)

    P = p.op

    _f32_consts = [1.0 / DM, EPS_LN, EPS_TR, -1.0, 0.0, 0.5, 1.0, 0.5 / (DFF * DM)]
    _bf_consts = [-1.0, 0.5, 1.0, 0.0]
    alloc("cvf", [128, len(_f32_consts)], F32)
    alloc("cvb", [128, len(_bf_consts)], BF16)
    for i, v in enumerate(_f32_consts):
        P("vector", (lambda i=i, v=v: (lambda e: e.memset(sb["cvf"][:, i:i + 1], v)))(),
          writes=("cvf",))
        nc.const_aps.register(F32, v, sb["cvf"][:, i:i + 1])
    for i, v in enumerate(_bf_consts):
        P("vector", (lambda i=i, v=v: (lambda e: e.memset(sb["cvb"][:, i:i + 1], v)))(),
          writes=("cvb",))
        nc.const_aps.register(BF16, v, sb["cvb"][:, i:i + 1])

    for k in cst:
        P("sync", (lambda k=k: (lambda e: e.dma_start(out=sb[k][:, :], in_=cst_h[k][:, :])))(),
          writes=(k,))
    P("sync", lambda e: e.dma_start(out=sb["bdcol"][:, :], in_=bd_h[:, :]), writes=("bdcol",))
    P("sync", lambda e: e.dma_start(out=sb["nwrows"][:, :], in_=nw_h[:, :]), writes=("nwrows",))
    P("tensor", lambda e: e.transpose(ps["psmall"][:, 0:8], sb["nwrows"][:, :], sb["idf32"][0:8, 0:8]),
      reads=("nwrows", "idf32"), writes=("psmall",))
    P("vector", lambda e: e.tensor_copy(sb["nwfm"][:, :], ps["psmall"][:, 0:8]),
      reads=("psmall",), writes=("nwfm",))
    for nm in ["wq", "wk", "wv", "wo", "wg", "wog", "wd"]:
        P("gpsimd", (lambda nm=nm: (lambda e: e.dma_start(out=dr[nm][:, :], in_=w_h[nm][:, :])))(),
          writes=(f"dr_{nm}",))
    for k in range(8):
        P("sync", (lambda k=k: (lambda e: e.dma_start(
            out=sb["wdT"][:, 16 * k:16 * k + 16],
            in_=dr["wd"][:, 128 * k:128 * k + 128], transpose=True)))(),
          reads=("dr_wd",), writes=("wdT",))
    P("vector", lambda e: e.memset(sb["S32"][:, :], 0.0), writes=("S32",))
    P("vector", lambda e: e.memset(sb["Sbf"][:, :], 0.0), writes=("Sbf",))
    P("vector", lambda e: e.memset(sb["stgL"][:, :], 1.0), writes=("stgL",))
    P("vector", lambda e: e.memset(sb["stgR"][:, :], 1.0), writes=("stgR",))
    P("vector", lambda e: e.memset(sb["acc_up"][:, :], 0.0), writes=("acc_up",))
    P("vector", lambda e: e.memset(sb["acc_dn"][:, :], 0.0), writes=("acc_dn",))
    P("vector", lambda e: e.memset(sb["ssqrow"][:, :], 0.0), writes=("ssqrow",))
    P("vector", lambda e: e.memset(sb["padbuf"][:, :], 0.0), writes=("padbuf",))
    P("vector", lambda e: e.memset(sb["y1"][:, :], 0.0), writes=("y1",))
    p.barrier()

    for c in range(4):
        P("sync", (lambda c=c: (lambda e: e.dma_start(
            out=sb[f"xb{c}"][:, :], in_=x_h[128 * c:128 * c + 128, :])))(),
          writes=(f"xb{c}",))

    def pretranspose_weights():
        for wi, nm in enumerate(["wo", "wg", "wk", "wq", "wog", "wv"]):
            buf = "wbuf0" if wi % 2 == 0 else "wbuf1"
            for k in range(8):
                P("sync", (lambda nm=nm, buf=buf, k=k: (lambda e: e.dma_start(
                    out=sb[buf][:, 1024 * k:1024 * k + 1024],
                    in_=dr[nm][:, 128 * k:128 * k + 128], transpose=True)))(),
                  reads=(f"dr_{nm}",), writes=(buf,))
            for k in range(8):
                P("scalar", (lambda nm=nm, buf=buf, k=k: (lambda e: e.dma_start(
                    out=dr[f"{nm}T"][128 * k:128 * k + 128, :],
                    in_=sb[buf][:, 1024 * k:1024 * k + 1024])))(),
                  reads=(buf,), writes=(f"dr_{nm}T",), kind="dma")

    def stream_weight(nm, buf, direct=False):
        for k in range(8):
            if direct:
                P("sync", (lambda nm=nm, buf=buf, k=k: (lambda e: e.dma_start(
                    out=sb[buf][:, 1024 * k:1024 * k + 1024],
                    in_=dr[nm][:, 128 * k:128 * k + 128], transpose=True)))(),
                  reads=(f"dr_{nm}",), writes=(buf,))
            else:
                P("sync", (lambda nm=nm, buf=buf, k=k: (lambda e: e.dma_start(
                    out=sb[buf][:, 1024 * k:1024 * k + 1024],
                    in_=dr[f"{nm}T"][128 * k:128 * k + 128, :])))(),
                  reads=(f"dr_{nm}T",), writes=(buf,))

    def ln_chunk(t, c):
        ci = 4 * t + c
        xb = f"xb{ci % 4}"
        if t > 1:
            P("sync", (lambda ci=ci, xb=xb: (lambda e: e.dma_start(
                out=sb[xb][:, :], in_=x_h[128 * ci:128 * ci + 128, :])))(),
              writes=(xb,))
        P("vector", (lambda xb=xb, c=c: (lambda e: e.reduce_sum(
            sb["sums"][:, c:c + 1], sb[xb][:, :], axis=AX.X)))(),
          reads=(xb,), writes=("sums",))
        P("scalar", (lambda xb=xb: (lambda e: e.activation(
            sb["junk"][:, :], sb[xb][:, :], AF.Square)))(),
          reads=(xb,), writes=("junk",))
        P("vector", (lambda c=c: (lambda e: e.reduce_sum(
            sb["ssqc"][:, c:c + 1], sb["junk"][:, :], axis=AX.X)))(),
          reads=("junk",), writes=("ssqc",))

    def ln_stats():
        P("vector", lambda e: e.tensor_scalar(sb["mu"][:, :], sb["sums"][:, :],
                                              1.0 / DM, None, ALU.mult),
          reads=("sums",), writes=("mu",), pad=500)
        P("vector", lambda e: e.tensor_tensor(sb["t4a"][:, :], sb["mu"][:, :],
                                              sb["mu"][:, :], ALU.mult),
          reads=("mu",), writes=("t4a",), pad=500)
        P("vector", lambda e: e.scalar_tensor_tensor(
            sb["var"][:, :], sb["ssqc"][:, :], 1.0 / DM, sb["t4a"][:, :],
            ALU.mult, ALU.subtract),
          reads=("ssqc", "t4a"), writes=("var",), pad=500)
        P("scalar", lambda e: e.activation(sb["t4b"][:, :], sb["var"][:, :], AF.Ln,
                                           bias=np.float32(EPS_LN)),
          reads=("var",), writes=("t4b",), pad=500)
        P("scalar", lambda e: e.activation(sb["rstd4"][:, :], sb["t4b"][:, :], AF.Exp,
                                           scale=np.float32(-0.5)),
          reads=("t4b",), writes=("rstd4",), pad=500)
        P("scalar", lambda e: e.activation(sb["t4a"][:, :], sb["var"][:, :], AF.Ln),
          reads=("var",), writes=("t4a",), pad=500)
        P("vector", lambda e: e.tensor_tensor(sb["t4a"][:, :], sb["t4a"][:, :],
                                              sb["t4b"][:, :], ALU.subtract),
          reads=("t4a", "t4b"), writes=("t4a",), pad=500)
        P("scalar", lambda e: e.activation(sb["t4c"][:, :], sb["t4a"][:, :], AF.Exp,
                                           scale=np.float32(0.5)),
          reads=("t4a",), writes=("t4c",), pad=500)
        P("scalar", lambda e: e.activation(sb["t4a"][:, :], sb["t4c"][:, :], AF.Ln,
                                           bias=np.float32(EPS_TR)),
          reads=("t4c",), writes=("t4a",), pad=500)
        P("scalar", lambda e: e.activation(sb["c4"][:, :], sb["t4a"][:, :], AF.Exp,
                                           scale=np.float32(-1.0)),
          reads=("t4a",), writes=("c4",), pad=500)
        P("vector", lambda e: e.tensor_tensor(sb["rc4"][:, :], sb["rstd4"][:, :],
                                              sb["c4"][:, :], ALU.mult),
          reads=("rstd4", "c4"), writes=("rc4",), pad=500)

    def xn_chunk(t, c):
        ci = 4 * t + c
        xb = f"xb{ci % 4}"
        xn, xnn = f"xntm{c % 2}", f"xnntm{c % 2}"
        P("vector", (lambda xb=xb, xn=xn, c=c: (lambda e: e.tensor_scalar(
            sb[xn][:, :], sb[xb][:, :], sb["mu"][:, c:c + 1],
            sb["rstd4"][:, c:c + 1], ALU.subtract, ALU.mult)))(),
          reads=(xb, "mu", "rstd4"), writes=(xn,))
        P("vector", (lambda xb=xb, xnn=xnn, c=c: (lambda e: e.tensor_scalar(
            sb[xnn][:, :], sb[xb][:, :], sb["mu"][:, c:c + 1],
            sb["rc4"][:, c:c + 1], ALU.subtract, ALU.mult)))(),
          reads=(xb, "mu", "rc4"), writes=(xnn,))
        P("scalar", (lambda ci=ci, xn=xn: (lambda e: e.dma_start(
            out=dr["xn"][128 * ci:128 * ci + 128, :], in_=sb[xn][:, :])))(),
          reads=(xn,), writes=("dr_xn",), kind="dma")
        P("scalar", (lambda ci=ci, xnn=xnn: (lambda e: e.dma_start(
            out=dr["xnn"][128 * ci:128 * ci + 128, :], in_=sb[xnn][:, :])))(),
          reads=(xnn,), writes=("dr_xnn",), kind="dma")

    def load_xnfm(t):
        for k in range(8):
            P("sync", (lambda t=t, k=k: (lambda e: e.dma_start(
                out=sb["xnfm"][:, 512 * k:512 * k + 512],
                in_=dr["xn"][TILE * t:TILE * t + TILE, 128 * k:128 * k + 128],
                transpose=True)))(),
              reads=("dr_xn",), writes=("xnfm",))

    def proj_fm(buf, evac):
        for g in range(8):
            pb = "pbig0" if g % 2 == 0 else "pbig1"
            for k in range(8):
                P("tensor", (lambda buf=buf, g=g, k=k, pb=pb: (lambda e: e.matmul(
                    ps[pb][:, :],
                    sb[buf][:, 1024 * k + 128 * g:1024 * k + 128 * g + 128],
                    sb["xnfm"][:, 512 * k:512 * k + 512],
                    start=(k == 0), stop=(k == 7))))(),
                  reads=(buf, "xnfm"), writes=(pb,))
            evac(g, pb)

    def proj_tm(buf, evac):
        for s4 in range(4):
            for fh in range(2):
                pb = "pbig0" if (2 * s4 + fh) % 2 == 0 else "pbig1"
                for k in range(8):
                    P("tensor", (lambda buf=buf, s4=s4, fh=fh, k=k, pb=pb: (lambda e: e.matmul(
                        ps[pb][:, :],
                        sb["xnfm"][:, 512 * k + 128 * s4:512 * k + 128 * s4 + 128],
                        sb[buf][:, 1024 * k + 512 * fh:1024 * k + 512 * fh + 512],
                        start=(k == 0), stop=(k == 7))))(),
                      reads=(buf, "xnfm"), writes=(pb,))
                evac(s4, fh, pb)

    def dproj_softplus(t):
        for k in range(8):
            P("tensor", (lambda k=k: (lambda e: e.matmul(
                ps["pbig0"][0:16, :], sb["wdT"][:, 16 * k:16 * k + 16],
                sb["xnfm"][:, 512 * k:512 * k + 512],
                start=(k == 0), stop=(k == 7))))(),
              reads=("wdT", "xnfm"), writes=("pbig0",))
        P("vector", lambda e: e.tensor_scalar(sb["zb"][:, :], ps["pbig0"][0:16, :],
                                              sb["bdcol"][:, :], None, ALU.add),
          reads=("pbig0", "bdcol"), writes=("zb",))
        P("scalar", lambda e: e.activation(sb["spa"][:, :], sb["zb"][:, :], AF.Abs),
          reads=("zb",), writes=("spa",))
        P("scalar", lambda e: e.activation(sb["spa"][:, :], sb["spa"][:, :], AF.Exp,
                                           scale=np.float32(-1.0)),
          reads=("spa",), writes=("spa",))
        P("scalar", lambda e: e.activation(sb["spa"][:, :], sb["spa"][:, :], AF.Ln,
                                           bias=np.float32(1.0)),
          reads=("spa",), writes=("spa",))
        P("vector", lambda e: e.tensor_scalar(sb["spb"][:, :], sb["zb"][:, :],
                                              0.0, None, ALU.max),
          reads=("zb",), writes=("spb",))
        P("vector", lambda e: e.scalar_tensor_tensor(
            sb["gfm"][:, :], sb["spb"][:, :], -1.0, sb["spa"][:, :],
            ALU.mult, ALU.subtract),
          reads=("spa", "spb"), writes=("gfm",))
        P("vector", lambda e: e.tensor_tensor_scan(
            sb["afm"][:, :], sb["cb"][:, :], sb["gfm"][:, :], 0.0,
            ALU.mult, ALU.add),
          reads=("cb", "gfm"), writes=("afm",))
        P("vector", lambda e: e.tensor_copy(sb["ahib"][:, :], sb["afm"][:, :]),
          reads=("afm",), writes=("ahib",))
        P("vector", lambda e: e.tensor_copy(sb["ahi32"][:, :], sb["ahib"][:, :]),
          reads=("ahib",), writes=("ahi32",))
        P("vector", lambda e: e.tensor_tensor(sb["alob"][:, :], sb["afm"][:, :],
                                              sb["ahi32"][:, :], ALU.subtract),
          reads=("afm", "ahi32"), writes=("alob",))
        P("vector", lambda e: e.tensor_scalar(sb["nahib"][:, :], sb["ahib"][:, :],
                                              -1.0, None, ALU.mult),
          reads=("ahib",), writes=("nahib",))
        P("vector", lambda e: e.tensor_scalar(sb["nalob"][:, :], sb["alob"][:, :],
                                              -1.0, None, ALU.mult),
          reads=("alob",), writes=("nalob",))
        for bi, nm in enumerate(["ahib", "alob", "nahib", "nalob"]):
            P("gpsimd", (lambda t=t, bi=bi, nm=nm: (lambda e: e.dma_start(
                out=dr["ast"][64 * t + 16 * bi:64 * t + 16 * bi + 16, :],
                in_=sb[nm][:, :])))(),
              reads=(nm,), writes=("dr_ast",))
        for c4i in range(4):
            P("vector", (lambda c4i=c4i: (lambda e: e.tensor_scalar(
                sb["tmpw"][:, 128 * c4i:128 * c4i + 128],
                sb["afm"][:, 128 * c4i:128 * c4i + 128], -1.0,
                sb["afm"][:, 128 * c4i + 127:128 * c4i + 128],
                ALU.mult, ALU.add)))(),
              reads=("afm",), writes=("tmpw",), pad=(500 if c4i == 3 else 0))
        P("scalar", lambda e: e.activation(sb["expaw"][:, :], sb["afm"][:, :], AF.Exp),
          reads=("afm",), writes=("expaw",))
        P("scalar", lambda e: e.activation(sb["emaw"][:, :], sb["tmpw"][:, :], AF.Exp),
          reads=("tmpw",), writes=("emaw",))
        ebv = sb["expbf"][:, :].rearrange("p (c t w) -> p c t w", c=4, t=2, w=128)
        P("vector", (lambda ebv=ebv: (lambda e: e.tensor_copy(
            ebv[:, :, 0, :], sb["expaw"][:, :].rearrange("p (c w) -> p c w", c=4))))(),
          reads=("expaw",), writes=("expbf",))
        P("vector", (lambda ebv=ebv: (lambda e: e.tensor_copy(
            ebv[:, :, 1, :], sb["emaw"][:, :].rearrange("p (c w) -> p c w", c=4))))(),
          reads=("emaw",), writes=("expbf",))
        for c4i in range(4):
            P("vector", (lambda c4i=c4i: (lambda e: e.tensor_scalar(
                sb["eAdall"][:, 8 * c4i:8 * c4i + 8], sb["dg"][:, :],
                sb["expaw"][:, 128 * c4i + 127:128 * c4i + 128], None, ALU.mult)))(),
              reads=("dg", "expaw"), writes=("eAdall",), pad=(500 if c4i == 3 else 0))
        for c4i in range(4):
            P("tensor", (lambda c4i=c4i: (lambda e: e.matmul(
                ps["psmall"][:, 8 * c4i:8 * c4i + 8], sb["p2"][:, :],
                sb["eAdall"][:, 8 * c4i:8 * c4i + 8], start=True, stop=True)))(),
              reads=("p2", "eAdall"), writes=("psmall",))
        P("vector", lambda e: e.tensor_copy(sb["eAcol"][:, :], ps["psmall"][:, 0:32]),
          reads=("psmall",), writes=("eAcol",))

    def scan_chunk(t, c, mask=240767):
        if not (mask & 1):
            return
        P("sync", (lambda t=t, c=c: (lambda e: e.dma_start(
            out=sb["stgL"][2:4, :].rearrange("s (h w) -> s h w", h=16),
            in_=dr["ast"][64 * t + 32:64 * t + 64, 128 * c:128 * c + 128]
                .rearrange("(s h) w -> s h w", s=2))))(),
          reads=("dr_ast",), writes=("stgL",))
        P("sync", (lambda t=t, c=c: (lambda e: e.dma_start(
            out=sb["stgR"][0:2, :].rearrange("s (h w) -> s h w", h=16),
            in_=dr["ast"][64 * t:64 * t + 32, 128 * c:128 * c + 128]
                .rearrange("(s h) w -> s h w", s=2))))(),
          reads=("dr_ast",), writes=("stgR",))
        for rnd in range(2 if (mask & 1024) else 0):
            for gg in range(4):
                g = 4 * rnd + gg
                pb = "pbig0" if gg < 2 else "pbig1"
                off = 256 * (gg % 2)
                P("tensor", (lambda g=g, pb=pb, off=off, c=c: (lambda e: e.matmul(
                    ps[pb][:, off:off + 256],
                    sb["eg"][:, 128 * g:128 * g + 128],
                    sb["expbf"][:, 256 * c:256 * c + 256],
                    start=True, stop=True)))(),
                  reads=("eg", "expbf"), writes=(pb,))
                P("scalar", (lambda g=g, pb=pb, off=off: (lambda e: e.activation(
                    sb["repb"][:, 256 * g:256 * g + 256], ps[pb][:, off:off + 256],
                    AF.Copy)))(),
                  reads=(pb,), writes=("repb",))
        for g in range(8 if (mask & 2048) else 0):
            P("vector", (lambda g=g, c=c: (lambda e: e.tensor_tensor(
                sb["qtil"][:, 128 * g:128 * g + 128],
                sb["qfm"][:, 512 * g + 128 * c:512 * g + 128 * c + 128],
                sb["repb"][:, 256 * g:256 * g + 128], ALU.mult)))(),
              reads=("qfm", "repb"), writes=("qtil",))
            P("vector", (lambda g=g, c=c: (lambda e: e.tensor_tensor(
                sb["khfm"][:, 128 * g:128 * g + 128],
                sb["kgfm"][:, 512 * g + 128 * c:512 * g + 128 * c + 128],
                sb["repb"][:, 256 * g + 128:256 * g + 256], ALU.mult)))(),
              reads=("kgfm", "repb"), writes=("khfm",), pad=(500 if g == 7 else 0))
        for g in range(8 if (mask & 2048) else 0):
            slot = g % 4
            P("tensor", (lambda g=g, slot=slot: (lambda e: e.transpose(
                ps["ptr"][:, 128 * slot:128 * slot + 128],
                sb["khfm"][:, 128 * g:128 * g + 128], sb["id128"][:, :])))(),
              reads=("khfm", "id128"), writes=("ptr",))
            P("vector", (lambda g=g, slot=slot: (lambda e: e.tensor_copy(
                sb["khtm"][:, 128 * g:128 * g + 128],
                ps["ptr"][:, 128 * slot:128 * slot + 128])))(),
              reads=("ptr",), writes=("khtm",))
        if not (mask & 4):
            return
        P("vector", lambda e: e.tensor_copy(sb["Sbf"][:, :], sb["S32"][:, :]),
          reads=("S32",), writes=("Sbf",))
        WAVES = [[0, 2, 4, 6], [1, 3, 5, 7], [8, 10, 12, 14], [9, 11, 13, 15]]
        for w, heads in enumerate(WAVES):
            for j, h in enumerate(heads):
                g, b = h // 2, h % 2
                if mask & 256:
                    b = 0
                if mask & 512:
                    g = 0
                P("tensor", (lambda g=g, b=b, j=j, c=c: (lambda e: e.matmul(
                    ps["psc"][:, 128 * j:128 * j + 128],
                    sb["kgfm"][64 * b:64 * b + 64, 512 * g + 128 * c:512 * g + 128 * c + 128],
                    sb["qfm"][64 * b:64 * b + 64, 512 * g + 128 * c:512 * g + 128 * c + 128],
                    start=True, stop=True)))(),
                  reads=("kgfm", "qfm"), writes=("psc",))
            if not (mask & 16):
                continue
            P("tensor", lambda e: e.matmul(ps["pt1"][:, :], sb["id128"][:, :],
                                           sb["negbig"][:, :], start=True, stop=False),
              reads=("id128", "negbig"), writes=("pt1",))
            for j, h in enumerate(heads):
                P("tensor", (lambda h=h, j=j, last=(j == 3): (lambda e: e.matmul(
                    ps["pt1"][:, 128 * j:128 * j + 128],
                    sb["stgL"][:, 128 * h:128 * h + 128],
                    sb["stgR"][:, 128 * h:128 * h + 128],
                    start=False, stop=last)))(),
                  reads=("stgL", "stgR"), writes=("pt1",))
            if not (mask & 32):
                continue
            P("scalar", lambda e: e.activation(sb["dbuf"][:, :], ps["pt1"][:, :], AF.Exp),
              reads=("pt1",), writes=("dbuf",))
            if not (mask & 64):
                continue
            P("vector", lambda e: e.tensor_tensor(sb["pbuf"][:, :], sb["dbuf"][:, :],
                                                  ps["psc"][:, :], ALU.mult),
              reads=("dbuf", "psc"), writes=("pbuf",))
            if not (mask & 8):
                continue
            def _tile(g):
                return ("po" if g % 2 == 0 else "po2",
                        128 * ((g // 2) % 2) + 256 * (g // 4))
            for pi in range(2):
                for j in (2 * pi, 2 * pi + 1):
                    h = heads[j]
                    g, b = h // 2, h % 2
                    pb, col = _tile(g)
                    P("tensor", (lambda h=h, j=j, b=b, c=c, pb=pb, col=col: (lambda e: e.matmul(
                        ps[pb][64 * b:64 * b + 64, col:col + 128],
                        sb["vtm"][:, 1024 * c + 64 * h:1024 * c + 64 * h + 64],
                        sb["pbuf"][:, 128 * j:128 * j + 128],
                        start=True, stop=False, tile_position=(0, 64 * b))))(),
                      reads=("vtm", "pbuf"), writes=(pb,))
                for j in (2 * pi, 2 * pi + 1):
                    h = heads[j]
                    g, b = h // 2, h % 2
                    pb, col = _tile(g)
                    P("tensor", (lambda g=g, b=b, pb=pb, col=col: (lambda e: e.matmul(
                        ps[pb][64 * b:64 * b + 64, col:col + 128],
                        sb["Sbf"][64 * b:64 * b + 64, 64 * g:64 * g + 64],
                        sb["qtil"][64 * b:64 * b + 64, 128 * g:128 * g + 128],
                        start=False, stop=True,
                        tile_position=(64 * b, 64 * b))))(),
                      reads=("Sbf", "qtil"), writes=(pb,))
            for j, h in enumerate(heads):
                if not (mask & 32768):
                    break
                g, b = h // 2, h % 2
                P("tensor", (lambda g=g, b=b, h=h, c=c: (lambda e: e.matmul(
                    ps["psmall"][64 * b:64 * b + 64, 128 + 64 * (g % 4):128 + 64 * (g % 4) + 64],
                    sb["khtm"][:, 128 * g + 64 * b:128 * g + 64 * b + 64],
                    sb["vtm"][:, 1024 * c + 64 * h:1024 * c + 64 * h + 64],
                    start=True, stop=True, tile_position=(0, 64 * b))))(),
                  reads=("khtm", "vtm"), writes=("psmall",))
            if w % 2 == 0:
                continue
            if not (mask & 8):
                continue
            base_g = 0 if w == 1 else 4
            for gi in range(4):
                if not (mask & 65536):
                    break
                g = base_g + gi
                pb = "po" if g % 2 == 0 else "po2"
                col = 128 * ((g // 2) % 2) + 256 * (g // 4)
                P("vector", (lambda g=g, gi=gi: (lambda e: e.scalar_tensor_tensor(
                    sb["S32"][:, 64 * g:64 * g + 64], sb["S32"][:, 64 * g:64 * g + 64],
                    sb["eAcol"][:, 8 * c + g:8 * c + g + 1],
                    ps["psmall"][:, 128 + 64 * gi:128 + 64 * gi + 64],
                    ALU.mult, ALU.add)))(),
                  reads=("S32", "eAcol", "psmall"), writes=("S32",))
                P("scalar", (lambda g=g, pb=pb, col=col: (lambda e: e.activation(
                    sb["sqb"][:, 128 * g:128 * g + 128],
                    ps[pb][:, col:col + 128], AF.Square)))(),
                  reads=(pb,), writes=("sqb", "po_rdlock"))
                P("vector", (lambda g=g, c=c, pb=pb, col=col: (lambda e: e.scalar_tensor_tensor(
                    sb["y1"][:, 512 * g + 128 * c:512 * g + 128 * c + 128],
                    ps[pb][:, col:col + 128],
                    sb["nwfm"][:, g:g + 1],
                    sb["sigog"][:, 512 * g + 128 * c:512 * g + 128 * c + 128],
                    ALU.mult, ALU.mult)))(),
                  reads=(pb, "nwfm", "sigog", "po_rdlock"), writes=("y1",))
        if not (mask & 131072):
            return
        for g in range(8):
            P("tensor", (lambda g=g: (lambda e: e.matmul(
                ps["psc"][0:1, 0:128], sb["onesbf"][:, :],
                sb["sqb"][:, 128 * g:128 * g + 128],
                start=(g == 0), stop=(g == 7))))(),
              reads=("onesbf", "sqb"), writes=("psc",))
        P("vector", (lambda c=c: (lambda e: e.tensor_copy(
            sb["ssqrow"][:, 128 * c:128 * c + 128], ps["psc"][0:1, 0:128])))(),
          reads=("psc",), writes=("ssqrow",))

    def attn_rstd_window():
        P("scalar", lambda e: e.activation(sb["rstdrow"][:, :], sb["ssqrow"][:, :],
                                           AF.Ln, scale=np.float32(1.0 / DM), bias=np.float32(EPS_RMS)),
          reads=("ssqrow",), writes=("rstdrow",))
        P("scalar", lambda e: e.activation(sb["rstdrow"][:, :], sb["rstdrow"][:, :],
                                           AF.Exp, scale=np.float32(-0.5)),
          reads=("rstdrow",), writes=("rstdrow",))
        for c in range(4):
            P("tensor", (lambda c=c: (lambda e: e.matmul(
                ps["psmall"][:, 32 + c:32 + c + 1],
                sb["rstdrow"][:, 128 * c:128 * c + 128], sb["onesf11"][:, :],
                start=True, stop=True)))(),
              reads=("rstdrow", "onesf11"), writes=("psmall",))
        P("vector", lambda e: e.tensor_copy(sb["rcol4"][:, :], ps["psmall"][:, 32:36]),
          reads=("psmall",), writes=("rcol4",))

    def wo_proj(tprev):
        stream_weight("wo", "wbuf1")
        for s4 in range(4):
            for fh in range(2):
                pb = "pbig0" if (2 * s4 + fh) % 2 == 0 else "pbig1"
                for k in range(8):
                    P("tensor", (lambda s4=s4, fh=fh, k=k, pb=pb: (lambda e: e.matmul(
                        ps[pb][:, :],
                        sb["y1"][:, 512 * k + 128 * s4:512 * k + 128 * s4 + 128],
                        sb["wbuf1"][:, 1024 * k + 512 * fh:1024 * k + 512 * fh + 512],
                        start=(k == 0), stop=(k == 7))))(),
                      reads=("y1", "wbuf1"), writes=(pb,))
                sv = f"sev{s4 % 2}"
                P("vector", (lambda s4=s4, fh=fh, pb=pb, sv=sv: (lambda e: e.tensor_scalar(
                    sb[sv][:, 512 * fh:512 * fh + 512], ps[pb][:, :],
                    sb["rcol4"][:, s4:s4 + 1], None, ALU.mult)))(),
                  reads=(pb, "rcol4"), writes=(sv,))
            ci = 4 * tprev + s4
            P("scalar", (lambda ci=ci, sv=f"sev{s4 % 2}": (lambda e: e.dma_start(
                out=dr["s"][128 * ci:128 * ci + 128, :], in_=sb[sv][:, :])))(),
              reads=(f"sev{s4 % 2}",), writes=("dr_s",), kind="dma")

    def alpha_tile(which, idx):
        wf = f"wfp{idx % 2}"
        if which == "up":
            src = w_h["wup"][128 * idx:128 * idx + 128, :]
            acc = "acc_up"
        else:
            r, q = idx // 4, idx % 4
            src = w_h["wdown"][128 * r:128 * r + 128, 1024 * q:1024 * q + 1024]
            acc = "acc_dn"
        P("sync", (lambda wf=wf, src=src: (lambda e: e.dma_start(out=sb[wf][:, :], in_=src)))(),
          writes=(wf,))
        P("vector", (lambda wf=wf: (lambda e: e.tensor_reduce(
            sb["red1"][:, :], sb[wf][:, :], axis=AX.X, op=ALU.add,
            apply_absolute_value=True)))(),
          reads=(wf,), writes=("red1",))
        P("vector", (lambda acc=acc: (lambda e: e.tensor_tensor(
            sb[acc][:, :], sb[acc][:, :], sb["red1"][:, :], ALU.add)))(),
          reads=("red1", acc), writes=(acc,))

    def thr_compute():
        for i, acc in enumerate(["acc_up", "acc_dn"]):
            P("tensor", (lambda i=i, acc=acc: (lambda e: e.matmul(
                ps["psmall"][0:1, 40 + i:41 + i], sb[acc][:, :], sb["onesf128"][:, :],
                start=True, stop=True)))(),
              reads=(acc, "onesf128"), writes=("psmall",))
        P("vector", lambda e: e.tensor_scalar(sb["thr11"][:, :], ps["psmall"][0:1, 40:42],
                                              0.5 / (DFF * DM), None, ALU.mult),
          reads=("psmall",), writes=("thr11",), pad=500)
        P("tensor", lambda e: e.matmul(ps["psmall"][:, 42:44], sb["onesf1x"][:, :],
                                       sb["thr11"][:, :], start=True, stop=True),
          reads=("onesf1x", "thr11"), writes=("psmall",))
        P("vector", lambda e: e.tensor_copy(sb["thrc"][:, :], ps["psmall"][:, 42:44]),
          reads=("psmall",), writes=("thrc",))
        P("vector", lambda e: e.tensor_scalar(sb["nthrc"][:, :], sb["thrc"][:, :],
                                              -1.0, None, ALU.mult),
          reads=("thrc",), writes=("nthrc",))

    def tern_tile(which, idx):
        wf = f"wfp{idx % 2}"
        col = 0 if which == "up" else 1
        if which == "up":
            src = w_h["wup"][128 * idx:128 * idx + 128, :]
        else:
            r, q = idx // 4, idx % 4
            src = w_h["wdown"][128 * r:128 * r + 128, 1024 * q:1024 * q + 1024]
        P("sync", (lambda wf=wf, src=src: (lambda e: e.dma_start(out=sb[wf][:, :], in_=src)))(),
          writes=(wf,))
        P("vector", (lambda wf=wf, col=col: (lambda e: e.tensor_scalar(
            sb["terna"][:, :], sb[wf][:, :], sb["thrc"][:, col:col + 1], None,
            ALU.is_gt)))(),
          reads=(wf, "thrc"), writes=("terna",))
        P("vector", (lambda wf=wf, col=col: (lambda e: e.tensor_scalar(
            sb["ternb"][:, :], sb[wf][:, :], sb["nthrc"][:, col:col + 1], None,
            ALU.is_lt)))(),
          reads=(wf, "nthrc"), writes=("ternb",))
        P("vector", lambda e: e.tensor_tensor(sb["ternw"][:, :], sb["terna"][:, :],
                                              sb["ternb"][:, :], ALU.subtract),
          reads=("terna", "ternb"), writes=("ternw",))
        for i in range(8):
            slot = i % 4
            P("tensor", (lambda i=i, slot=slot: (lambda e: e.transpose(
                ps["ptr"][:, 128 * slot:128 * slot + 128],
                sb["ternw"][:, 128 * i:128 * i + 128], sb["id128"][:, :])))(),
              reads=("ternw", "id128"), writes=("ptr",))
            P("vector", (lambda slot=slot: (lambda e: e.tensor_copy(
                sb["tern_t"][:, 128 * slot:128 * slot + 128],
                ps["ptr"][:, 128 * slot:128 * slot + 128])))(),
              reads=("ptr",), writes=(f"tt{slot}",))
            if i % 4 != 3:
                continue
            h = i // 4
            if which == "up":
                dst = dr["wutT"][512 * h:512 * h + 512,
                                 128 * idx:128 * idx + 128]
            else:
                r, q = idx // 4, idx % 4
                dst = dr["wdtT"][1024 * q + 512 * h:1024 * q + 512 * h + 512,
                                 128 * r:128 * r + 128]
            P("gpsimd", (lambda dst=dst: (lambda e: e.dma_start(
                out=dst.rearrange("(i p) c -> p i c", i=4),
                in_=sb["tern_t"][:, :].rearrange("p (i c) -> p i c", i=4))))(),
              reads=("tt0", "tt1", "tt2", "tt3"), writes=("dr_wt",))

    for t in range(NT):
        for c in range(4):
            ln_chunk(t, c)
        ln_stats()
        if t > 0:
            attn_rstd_window()
        for c in range(4):
            xn_chunk(t, c)
        if t > 0:
            wo_proj(t - 1)
        load_xnfm(t)
        dproj_softplus(t)
        stream_weight("wg", "wbuf0", direct=(t == 0))

        def evac_g(g, pb):
            tg = f"tanhg{g % 2}"
            P("scalar", (lambda pb=pb, tg=tg: (lambda e: e.activation(
                sb[tg][:, :], ps[pb][:, :], AF.Tanh, scale=np.float32(0.5))))(),
              reads=(pb,), writes=(tg,))
            P("vector", (lambda g=g, tg=tg: (lambda e: e.tensor_scalar(
                sb["sigg"][:, 512 * g:512 * g + 512], sb[tg][:, :],
                0.5, 0.5, ALU.mult, ALU.add)))(),
              reads=(tg,), writes=("sigg",))
        proj_fm("wbuf0", evac_g)
        stream_weight("wk", "wbuf1", direct=(t == 0))

        def evac_k(g, pb):
            P("vector", (lambda g=g, pb=pb: (lambda e: e.tensor_tensor(
                sb["kgfm"][:, 512 * g:512 * g + 512], ps[pb][:, :],
                sb["sigg"][:, 512 * g:512 * g + 512], ALU.mult)))(),
              reads=(pb, "sigg"), writes=("kgfm",))
        proj_fm("wbuf1", evac_k)
        stream_weight("wq", "wbuf0", direct=(t == 0))

        def evac_q(g, pb):
            P("vector", (lambda g=g, pb=pb: (lambda e: e.tensor_copy(
                sb["qfm"][:, 512 * g:512 * g + 512], ps[pb][:, :])))(),
              reads=(pb,), writes=("qfm",))
        proj_fm("wbuf0", evac_q)
        stream_weight("wog", "wbuf1", direct=(t == 0))

        def evac_og(g, pb):
            tg = f"tanhg{g % 2}"
            P("scalar", (lambda pb=pb, tg=tg: (lambda e: e.activation(
                sb[tg][:, :], ps[pb][:, :], AF.Tanh, scale=np.float32(0.5))))(),
              reads=(pb,), writes=(tg,))
            P("vector", (lambda g=g, tg=tg: (lambda e: e.tensor_scalar(
                sb["sigog"][:, 512 * g:512 * g + 512], sb[tg][:, :],
                0.5, 0.5, ALU.mult, ALU.add)))(),
              reads=(tg,), writes=("sigog",))
        proj_fm("wbuf1", evac_og)
        stream_weight("wv", "wbuf0", direct=(t == 0))

        def evac_v(s4, fh, pb):
            P("vector", (lambda s4=s4, fh=fh, pb=pb: (lambda e: e.tensor_copy(
                sb["vtm"][:, 1024 * s4 + 512 * fh:1024 * s4 + 512 * fh + 512],
                ps[pb][:, :])))(),
              reads=(pb,), writes=("vtm",))
        proj_tm("wbuf0", evac_v)
        for c in range(scan_chunks):
            scan_chunk(t, c, scan_mask)
        if t == 0:
            for c in range(4):
                P("sync", (lambda c=c: (lambda e: e.dma_start(
                    out=sb[f"xb{c}"][:, :],
                    in_=x_h[128 * (4 + c):128 * (4 + c) + 128, :])))(),
                  writes=(f"xb{c}",))
            pretranspose_weights()
        if t < 2:
            for i in range(8):
                alpha_tile("up", 16 * t + 2 * i), alpha_tile("up", 16 * t + 2 * i + 1)
        elif t == 2:
            for i in range(16):
                alpha_tile("dn", i)
        elif t == 3:
            for i in range(16):
                alpha_tile("dn", 16 + i)
            thr_compute()
        else:
            for i in range(8):
                tern_tile("up", 8 * (t - 4) + i)
            for i in range(8):
                tern_tile("dn", 8 * (t - 4) + i)
    attn_rstd_window()
    wo_proj(NT - 1)

    if phase2:
        p.barrier()

        def load_xnnt(t2):
            for k in range(8):
                P("sync", (lambda t2=t2, k=k: (lambda e: e.dma_start(
                    out=sb["xnnt"][:, 512 * k:512 * k + 512],
                    in_=dr["xnn"][T2 * t2:T2 * t2 + T2, 128 * k:128 * k + 128],
                    transpose=True)))(),
                  reads=("dr_xnn",), writes=("xnnt",))

        load_xnnt(0)
        for t2 in range(NT2):
            for jw in range(8):
                wb = "wbuf0" if jw % 2 == 0 else "wbuf1"
                for k in range(8):
                    P("sync", (lambda jw=jw, k=k, wb=wb: (lambda e: e.dma_start(
                        out=sb[wb][:, 512 * k:512 * k + 512],
                        in_=dr["wutT"][128 * k:128 * k + 128,
                                       512 * jw:512 * jw + 512])))(),
                      reads=("dr_wt",), writes=(wb,))
                for jj in range(4):
                    j = 4 * jw + jj
                    pb = "pbig0" if jj % 2 == 0 else "pbig1"
                    for k in range(8):
                        P("tensor", (lambda jj=jj, k=k, pb=pb, wb=wb: (lambda e: e.matmul(
                            ps[pb][:, :],
                            sb[wb][:, 512 * k + 128 * jj:512 * k + 128 * jj + 128],
                            sb["xnnt"][:, 512 * k:512 * k + 512],
                            start=(k == 0), stop=(k == 7))))(),
                          reads=(wb, "xnnt"), writes=(pb,))
                    if j > 0:
                        P("tensor", (lambda jp=j - 1: (lambda e: e.matmul(
                            ps["psc"][0:1, :], sb["onesbf"][:, :],
                            sb["sqb"][:, 512 * (jp % 2):512 * (jp % 2) + 512],
                            start=(jp == 0), stop=False)))(),
                          reads=("onesbf", f"sqb2{(j - 1) % 2}"), writes=("psc",))
                    P("scalar", (lambda j=j, pb=pb: (lambda e: e.activation(
                        sb["hfm"][:, 512 * j:512 * j + 512], ps[pb][:, :], AF.Gelu)))(),
                      reads=(pb,), writes=(f"hfm{j}",))
                    P("scalar", (lambda j=j: (lambda e: e.activation(
                        sb["sqb"][:, 512 * (j % 2):512 * (j % 2) + 512],
                        sb["hfm"][:, 512 * j:512 * j + 512], AF.Square)))(),
                      reads=(f"hfm{j}",), writes=(f"sqb2{j % 2}",))
            P("tensor", lambda e: e.matmul(
                ps["psc"][0:1, :], sb["onesbf"][:, :], sb["sqb"][:, 512:1024],
                start=False, stop=True),
              reads=("onesbf", "sqb21"), writes=("psc",))
            pbanks = ["pt1", "po", "po2", "psmall"]
            for pss in range(2):
                for grp in range(8):
                    for kk in range(4):
                        k = 4 * grp + kk
                        P("sync", (lambda pss=pss, k=k, wr=f"wdnr{kk}": (lambda e: e.dma_start(
                            out=sb[wr][:, :],
                            in_=dr["wdtT"][128 * k:128 * k + 128,
                                           512 * pss:512 * pss + 512])))(),
                          reads=("dr_wt",), writes=(f"wdnr{kk}",))
                    for kk in range(4):
                        k = 4 * grp + kk
                        wr = f"wdnr{kk}"
                        for s4 in range(4):
                            P("tensor", (lambda k=k, s4=s4, wr=wr, pb=pbanks[s4]: (lambda e: e.matmul(
                                ps[pb][:, :],
                                sb["hfm"][:, 512 * k + 128 * s4:512 * k + 128 * s4 + 128],
                                sb[wr][:, :],
                                start=(k == 0), stop=(k == 31))))(),
                              reads=(f"hfm{k}", wr), writes=(pbanks[s4],))
                if pss == 0:
                    if t2 + 1 < NT2:
                        load_xnnt(t2 + 1)
                    P("scalar", lambda e: e.activation(sb["zrow"][:, :], ps["psc"][0:1, :],
                                                       AF.Ln, scale=np.float32(1.0 / DFF)),
                      reads=("psc",), writes=("zrow",))
                    P("scalar", lambda e: e.activation(sb["zrow"][:, :], sb["zrow"][:, :],
                                                       AF.Exp, scale=np.float32(0.5)),
                      reads=("zrow",), writes=("zrow",))
                    P("scalar", lambda e: e.activation(sb["zrow"][:, :], sb["zrow"][:, :],
                                                       AF.Ln, bias=np.float32(EPS_TR)),
                      reads=("zrow",), writes=("zrow",))
                    P("scalar", lambda e: e.activation(sb["zrow"][:, :], sb["zrow"][:, :],
                                                       AF.Exp, scale=np.float32(-1.0)),
                      reads=("zrow",), writes=("zrow",))
                    for c in range(4):
                        P("tensor", (lambda c=c: (lambda e: e.matmul(
                            ps["psc"][:, c:c + 1],
                            sb["zrow"][:, 128 * c:128 * c + 128], sb["onesf11"][:, :],
                            start=True, stop=True)))(),
                          reads=("zrow", "onesf11"), writes=("psc",))
                    P("vector", lambda e: e.tensor_copy(sb["zc4"][:, :], ps["psc"][:, 0:4]),
                      reads=("psc",), writes=("zc4",), pad=500)
                for s4 in range(4):
                    P("vector", (lambda s4=s4, pss=pss, pb=pbanks[s4]: (lambda e: e.tensor_scalar(
                        sb["dsub"][:, 1024 * s4 + 512 * pss:1024 * s4 + 512 * pss + 512],
                        ps[pb][:, :], sb["zc4"][:, s4:s4 + 1], None, ALU.mult)))(),
                      reads=(pbanks[s4], "zc4"), writes=("dsub",))
            for s4 in range(4):
                ci = 4 * t2 + s4
                sv, xv, ov = f"ssb{s4 % 2}", f"xsb{s4 % 2}", f"osb{s4 % 2}"
                P("scalar", (lambda ci=ci, sv=sv: (lambda e: e.dma_start(
                    out=sb[sv][:, :], in_=dr["s"][128 * ci:128 * ci + 128, :])))(),
                  reads=("dr_s",), writes=(sv,), kind="dma")
                P("scalar", (lambda ci=ci, xv=xv: (lambda e: e.dma_start(
                    out=sb[xv][:, :], in_=x_h[128 * ci:128 * ci + 128, :])))(),
                  writes=(xv,), kind="dma")
                P("scalar", (lambda s4=s4: (lambda e: e.activation(
                    sb["tnh"][:, :], sb["dsub"][:, 1024 * s4:1024 * s4 + 1024],
                    AF.Tanh, scale=np.float32(0.5))))(),
                  reads=("dsub",), writes=("tnh",))
                P("vector", lambda e: e.tensor_scalar(sb["sgt"][:, :], sb["tnh"][:, :],
                                                      0.5, 0.5, ALU.mult, ALU.add),
                  reads=("tnh",), writes=("sgt",))
                P("vector", (lambda s4=s4: (lambda e: e.tensor_tensor(
                    sb["gt"][:, :], sb["sgt"][:, :],
                    sb["dsub"][:, 1024 * s4:1024 * s4 + 1024], ALU.mult)))(),
                  reads=("sgt", "dsub"), writes=("gt",))
                P("vector", (lambda sv=sv: (lambda e: e.tensor_tensor(
                    sb["sgt"][:, :], sb["gt"][:, :], sb[sv][:, :], ALU.mult)))(),
                  reads=("gt", sv), writes=("sgt",))
                P("vector", (lambda xv=xv, ov=ov: (lambda e: e.tensor_tensor(
                    sb[ov][:, :], sb["sgt"][:, :], sb[xv][:, :], ALU.add)))(),
                  reads=("sgt", xv), writes=(ov,))
                P("scalar", (lambda ci=ci, ov=ov: (lambda e: e.dma_start(
                    out=out_h[128 * ci:128 * ci + 128, :], in_=sb[ov][:, :])))(),
                  reads=(ov,), writes=("dr_out",), kind="dma")
    else:
        p.barrier()
        for ci in range(4 * NT):
            sv = f"ssb{ci % 2}"
            P("sync", (lambda ci=ci, sv=sv: (lambda e: e.dma_start(
                out=sb[sv][:, :], in_=dr["s"][128 * ci:128 * ci + 128, :])))(),
              reads=("dr_s",), writes=(sv,))
            P("sync", (lambda ci=ci, sv=sv: (lambda e: e.dma_start(
                out=out_h[128 * ci:128 * ci + 128, :], in_=sb[sv][:, :])))(),
              reads=(sv,), writes=("dr_out",))

    for nm in taps:
        th = nc.dram_tensor(f"tap_{nm}", list(sb[nm].shape), sb[nm].dtype,
                            kind="ExternalOutput").ap()
        P("sync", (lambda nm=nm, th=th: (lambda e: e.dma_start(out=th[:, :], in_=sb[nm][:, :])))(),
          reads=(nm,), writes=(f"tap_{nm}",))

    p.barrier()
    p.emit(nc, sb["padbuf"][:, :])
    nc._dbg_sb = sb
    return nc


def kernel(**inputs):
    x = np.ascontiguousarray(np.asarray(inputs["L0_attn_x"], dtype=np.float32))
    base = {
        "wq": np.ascontiguousarray(inputs["L0_attn_Wq"], dtype=np.float32),
        "wk": np.ascontiguousarray(inputs["L0_attn_Wk"], dtype=np.float32),
        "wv": np.ascontiguousarray(inputs["L0_attn_Wv"], dtype=np.float32),
        "wo": np.ascontiguousarray(inputs["L0_attn_Wo"], dtype=np.float32),
        "wg": np.ascontiguousarray(inputs["L0_attn_Wg"], dtype=np.float32),
        "wog": np.ascontiguousarray(inputs["L0_attn_Wog"], dtype=np.float32),
        "wd": np.ascontiguousarray(inputs["L0_attn_Wd"], dtype=np.float32),
        "bd": np.ascontiguousarray(np.asarray(inputs["L0_attn_bd"], dtype=np.float32).reshape(H, 1)),
        "normw": np.ascontiguousarray(np.asarray(inputs["L0_attn_norm_w"], dtype=np.float32).reshape(8, 128)),
        "wup": np.ascontiguousarray(inputs["L0_ffn_w_up"], dtype=np.float32),
        "wdown": np.ascontiguousarray(inputs["L0_ffn_w_down"], dtype=np.float32),
    }
    nc = build_program()
    in_maps = [dict(base, x=np.ascontiguousarray(x[i])) for i in range(8)]
    res = run_bass_kernel_spmd(nc, in_maps, core_ids=list(range(8)))
    out = np.stack([r["out"] for r in res.results], axis=0)
    return out.astype(np.float32)


# version 13
# speedup vs baseline: 1.0218x; 1.0218x over previous
import numpy as np
import ml_dtypes
import concourse.bass as bass
import concourse.mybir as mybir
from concourse.bass_utils import run_bass_kernel_spmd

F32 = mybir.dt.float32
BF16 = mybir.dt.bfloat16
AF = mybir.ActivationFunctionType
ALU = mybir.AluOpType
AX = mybir.AxisListType

SEQ, DM, H, DH, DFF = 4096, 1024, 16, 64, 4096
C = 128
TILE = 512
T2 = 512
EPS_LN = 1e-5
EPS_RMS = float(np.finfo(np.float32).eps)
EPS_TR = 1e-8
BIGNEG = -1e6

ENGS = ("sync", "gpsimd", "tensor", "vector", "scalar")
DMA_ENGS = ("sync", "gpsimd")
DMA_QUEUES = ("sync", "gpsimd", "scalar")


class Plan:

    def __init__(self):
        self.ops = []
        self.last_writer = {}
        self.readers = {}

    def op(self, eng, fn, reads=(), writes=(), kind=None, pad=0):
        idx = len(self.ops)
        if kind is None:
            kind = "dma" if eng in DMA_ENGS else "comp"
        waits = set()
        soft = set()
        for b in reads:
            for e2, w in self.last_writer.get(b, {}).items():
                (waits if e2 != eng else soft).add(w)
            self.readers.setdefault(b, {})[eng] = idx
        for b in writes:
            for e2, w in self.last_writer.get(b, {}).items():
                if w != idx:
                    (waits if e2 != eng else soft).add(w)
            for e2, r in self.readers.get(b, {}).items():
                if r != idx:
                    (waits if e2 != eng else soft).add(r)
            self.readers[b] = {eng: idx}
            self.last_writer[b] = {eng: idx}
        self.ops.append([eng, fn, waits, kind, soft, pad])
        return idx

    def barrier(self):
        lasts = {}
        for i, op_ in enumerate(self.ops):
            e = op_[0]
            lasts[e] = i
        fence = {}
        for e in ENGS:
            if e not in lasts:
                continue
            deps = {lasts[e2] for e2 in lasts if e2 != e}
            fence[e] = len(self.ops)
            self.ops.append([e, lambda eng: eng.nop(), deps, "comp", set(), 0])
        self.last_writer = {}
        self.readers = {}
        for e, i in fence.items():
            self.last_writer.setdefault("__barrier__", {})[e] = i
        return fence

    def emit(self, nc, padbuf):
        plan_ops = self.ops
        marked = set()
        for eng, fn, waits, kind, soft, pad in plan_ops:
            marked |= waits
        for i, (eng, fn, waits, kind, soft, pad) in enumerate(plan_ops):
            if kind == "dma":
                marked.add(i)
        semval = {}
        opval = {}
        for i, (eng, fn, waits, kind, soft, pad) in enumerate(plan_ops):
            if i in marked:
                key = (eng, kind)
                semval[key] = semval.get(key, 0) + (16 if kind == "dma" else 1)
                opval[i] = semval[key]
        sems = {}
        for (eng, kind) in semval:
            sems[(eng, kind)] = nc.semaphore(f"sem_{eng}_{kind}").__enter__()
        dma_prefix = {e: [] for e in DMA_QUEUES}
        for i, (eng, fn, waits, kind, soft, pad) in enumerate(plan_ops):
            if kind == "dma" and i in marked:
                dma_prefix[eng].append((i, opval[i]))

        def dma_wait_val(eng, before_idx):
            import bisect
            lst = dma_prefix[eng]
            pos = bisect.bisect_left(lst, (before_idx, -1))
            return lst[pos - 1][1] if pos > 0 else 0

        thresholds = {e: set() for e in DMA_QUEUES}
        for i, (eng, fn, waits, kind, soft, pad) in enumerate(plan_ops):
            for w in waits:
                if plan_ops[w][3] == "dma" and plan_ops[w][0] != eng:
                    thresholds[plan_ops[w][0]].add(dma_wait_val(plan_ops[w][0], i))
        thr_sorted = {e: sorted(v) for e, v in thresholds.items()}

        def body_for(eng_name):
            def body(eng):
                import bisect
                waited = {}
                self_waited = 0
                cum = 0
                for i, (e, fn, waits, kind, soft, pad) in enumerate(plan_ops):
                    if e != eng_name:
                        continue
                    allw = sorted(waits) + sorted(w for w in soft if w in marked)
                    for w in allw:
                        we, wkind = plan_ops[w][0], plan_ops[w][3]
                        if wkind == "dma":
                            v = dma_wait_val(we, i) if we != eng_name else opval[w]
                            skey = (we, "dma")
                        else:
                            v = opval[w]
                            skey = (we, "comp")
                        if v == 0 or waited.get(skey, 0) >= v:
                            continue
                        eng.wait_ge(sems[skey], v)
                        waited[skey] = v
                    if kind == "dma" and i in marked:
                        lst = thr_sorted[eng_name]
                        pos = bisect.bisect_right(lst, cum)
                        tstar = lst[pos - 1] if pos > 0 else 0
                        if tstar > self_waited:
                            eng.wait_ge(sems[(eng_name, "dma")], tstar)
                            self_waited = tstar
                    inst = fn(eng)
                    if pad and eng_name == "vector":
                        inst = eng.tensor_copy(padbuf[:, 1:601:2], padbuf[:, 0:600:2])
                    elif pad and eng_name == "scalar":
                        inst = eng.activation(padbuf[:, 1:601:2], padbuf[:, 0:600:2],
                                              mybir.ActivationFunctionType.Copy)
                    if i in marked:
                        amt = 16 if kind == "dma" else 1
                        inst.then_inc(sems[(eng_name, kind)], amt)
                        if kind == "dma":
                            cum += amt
            return body

        blk = nc.Block().__enter__()
        for e in ENGS:
            if any(o[0] == e for o in plan_ops):
                getattr(blk, e)(body_for(e))
        blk.__exit__(None, None, None)


def _consts():
    bf = ml_dtypes.bfloat16
    id128 = np.eye(128, dtype=bf)
    idf32 = np.eye(128, dtype=np.float32)
    blk = np.where(np.arange(128)[:, None] > np.arange(128)[None, :], BIGNEG, 0.0)
    negbig = np.tile(blk, (1, 4)).astype(bf)
    eg = np.zeros((16, 1024), dtype=bf)
    for g in range(8):
        for pp in range(128):
            eg[2 * g + pp // 64, 128 * g + pp] = 1
    p2 = np.zeros((16, 128), dtype=np.float32)
    for h in range(16):
        for pp in range(128):
            if h % 2 == pp // 64:
                p2[h, pp] = 1
    dg = np.zeros((16, 8), dtype=np.float32)
    for h in range(16):
        dg[h, h // 2] = 1
    onesbf = np.ones((128, 1), dtype=bf)
    onesf11 = np.ones((1, 1), dtype=np.float32)
    onesf128 = np.ones((128, 1), dtype=np.float32)
    onesf1x = np.ones((1, 128), dtype=np.float32)
    cb = np.ones((16, 512), dtype=np.float32)
    cb[:, 0::128] = 0.0
    return dict(id128=id128, idf32=idf32, negbig=negbig, eg=eg, p2=p2, dg=dg,
                onesbf=onesbf, onesf11=onesf11, onesf128=onesf128,
                onesf1x=onesf1x, cb=cb)


def build_program(n_tiles=8, phase2=True, taps=(), scan_chunks=4, scan_mask=240767):
    nc = bass.Bass("TRN2", target_bir_lowering=False, detect_race_conditions=False)
    p = Plan()
    cst = _consts()
    NT = n_tiles
    NT2 = SEQ // T2 if phase2 else 0

    x_h = nc.dram_tensor("x", [SEQ, DM], F32, kind="ExternalInput").ap()
    w_h = {}
    for nm, shp in [("wq", [DM, DM]), ("wk", [DM, DM]), ("wv", [DM, DM]),
                    ("wo", [DM, DM]), ("wg", [DM, DM]), ("wog", [DM, DM]),
                    ("wd", [H, DM]), ("wup", [DFF, DM]), ("wdown", [DM, DFF])]:
        w_h[nm] = nc.dram_tensor(nm, shp, F32, kind="ExternalInput").ap()
    bd_h = nc.dram_tensor("bd", [H, 1], F32, kind="ExternalInput").ap()
    nw_h = nc.dram_tensor("normw", [8, 128], F32, kind="ExternalInput").ap()
    out_h = nc.dram_tensor("out", [SEQ, DM], F32, kind="ExternalOutput").ap()

    dr = {}
    for nm in ["wq", "wk", "wv", "wo", "wg", "wog"]:
        dr[nm] = nc.dram_tensor(f"{nm}_bf", [DM, DM], BF16, kind="Internal").ap()
        dr[f"{nm}T"] = nc.dram_tensor(f"{nm}_bfT", [DM, DM], BF16, kind="Internal").ap()
    dr["wd"] = nc.dram_tensor("wd_bf", [H, DM], BF16, kind="Internal").ap()
    dr["xn"] = nc.dram_tensor("xn_sc", [SEQ, DM], BF16, kind="Internal").ap()
    dr["xnn"] = nc.dram_tensor("xnn_sc", [SEQ, DM], BF16, kind="Internal").ap()
    dr["s"] = nc.dram_tensor("s_sc", [SEQ, DM], F32, kind="Internal").ap()
    dr["ast"] = nc.dram_tensor("ast_sc", [8 * 64, TILE], BF16, kind="Internal").ap()
    dr["wutT"] = nc.dram_tensor("wutT_sc", [DM, DFF], BF16, kind="Internal").ap()
    dr["wdtT"] = nc.dram_tensor("wdtT_sc", [DFF, DM], BF16, kind="Internal").ap()

    cst_h = {k: nc.inline_tensor(v, name=f"c_{k}").ap() for k, v in cst.items()}

    sb = {}

    def alloc(name, shape, dtype):
        sb[name] = nc.alloc_sbuf_tensor(name, shape, dtype)

    for k, v in cst.items():
        alloc(k, list(v.shape), BF16 if v.dtype == ml_dtypes.bfloat16 else F32)
    alloc("nwrows", [8, 128], F32)
    alloc("nwfm", [128, 8], F32)
    alloc("bdcol", [H, 1], F32)
    alloc("wbuf0", [128, 8192], BF16)
    alloc("wbuf1", [128, 8192], BF16)
    alloc("wdT", [128, 128], BF16)
    for nm in ["sums", "ssqc", "mu", "var", "rstd4", "c4", "rc4", "nb1", "nb2",
               "t4a", "t4b", "t4c"]:
        alloc(nm, [128, 4], F32)
    alloc("xntm1", [128, DM], BF16)
    for i in range(2):
        alloc(f"xnntm{i}", [128, DM], BF16)
    pool = nc.alloc_sbuf_tensor("pool", [128, 57344], mybir.dt.uint8)
    pool_base = nc.lookup_mloc(pool).addr

    def alloc_at(name, shape, dtype, off):
        sb[name] = nc.alloc_sbuf_tensor_at(name, shape, dtype, offset=pool_base + off)

    alloc("qfm", [128, 8 * 512], BF16)
    alloc("kgfm", [128, 8 * 512], BF16)
    alloc_at("vtm", [128, 4 * 1024], BF16, 0)
    alloc_at("sigg", [128, 8 * 512], BF16, 8192)
    alloc_at("sigog", [128, 8 * 512], BF16, 16384)
    alloc_at("y1", [128, 8 * 512], BF16, 24576)
    alloc_at("xnfm", [128, 8 * 512], BF16, 32768)
    alloc_at("repb", [128, 2048], BF16, 40960)
    alloc_at("khfm", [128, 8 * 128], BF16, 45056)
    alloc_at("khtm", [128, 8 * 128], BF16, 47104)
    alloc_at("dbuf", [128, 512], F32, 49152)
    alloc_at("pbuf", [128, 512], BF16, 51200)
    alloc_at("ahib", [16, 512], BF16, 52224)
    alloc_at("alob", [16, 512], BF16, 53248)
    alloc_at("nahib", [16, 512], BF16, 54272)
    alloc_at("nalob", [16, 512], BF16, 55296)
    alloc_at("hfm", [128, 32 * 512], BF16, 0)
    alloc_at("xnnt", [128, 8 * 512], BF16, 32768)
    alloc_at("dsub", [128, 4 * 1024], F32, 40960)
    pool2 = nc.alloc_sbuf_tensor("pool2", [128, 40960], mybir.dt.uint8)
    pool2_base = nc.lookup_mloc(pool2).addr

    def alloc_at2(name, shape, dtype, off):
        sb[name] = nc.alloc_sbuf_tensor_at(name, shape, dtype, offset=pool2_base + off)

    alloc_at2("sev0", [128, DM], F32, 0)
    alloc_at2("sev1", [128, DM], F32, 4096)
    alloc_at2("wfp0", [128, DM], F32, 8192)
    alloc_at2("wfp1", [128, DM], F32, 12288)
    alloc_at2("xb0", [128, DM], F32, 16384)
    alloc_at2("xb1", [128, DM], F32, 20480)
    alloc_at2("xb2", [128, DM], F32, 24576)
    alloc("xb3", [128, DM], F32)
    alloc_at2("junk", [128, DM], F32, 28672)
    alloc_at2("terna", [128, DM], BF16, 32768)
    alloc_at2("ternb", [128, DM], BF16, 34816)
    alloc_at2("ternw", [128, DM], BF16, 36864)
    alloc_at2("xntm0", [128, DM], BF16, 38912)
    alloc_at2("tnh", [128, 1024], F32, 0)
    alloc_at2("sgt", [128, 1024], F32, 4096)
    alloc_at2("gt", [128, 1024], F32, 8192)
    alloc_at2("ssb0", [128, DM], F32, 12288)
    alloc_at2("ssb1", [128, DM], F32, 16384)
    alloc_at2("xsb0", [128, DM], F32, 20480)
    alloc_at2("xsb1", [128, DM], F32, 24576)
    alloc_at2("osb0", [128, DM], F32, 28672)
    alloc_at2("osb1", [128, DM], F32, 32768)
    alloc_at2("wdnr0", [128, 512], BF16, 36864)
    alloc_at2("wdnr1", [128, 512], BF16, 37888)
    alloc_at2("wdnr2", [128, 512], BF16, 38912)
    alloc_at2("wdnr3", [128, 512], BF16, 39936)
    for i in range(2):
        alloc(f"tanhg{i}", [128, 512], BF16)
    alloc("zb", [16, 512], F32)
    alloc("spa", [16, 512], F32)
    alloc("spb", [16, 512], F32)
    alloc("gfm", [16, 512], F32)
    alloc("afm", [16, 512], F32)
    alloc("ahi32", [16, 512], F32)
    alloc("stgL", [4, 2048], BF16)
    alloc("stgR", [4, 2048], BF16)
    alloc("tmpw", [16, 512], F32)
    alloc("expaw", [16, 512], F32)
    alloc("emaw", [16, 512], F32)
    alloc("expbf", [16, 1024], BF16)
    alloc("eAdall", [16, 32], F32)
    alloc("eAcol", [128, 32], F32)
    alloc("qtil", [128, 8 * 128], BF16)
    alloc("sqb", [128, 1024], BF16)
    alloc("ssqrow", [1, 512], F32)
    alloc("S32", [128, 8 * 64], F32)
    alloc("Sbf", [128, 8 * 64], BF16)
    alloc("rstdrow", [1, 512], F32)
    alloc("rcol4", [128, 4], F32)
    alloc("acc_up", [128, 1], F32)
    alloc("acc_dn", [128, 1], F32)
    alloc("red1", [128, 1], F32)
    alloc("thrc", [128, 2], F32)
    alloc("nthrc", [128, 2], F32)
    alloc("thr11", [1, 2], F32)
    alloc("tern_t", [128, 512], BF16)
    alloc("zrow", [1, 512], F32)
    alloc("padbuf", [128, 608], BF16)
    alloc("zc4", [128, 4], F32)

    ps = {}
    for nm in ["pbig0", "pbig1", "psc", "pt1", "po", "po2", "psmall"]:
        ps[nm] = nc.alloc_psum_tensor(nm, [128, 512], F32)
    ps["ptr"] = nc.alloc_psum_tensor("ptr", [128, 512], BF16)

    P = p.op

    _f32_consts = [1.0 / DM, EPS_LN, EPS_TR, -1.0, 0.0, 0.5, 1.0, 0.5 / (DFF * DM)]
    _bf_consts = [-1.0, 0.5, 1.0, 0.0]
    alloc("cvf", [128, len(_f32_consts)], F32)
    alloc("cvb", [128, len(_bf_consts)], BF16)
    for i, v in enumerate(_f32_consts):
        P("vector", (lambda i=i, v=v: (lambda e: e.memset(sb["cvf"][:, i:i + 1], v)))(),
          writes=("cvf",))
        nc.const_aps.register(F32, v, sb["cvf"][:, i:i + 1])
    for i, v in enumerate(_bf_consts):
        P("vector", (lambda i=i, v=v: (lambda e: e.memset(sb["cvb"][:, i:i + 1], v)))(),
          writes=("cvb",))
        nc.const_aps.register(BF16, v, sb["cvb"][:, i:i + 1])

    for k in cst:
        P("sync", (lambda k=k: (lambda e: e.dma_start(out=sb[k][:, :], in_=cst_h[k][:, :])))(),
          writes=(k,))
    P("sync", lambda e: e.dma_start(out=sb["bdcol"][:, :], in_=bd_h[:, :]), writes=("bdcol",))
    P("sync", lambda e: e.dma_start(out=sb["nwrows"][:, :], in_=nw_h[:, :]), writes=("nwrows",))
    P("tensor", lambda e: e.transpose(ps["psmall"][:, 0:8], sb["nwrows"][:, :], sb["idf32"][0:8, 0:8]),
      reads=("nwrows", "idf32"), writes=("psmall",))
    P("vector", lambda e: e.tensor_copy(sb["nwfm"][:, :], ps["psmall"][:, 0:8]),
      reads=("psmall",), writes=("nwfm",))
    for nm in ["wq", "wk", "wv", "wo", "wg", "wog", "wd"]:
        P("gpsimd", (lambda nm=nm: (lambda e: e.dma_start(out=dr[nm][:, :], in_=w_h[nm][:, :])))(),
          writes=(f"dr_{nm}",))
    for k in range(8):
        P("sync", (lambda k=k: (lambda e: e.dma_start(
            out=sb["wdT"][:, 16 * k:16 * k + 16],
            in_=dr["wd"][:, 128 * k:128 * k + 128], transpose=True)))(),
          reads=("dr_wd",), writes=("wdT",))
    P("vector", lambda e: e.memset(sb["S32"][:, :], 0.0), writes=("S32",))
    P("vector", lambda e: e.memset(sb["Sbf"][:, :], 0.0), writes=("Sbf",))
    P("vector", lambda e: e.memset(sb["stgL"][:, :], 1.0), writes=("stgL",))
    P("vector", lambda e: e.memset(sb["stgR"][:, :], 1.0), writes=("stgR",))
    P("vector", lambda e: e.memset(sb["acc_up"][:, :], 0.0), writes=("acc_up",))
    P("vector", lambda e: e.memset(sb["acc_dn"][:, :], 0.0), writes=("acc_dn",))
    P("vector", lambda e: e.memset(sb["ssqrow"][:, :], 0.0), writes=("ssqrow",))
    P("vector", lambda e: e.memset(sb["padbuf"][:, :], 0.0), writes=("padbuf",))
    P("vector", lambda e: e.memset(sb["y1"][:, :], 0.0), writes=("y1",))
    p.barrier()

    for c in range(4):
        P("sync", (lambda c=c: (lambda e: e.dma_start(
            out=sb[f"xb{c}"][:, :], in_=x_h[128 * c:128 * c + 128, :])))(),
          writes=(f"xb{c}",))

    def pretranspose_weights():
        for wi, nm in enumerate(["wo", "wg", "wk", "wq", "wog", "wv"]):
            buf = "wbuf0" if wi % 2 == 0 else "wbuf1"
            for k in range(8):
                P("sync", (lambda nm=nm, buf=buf, k=k: (lambda e: e.dma_start(
                    out=sb[buf][:, 1024 * k:1024 * k + 1024],
                    in_=dr[nm][:, 128 * k:128 * k + 128], transpose=True)))(),
                  reads=(f"dr_{nm}",), writes=(buf,))
            for k in range(8):
                P("scalar", (lambda nm=nm, buf=buf, k=k: (lambda e: e.dma_start(
                    out=dr[f"{nm}T"][128 * k:128 * k + 128, :],
                    in_=sb[buf][:, 1024 * k:1024 * k + 1024])))(),
                  reads=(buf,), writes=(f"dr_{nm}T",), kind="dma")

    def stream_weight(nm, buf, direct=False):
        for k in range(8):
            if direct:
                P("sync", (lambda nm=nm, buf=buf, k=k: (lambda e: e.dma_start(
                    out=sb[buf][:, 1024 * k:1024 * k + 1024],
                    in_=dr[nm][:, 128 * k:128 * k + 128], transpose=True)))(),
                  reads=(f"dr_{nm}",), writes=(buf,))
            else:
                P("sync", (lambda nm=nm, buf=buf, k=k: (lambda e: e.dma_start(
                    out=sb[buf][:, 1024 * k:1024 * k + 1024],
                    in_=dr[f"{nm}T"][128 * k:128 * k + 128, :])))(),
                  reads=(f"dr_{nm}T",), writes=(buf,))

    def ln_chunk(t, c):
        ci = 4 * t + c
        xb = f"xb{ci % 4}"
        if t > 1:
            P("sync", (lambda ci=ci, xb=xb: (lambda e: e.dma_start(
                out=sb[xb][:, :], in_=x_h[128 * ci:128 * ci + 128, :])))(),
              writes=(xb,))
        P("vector", (lambda xb=xb, c=c: (lambda e: e.reduce_sum(
            sb["sums"][:, c:c + 1], sb[xb][:, :], axis=AX.X)))(),
          reads=(xb,), writes=("sums",))
        P("scalar", (lambda xb=xb: (lambda e: e.activation(
            sb["junk"][:, :], sb[xb][:, :], AF.Square)))(),
          reads=(xb,), writes=("junk",))
        P("vector", (lambda c=c: (lambda e: e.reduce_sum(
            sb["ssqc"][:, c:c + 1], sb["junk"][:, :], axis=AX.X)))(),
          reads=("junk",), writes=("ssqc",))

    def ln_stats():
        P("vector", lambda e: e.tensor_scalar(sb["mu"][:, :], sb["sums"][:, :],
                                              1.0 / DM, None, ALU.mult),
          reads=("sums",), writes=("mu",), pad=500)
        P("vector", lambda e: e.tensor_tensor(sb["t4a"][:, :], sb["mu"][:, :],
                                              sb["mu"][:, :], ALU.mult),
          reads=("mu",), writes=("t4a",), pad=500)
        P("vector", lambda e: e.scalar_tensor_tensor(
            sb["var"][:, :], sb["ssqc"][:, :], 1.0 / DM, sb["t4a"][:, :],
            ALU.mult, ALU.subtract),
          reads=("ssqc", "t4a"), writes=("var",), pad=500)
        P("scalar", lambda e: e.activation(sb["t4b"][:, :], sb["var"][:, :], AF.Ln,
                                           bias=np.float32(EPS_LN)),
          reads=("var",), writes=("t4b",), pad=500)
        P("scalar", lambda e: e.activation(sb["rstd4"][:, :], sb["t4b"][:, :], AF.Exp,
                                           scale=np.float32(-0.5)),
          reads=("t4b",), writes=("rstd4",), pad=500)
        P("scalar", lambda e: e.activation(sb["t4a"][:, :], sb["var"][:, :], AF.Ln),
          reads=("var",), writes=("t4a",), pad=500)
        P("vector", lambda e: e.tensor_tensor(sb["t4a"][:, :], sb["t4a"][:, :],
                                              sb["t4b"][:, :], ALU.subtract),
          reads=("t4a", "t4b"), writes=("t4a",), pad=500)
        P("scalar", lambda e: e.activation(sb["t4c"][:, :], sb["t4a"][:, :], AF.Exp,
                                           scale=np.float32(0.5)),
          reads=("t4a",), writes=("t4c",), pad=500)
        P("scalar", lambda e: e.activation(sb["t4a"][:, :], sb["t4c"][:, :], AF.Ln,
                                           bias=np.float32(EPS_TR)),
          reads=("t4c",), writes=("t4a",), pad=500)
        P("scalar", lambda e: e.activation(sb["c4"][:, :], sb["t4a"][:, :], AF.Exp,
                                           scale=np.float32(-1.0)),
          reads=("t4a",), writes=("c4",), pad=500)
        P("vector", lambda e: e.tensor_tensor(sb["rc4"][:, :], sb["rstd4"][:, :],
                                              sb["c4"][:, :], ALU.mult),
          reads=("rstd4", "c4"), writes=("rc4",), pad=500)

    def xn_chunk(t, c):
        ci = 4 * t + c
        xb = f"xb{ci % 4}"
        xn, xnn = f"xntm{c % 2}", f"xnntm{c % 2}"
        P("vector", (lambda xb=xb, xn=xn, c=c: (lambda e: e.tensor_scalar(
            sb[xn][:, :], sb[xb][:, :], sb["mu"][:, c:c + 1],
            sb["rstd4"][:, c:c + 1], ALU.subtract, ALU.mult)))(),
          reads=(xb, "mu", "rstd4"), writes=(xn,))
        P("vector", (lambda xb=xb, xnn=xnn, c=c: (lambda e: e.tensor_scalar(
            sb[xnn][:, :], sb[xb][:, :], sb["mu"][:, c:c + 1],
            sb["rc4"][:, c:c + 1], ALU.subtract, ALU.mult)))(),
          reads=(xb, "mu", "rc4"), writes=(xnn,))
        P("scalar", (lambda ci=ci, xn=xn: (lambda e: e.dma_start(
            out=dr["xn"][128 * ci:128 * ci + 128, :], in_=sb[xn][:, :])))(),
          reads=(xn,), writes=("dr_xn",), kind="dma")
        P("scalar", (lambda ci=ci, xnn=xnn: (lambda e: e.dma_start(
            out=dr["xnn"][128 * ci:128 * ci + 128, :], in_=sb[xnn][:, :])))(),
          reads=(xnn,), writes=("dr_xnn",), kind="dma")

    def load_xnfm(t):
        for k in range(8):
            P("sync", (lambda t=t, k=k: (lambda e: e.dma_start(
                out=sb["xnfm"][:, 512 * k:512 * k + 512],
                in_=dr["xn"][TILE * t:TILE * t + TILE, 128 * k:128 * k + 128],
                transpose=True)))(),
              reads=("dr_xn",), writes=("xnfm",))

    def proj_fm(buf, evac):
        for g in range(8):
            pb = "pbig0" if g % 2 == 0 else "pbig1"
            for k in range(8):
                P("tensor", (lambda buf=buf, g=g, k=k, pb=pb: (lambda e: e.matmul(
                    ps[pb][:, :],
                    sb[buf][:, 1024 * k + 128 * g:1024 * k + 128 * g + 128],
                    sb["xnfm"][:, 512 * k:512 * k + 512],
                    start=(k == 0), stop=(k == 7))))(),
                  reads=(buf, "xnfm"), writes=(pb,))
            evac(g, pb)

    def proj_tm(buf, evac):
        for s4 in range(4):
            for fh in range(2):
                pb = "pbig0" if (2 * s4 + fh) % 2 == 0 else "pbig1"
                for k in range(8):
                    P("tensor", (lambda buf=buf, s4=s4, fh=fh, k=k, pb=pb: (lambda e: e.matmul(
                        ps[pb][:, :],
                        sb["xnfm"][:, 512 * k + 128 * s4:512 * k + 128 * s4 + 128],
                        sb[buf][:, 1024 * k + 512 * fh:1024 * k + 512 * fh + 512],
                        start=(k == 0), stop=(k == 7))))(),
                      reads=(buf, "xnfm"), writes=(pb,))
                evac(s4, fh, pb)

    def dproj_softplus(t):
        for k in range(8):
            P("tensor", (lambda k=k: (lambda e: e.matmul(
                ps["pbig0"][0:16, :], sb["wdT"][:, 16 * k:16 * k + 16],
                sb["xnfm"][:, 512 * k:512 * k + 512],
                start=(k == 0), stop=(k == 7))))(),
              reads=("wdT", "xnfm"), writes=("pbig0",))
        P("vector", lambda e: e.tensor_scalar(sb["zb"][:, :], ps["pbig0"][0:16, :],
                                              sb["bdcol"][:, :], None, ALU.add),
          reads=("pbig0", "bdcol"), writes=("zb",))
        P("scalar", lambda e: e.activation(sb["spa"][:, :], sb["zb"][:, :], AF.Abs),
          reads=("zb",), writes=("spa",))
        P("scalar", lambda e: e.activation(sb["spa"][:, :], sb["spa"][:, :], AF.Exp,
                                           scale=np.float32(-1.0)),
          reads=("spa",), writes=("spa",))
        P("scalar", lambda e: e.activation(sb["spa"][:, :], sb["spa"][:, :], AF.Ln,
                                           bias=np.float32(1.0)),
          reads=("spa",), writes=("spa",))
        P("vector", lambda e: e.tensor_scalar(sb["spb"][:, :], sb["zb"][:, :],
                                              0.0, None, ALU.max),
          reads=("zb",), writes=("spb",))
        P("vector", lambda e: e.scalar_tensor_tensor(
            sb["gfm"][:, :], sb["spb"][:, :], -1.0, sb["spa"][:, :],
            ALU.mult, ALU.subtract),
          reads=("spa", "spb"), writes=("gfm",))
        P("vector", lambda e: e.tensor_tensor_scan(
            sb["afm"][:, :], sb["cb"][:, :], sb["gfm"][:, :], 0.0,
            ALU.mult, ALU.add),
          reads=("cb", "gfm"), writes=("afm",))
        P("vector", lambda e: e.tensor_copy(sb["ahib"][:, :], sb["afm"][:, :]),
          reads=("afm",), writes=("ahib",))
        P("vector", lambda e: e.tensor_copy(sb["ahi32"][:, :], sb["ahib"][:, :]),
          reads=("ahib",), writes=("ahi32",))
        P("vector", lambda e: e.tensor_tensor(sb["alob"][:, :], sb["afm"][:, :],
                                              sb["ahi32"][:, :], ALU.subtract),
          reads=("afm", "ahi32"), writes=("alob",))
        P("vector", lambda e: e.tensor_scalar(sb["nahib"][:, :], sb["ahib"][:, :],
                                              -1.0, None, ALU.mult),
          reads=("ahib",), writes=("nahib",))
        P("vector", lambda e: e.tensor_scalar(sb["nalob"][:, :], sb["alob"][:, :],
                                              -1.0, None, ALU.mult),
          reads=("alob",), writes=("nalob",))
        for bi, nm in enumerate(["ahib", "alob", "nahib", "nalob"]):
            P("gpsimd", (lambda t=t, bi=bi, nm=nm: (lambda e: e.dma_start(
                out=dr["ast"][64 * t + 16 * bi:64 * t + 16 * bi + 16, :],
                in_=sb[nm][:, :])))(),
              reads=(nm,), writes=("dr_ast",))
        for c4i in range(4):
            P("vector", (lambda c4i=c4i: (lambda e: e.tensor_scalar(
                sb["tmpw"][:, 128 * c4i:128 * c4i + 128],
                sb["afm"][:, 128 * c4i:128 * c4i + 128], -1.0,
                sb["afm"][:, 128 * c4i + 127:128 * c4i + 128],
                ALU.mult, ALU.add)))(),
              reads=("afm",), writes=("tmpw",), pad=(500 if c4i == 3 else 0))
        P("scalar", lambda e: e.activation(sb["expaw"][:, :], sb["afm"][:, :], AF.Exp),
          reads=("afm",), writes=("expaw",))
        P("scalar", lambda e: e.activation(sb["emaw"][:, :], sb["tmpw"][:, :], AF.Exp),
          reads=("tmpw",), writes=("emaw",))
        ebv = sb["expbf"][:, :].rearrange("p (c t w) -> p c t w", c=4, t=2, w=128)
        P("vector", (lambda ebv=ebv: (lambda e: e.tensor_copy(
            ebv[:, :, 0, :], sb["expaw"][:, :].rearrange("p (c w) -> p c w", c=4))))(),
          reads=("expaw",), writes=("expbf",))
        P("vector", (lambda ebv=ebv: (lambda e: e.tensor_copy(
            ebv[:, :, 1, :], sb["emaw"][:, :].rearrange("p (c w) -> p c w", c=4))))(),
          reads=("emaw",), writes=("expbf",))
        for c4i in range(4):
            P("vector", (lambda c4i=c4i: (lambda e: e.tensor_scalar(
                sb["eAdall"][:, 8 * c4i:8 * c4i + 8], sb["dg"][:, :],
                sb["expaw"][:, 128 * c4i + 127:128 * c4i + 128], None, ALU.mult)))(),
              reads=("dg", "expaw"), writes=("eAdall",), pad=(500 if c4i == 3 else 0))
        for c4i in range(4):
            P("tensor", (lambda c4i=c4i: (lambda e: e.matmul(
                ps["psmall"][:, 8 * c4i:8 * c4i + 8], sb["p2"][:, :],
                sb["eAdall"][:, 8 * c4i:8 * c4i + 8], start=True, stop=True)))(),
              reads=("p2", "eAdall"), writes=("psmall",))
        P("vector", lambda e: e.tensor_copy(sb["eAcol"][:, :], ps["psmall"][:, 0:32]),
          reads=("psmall",), writes=("eAcol",))

    def scan_chunk(t, c, mask=240767):
        if not (mask & 1):
            return
        P("sync", (lambda t=t, c=c: (lambda e: e.dma_start(
            out=sb["stgL"][2:4, :].rearrange("s (h w) -> s h w", h=16),
            in_=dr["ast"][64 * t + 32:64 * t + 64, 128 * c:128 * c + 128]
                .rearrange("(s h) w -> s h w", s=2))))(),
          reads=("dr_ast",), writes=("stgL",))
        P("sync", (lambda t=t, c=c: (lambda e: e.dma_start(
            out=sb["stgR"][0:2, :].rearrange("s (h w) -> s h w", h=16),
            in_=dr["ast"][64 * t:64 * t + 32, 128 * c:128 * c + 128]
                .rearrange("(s h) w -> s h w", s=2))))(),
          reads=("dr_ast",), writes=("stgR",))
        for rnd in range(2 if (mask & 1024) else 0):
            for gg in range(4):
                g = 4 * rnd + gg
                pb = "pbig0" if gg < 2 else "pbig1"
                off = 256 * (gg % 2)
                P("tensor", (lambda g=g, pb=pb, off=off, c=c: (lambda e: e.matmul(
                    ps[pb][:, off:off + 256],
                    sb["eg"][:, 128 * g:128 * g + 128],
                    sb["expbf"][:, 256 * c:256 * c + 256],
                    start=True, stop=True)))(),
                  reads=("eg", "expbf"), writes=(pb,))
                P("scalar", (lambda g=g, pb=pb, off=off: (lambda e: e.activation(
                    sb["repb"][:, 256 * g:256 * g + 256], ps[pb][:, off:off + 256],
                    AF.Copy)))(),
                  reads=(pb,), writes=("repb",))
        for g in range(8 if (mask & 2048) else 0):
            P("vector", (lambda g=g, c=c: (lambda e: e.tensor_tensor(
                sb["qtil"][:, 128 * g:128 * g + 128],
                sb["qfm"][:, 512 * g + 128 * c:512 * g + 128 * c + 128],
                sb["repb"][:, 256 * g:256 * g + 128], ALU.mult)))(),
              reads=("qfm", "repb"), writes=("qtil",))
            P("vector", (lambda g=g, c=c: (lambda e: e.tensor_tensor(
                sb["khfm"][:, 128 * g:128 * g + 128],
                sb["kgfm"][:, 512 * g + 128 * c:512 * g + 128 * c + 128],
                sb["repb"][:, 256 * g + 128:256 * g + 256], ALU.mult)))(),
              reads=("kgfm", "repb"), writes=("khfm",), pad=(500 if g == 7 else 0))
        for g in range(8 if (mask & 2048) else 0):
            slot = g % 4
            P("tensor", (lambda g=g, slot=slot: (lambda e: e.transpose(
                ps["ptr"][:, 128 * slot:128 * slot + 128],
                sb["khfm"][:, 128 * g:128 * g + 128], sb["id128"][:, :])))(),
              reads=("khfm", "id128"), writes=("ptr",))
            P("vector", (lambda g=g, slot=slot: (lambda e: e.tensor_copy(
                sb["khtm"][:, 128 * g:128 * g + 128],
                ps["ptr"][:, 128 * slot:128 * slot + 128])))(),
              reads=("ptr",), writes=("khtm",))
        if not (mask & 4):
            return
        P("vector", lambda e: e.tensor_copy(sb["Sbf"][:, :], sb["S32"][:, :]),
          reads=("S32",), writes=("Sbf",))
        WAVES = [[0, 2, 4, 6], [1, 3, 5, 7], [8, 10, 12, 14], [9, 11, 13, 15]]
        for w, heads in enumerate(WAVES):
            for j, h in enumerate(heads):
                g, b = h // 2, h % 2
                if mask & 256:
                    b = 0
                if mask & 512:
                    g = 0
                P("tensor", (lambda g=g, b=b, j=j, c=c: (lambda e: e.matmul(
                    ps["psc"][:, 128 * j:128 * j + 128],
                    sb["kgfm"][64 * b:64 * b + 64, 512 * g + 128 * c:512 * g + 128 * c + 128],
                    sb["qfm"][64 * b:64 * b + 64, 512 * g + 128 * c:512 * g + 128 * c + 128],
                    start=True, stop=True)))(),
                  reads=("kgfm", "qfm"), writes=("psc",))
            if not (mask & 16):
                continue
            P("tensor", lambda e: e.matmul(ps["pt1"][:, :], sb["id128"][:, :],
                                           sb["negbig"][:, :], start=True, stop=False),
              reads=("id128", "negbig"), writes=("pt1",))
            for j, h in enumerate(heads):
                P("tensor", (lambda h=h, j=j, last=(j == 3): (lambda e: e.matmul(
                    ps["pt1"][:, 128 * j:128 * j + 128],
                    sb["stgL"][:, 128 * h:128 * h + 128],
                    sb["stgR"][:, 128 * h:128 * h + 128],
                    start=False, stop=last)))(),
                  reads=("stgL", "stgR"), writes=("pt1",))
            if not (mask & 32):
                continue
            P("scalar", lambda e: e.activation(sb["dbuf"][:, :], ps["pt1"][:, :], AF.Exp),
              reads=("pt1",), writes=("dbuf",))
            if not (mask & 64):
                continue
            P("vector", lambda e: e.tensor_tensor(sb["pbuf"][:, :], sb["dbuf"][:, :],
                                                  ps["psc"][:, :], ALU.mult),
              reads=("dbuf", "psc"), writes=("pbuf",))
            if not (mask & 8):
                continue
            def _tile(g):
                return ("po" if g % 2 == 0 else "po2",
                        128 * ((g // 2) % 2) + 256 * (g // 4))
            for pi in range(2):
                for j in (2 * pi, 2 * pi + 1):
                    h = heads[j]
                    g, b = h // 2, h % 2
                    pb, col = _tile(g)
                    P("tensor", (lambda h=h, j=j, b=b, c=c, pb=pb, col=col: (lambda e: e.matmul(
                        ps[pb][64 * b:64 * b + 64, col:col + 128],
                        sb["vtm"][:, 1024 * c + 64 * h:1024 * c + 64 * h + 64],
                        sb["pbuf"][:, 128 * j:128 * j + 128],
                        start=True, stop=False, tile_position=(0, 64 * b))))(),
                      reads=("vtm", "pbuf"), writes=(pb,))
                for j in (2 * pi, 2 * pi + 1):
                    h = heads[j]
                    g, b = h // 2, h % 2
                    pb, col = _tile(g)
                    P("tensor", (lambda g=g, b=b, pb=pb, col=col: (lambda e: e.matmul(
                        ps[pb][64 * b:64 * b + 64, col:col + 128],
                        sb["Sbf"][64 * b:64 * b + 64, 64 * g:64 * g + 64],
                        sb["qtil"][64 * b:64 * b + 64, 128 * g:128 * g + 128],
                        start=False, stop=True,
                        tile_position=(64 * b, 64 * b))))(),
                      reads=("Sbf", "qtil"), writes=(pb,))
            for j, h in enumerate(heads):
                if not (mask & 32768):
                    break
                g, b = h // 2, h % 2
                P("tensor", (lambda g=g, b=b, h=h, c=c: (lambda e: e.matmul(
                    ps["psmall"][64 * b:64 * b + 64, 128 + 64 * (g % 4):128 + 64 * (g % 4) + 64],
                    sb["khtm"][:, 128 * g + 64 * b:128 * g + 64 * b + 64],
                    sb["vtm"][:, 1024 * c + 64 * h:1024 * c + 64 * h + 64],
                    start=True, stop=True, tile_position=(0, 64 * b))))(),
                  reads=("khtm", "vtm"), writes=("psmall",))
            if w % 2 == 0:
                continue
            if not (mask & 8):
                continue
            base_g = 0 if w == 1 else 4
            for gi in range(4):
                if not (mask & 65536):
                    break
                g = base_g + gi
                pb = "po" if g % 2 == 0 else "po2"
                col = 128 * ((g // 2) % 2) + 256 * (g // 4)
                P("vector", (lambda g=g, gi=gi: (lambda e: e.scalar_tensor_tensor(
                    sb["S32"][:, 64 * g:64 * g + 64], sb["S32"][:, 64 * g:64 * g + 64],
                    sb["eAcol"][:, 8 * c + g:8 * c + g + 1],
                    ps["psmall"][:, 128 + 64 * gi:128 + 64 * gi + 64],
                    ALU.mult, ALU.add)))(),
                  reads=("S32", "eAcol", "psmall"), writes=("S32",))
                P("scalar", (lambda g=g, pb=pb, col=col: (lambda e: e.activation(
                    sb["sqb"][:, 128 * g:128 * g + 128],
                    ps[pb][:, col:col + 128], AF.Square)))(),
                  reads=(pb,), writes=("sqb", "po_rdlock"))
                P("vector", (lambda g=g, c=c, pb=pb, col=col: (lambda e: e.scalar_tensor_tensor(
                    sb["y1"][:, 512 * g + 128 * c:512 * g + 128 * c + 128],
                    ps[pb][:, col:col + 128],
                    sb["nwfm"][:, g:g + 1],
                    sb["sigog"][:, 512 * g + 128 * c:512 * g + 128 * c + 128],
                    ALU.mult, ALU.mult)))(),
                  reads=(pb, "nwfm", "sigog", "po_rdlock"), writes=("y1",))
        if not (mask & 131072):
            return
        for g in range(8):
            P("tensor", (lambda g=g: (lambda e: e.matmul(
                ps["psc"][0:1, 0:128], sb["onesbf"][:, :],
                sb["sqb"][:, 128 * g:128 * g + 128],
                start=(g == 0), stop=(g == 7))))(),
              reads=("onesbf", "sqb"), writes=("psc",))
        P("vector", (lambda c=c: (lambda e: e.tensor_copy(
            sb["ssqrow"][:, 128 * c:128 * c + 128], ps["psc"][0:1, 0:128])))(),
          reads=("psc",), writes=("ssqrow",))

    def attn_rstd_window():
        P("scalar", lambda e: e.activation(sb["rstdrow"][:, :], sb["ssqrow"][:, :],
                                           AF.Ln, scale=np.float32(1.0 / DM), bias=np.float32(EPS_RMS)),
          reads=("ssqrow",), writes=("rstdrow",))
        P("scalar", lambda e: e.activation(sb["rstdrow"][:, :], sb["rstdrow"][:, :],
                                           AF.Exp, scale=np.float32(-0.5)),
          reads=("rstdrow",), writes=("rstdrow",))
        for c in range(4):
            P("tensor", (lambda c=c: (lambda e: e.matmul(
                ps["psmall"][:, 32 + c:32 + c + 1],
                sb["rstdrow"][:, 128 * c:128 * c + 128], sb["onesf11"][:, :],
                start=True, stop=True)))(),
              reads=("rstdrow", "onesf11"), writes=("psmall",))
        P("vector", lambda e: e.tensor_copy(sb["rcol4"][:, :], ps["psmall"][:, 32:36]),
          reads=("psmall",), writes=("rcol4",))

    def wo_proj(tprev):
        stream_weight("wo", "wbuf1")
        for s4 in range(4):
            for fh in range(2):
                pb = "pbig0" if (2 * s4 + fh) % 2 == 0 else "pbig1"
                for k in range(8):
                    P("tensor", (lambda s4=s4, fh=fh, k=k, pb=pb: (lambda e: e.matmul(
                        ps[pb][:, :],
                        sb["y1"][:, 512 * k + 128 * s4:512 * k + 128 * s4 + 128],
                        sb["wbuf1"][:, 1024 * k + 512 * fh:1024 * k + 512 * fh + 512],
                        start=(k == 0), stop=(k == 7))))(),
                      reads=("y1", "wbuf1"), writes=(pb,))
                sv = f"sev{s4 % 2}"
                P("vector", (lambda s4=s4, fh=fh, pb=pb, sv=sv: (lambda e: e.tensor_scalar(
                    sb[sv][:, 512 * fh:512 * fh + 512], ps[pb][:, :],
                    sb["rcol4"][:, s4:s4 + 1], None, ALU.mult)))(),
                  reads=(pb, "rcol4"), writes=(sv,))
            ci = 4 * tprev + s4
            P("scalar", (lambda ci=ci, sv=f"sev{s4 % 2}": (lambda e: e.dma_start(
                out=dr["s"][128 * ci:128 * ci + 128, :], in_=sb[sv][:, :])))(),
              reads=(f"sev{s4 % 2}",), writes=("dr_s",), kind="dma")

    def alpha_tile(which, idx):
        wf = f"wfp{idx % 2}"
        if which == "up":
            src = w_h["wup"][128 * idx:128 * idx + 128, :]
            acc = "acc_up"
        else:
            r, q = idx // 4, idx % 4
            src = w_h["wdown"][128 * r:128 * r + 128, 1024 * q:1024 * q + 1024]
            acc = "acc_dn"
        P("sync", (lambda wf=wf, src=src: (lambda e: e.dma_start(out=sb[wf][:, :], in_=src)))(),
          writes=(wf,))
        P("vector", (lambda wf=wf: (lambda e: e.tensor_reduce(
            sb["red1"][:, :], sb[wf][:, :], axis=AX.X, op=ALU.add,
            apply_absolute_value=True)))(),
          reads=(wf,), writes=("red1",))
        P("vector", (lambda acc=acc: (lambda e: e.tensor_tensor(
            sb[acc][:, :], sb[acc][:, :], sb["red1"][:, :], ALU.add)))(),
          reads=("red1", acc), writes=(acc,))

    def thr_compute():
        for i, acc in enumerate(["acc_up", "acc_dn"]):
            P("tensor", (lambda i=i, acc=acc: (lambda e: e.matmul(
                ps["psmall"][0:1, 40 + i:41 + i], sb[acc][:, :], sb["onesf128"][:, :],
                start=True, stop=True)))(),
              reads=(acc, "onesf128"), writes=("psmall",))
        P("vector", lambda e: e.tensor_scalar(sb["thr11"][:, :], ps["psmall"][0:1, 40:42],
                                              0.5 / (DFF * DM), None, ALU.mult),
          reads=("psmall",), writes=("thr11",), pad=500)
        P("tensor", lambda e: e.matmul(ps["psmall"][:, 42:44], sb["onesf1x"][:, :],
                                       sb["thr11"][:, :], start=True, stop=True),
          reads=("onesf1x", "thr11"), writes=("psmall",))
        P("vector", lambda e: e.tensor_copy(sb["thrc"][:, :], ps["psmall"][:, 42:44]),
          reads=("psmall",), writes=("thrc",))
        P("vector", lambda e: e.tensor_scalar(sb["nthrc"][:, :], sb["thrc"][:, :],
                                              -1.0, None, ALU.mult),
          reads=("thrc",), writes=("nthrc",))

    def tern_tile(which, idx):
        wf = f"wfp{idx % 2}"
        col = 0 if which == "up" else 1
        if which == "up":
            src = w_h["wup"][128 * idx:128 * idx + 128, :]
        else:
            r, q = idx // 4, idx % 4
            src = w_h["wdown"][128 * r:128 * r + 128, 1024 * q:1024 * q + 1024]
        P("sync", (lambda wf=wf, src=src: (lambda e: e.dma_start(out=sb[wf][:, :], in_=src)))(),
          writes=(wf,))
        P("vector", (lambda wf=wf, col=col: (lambda e: e.tensor_scalar(
            sb["terna"][:, :], sb[wf][:, :], sb["thrc"][:, col:col + 1], None,
            ALU.is_gt)))(),
          reads=(wf, "thrc"), writes=("terna",))
        P("vector", (lambda wf=wf, col=col: (lambda e: e.tensor_scalar(
            sb["ternb"][:, :], sb[wf][:, :], sb["nthrc"][:, col:col + 1], None,
            ALU.is_lt)))(),
          reads=(wf, "nthrc"), writes=("ternb",))
        P("vector", lambda e: e.tensor_tensor(sb["ternw"][:, :], sb["terna"][:, :],
                                              sb["ternb"][:, :], ALU.subtract),
          reads=("terna", "ternb"), writes=("ternw",))
        for i in range(8):
            slot = i % 4
            P("tensor", (lambda i=i, slot=slot: (lambda e: e.transpose(
                ps["ptr"][:, 128 * slot:128 * slot + 128],
                sb["ternw"][:, 128 * i:128 * i + 128], sb["id128"][:, :])))(),
              reads=("ternw", "id128"), writes=("ptr",))
            P("vector", (lambda slot=slot: (lambda e: e.tensor_copy(
                sb["tern_t"][:, 128 * slot:128 * slot + 128],
                ps["ptr"][:, 128 * slot:128 * slot + 128])))(),
              reads=("ptr",), writes=(f"tt{slot}",))
            if i % 4 != 3:
                continue
            h = i // 4
            if which == "up":
                dst = dr["wutT"][512 * h:512 * h + 512,
                                 128 * idx:128 * idx + 128]
            else:
                r, q = idx // 4, idx % 4
                dst = dr["wdtT"][1024 * q + 512 * h:1024 * q + 512 * h + 512,
                                 128 * r:128 * r + 128]
            P("gpsimd", (lambda dst=dst: (lambda e: e.dma_start(
                out=dst.rearrange("(i p) c -> p i c", i=4),
                in_=sb["tern_t"][:, :].rearrange("p (i c) -> p i c", i=4))))(),
              reads=("tt0", "tt1", "tt2", "tt3"), writes=("dr_wt",))

    for t in range(NT):
        if t > 0:
            attn_rstd_window()
            wo_proj(t - 1)
        for c in range(4):
            ln_chunk(t, c)
        ln_stats()
        for c in range(4):
            xn_chunk(t, c)
        load_xnfm(t)
        dproj_softplus(t)
        stream_weight("wg", "wbuf0", direct=(t == 0))

        def evac_g(g, pb):
            tg = f"tanhg{g % 2}"
            P("scalar", (lambda pb=pb, tg=tg: (lambda e: e.activation(
                sb[tg][:, :], ps[pb][:, :], AF.Tanh, scale=np.float32(0.5))))(),
              reads=(pb,), writes=(tg,))
            P("vector", (lambda g=g, tg=tg: (lambda e: e.tensor_scalar(
                sb["sigg"][:, 512 * g:512 * g + 512], sb[tg][:, :],
                0.5, 0.5, ALU.mult, ALU.add)))(),
              reads=(tg,), writes=("sigg",))
        proj_fm("wbuf0", evac_g)
        stream_weight("wk", "wbuf1", direct=(t == 0))

        def evac_k(g, pb):
            P("vector", (lambda g=g, pb=pb: (lambda e: e.tensor_tensor(
                sb["kgfm"][:, 512 * g:512 * g + 512], ps[pb][:, :],
                sb["sigg"][:, 512 * g:512 * g + 512], ALU.mult)))(),
              reads=(pb, "sigg"), writes=("kgfm",))
        proj_fm("wbuf1", evac_k)
        stream_weight("wq", "wbuf0", direct=(t == 0))

        def evac_q(g, pb):
            P("vector", (lambda g=g, pb=pb: (lambda e: e.tensor_copy(
                sb["qfm"][:, 512 * g:512 * g + 512], ps[pb][:, :])))(),
              reads=(pb,), writes=("qfm",))
        proj_fm("wbuf0", evac_q)
        stream_weight("wog", "wbuf1", direct=(t == 0))

        def evac_og(g, pb):
            tg = f"tanhg{g % 2}"
            P("scalar", (lambda pb=pb, tg=tg: (lambda e: e.activation(
                sb[tg][:, :], ps[pb][:, :], AF.Tanh, scale=np.float32(0.5))))(),
              reads=(pb,), writes=(tg,))
            P("vector", (lambda g=g, tg=tg: (lambda e: e.tensor_scalar(
                sb["sigog"][:, 512 * g:512 * g + 512], sb[tg][:, :],
                0.5, 0.5, ALU.mult, ALU.add)))(),
              reads=(tg,), writes=("sigog",))
        proj_fm("wbuf1", evac_og)
        stream_weight("wv", "wbuf0", direct=(t == 0))

        def evac_v(s4, fh, pb):
            P("vector", (lambda s4=s4, fh=fh, pb=pb: (lambda e: e.tensor_copy(
                sb["vtm"][:, 1024 * s4 + 512 * fh:1024 * s4 + 512 * fh + 512],
                ps[pb][:, :])))(),
              reads=(pb,), writes=("vtm",))
        proj_tm("wbuf0", evac_v)
        for c in range(scan_chunks):
            scan_chunk(t, c, scan_mask)
        if t == 0:
            for c in range(4):
                P("sync", (lambda c=c: (lambda e: e.dma_start(
                    out=sb[f"xb{c}"][:, :],
                    in_=x_h[128 * (4 + c):128 * (4 + c) + 128, :])))(),
                  writes=(f"xb{c}",))
            pretranspose_weights()
        if t < 2:
            for i in range(8):
                alpha_tile("up", 16 * t + 2 * i), alpha_tile("up", 16 * t + 2 * i + 1)
        elif t == 2:
            for i in range(16):
                alpha_tile("dn", i)
        elif t == 3:
            for i in range(16):
                alpha_tile("dn", 16 + i)
            thr_compute()
        else:
            for i in range(8):
                tern_tile("up", 8 * (t - 4) + i)
            for i in range(8):
                tern_tile("dn", 8 * (t - 4) + i)
    attn_rstd_window()
    wo_proj(NT - 1)

    if phase2:
        p.barrier()

        def load_xnnt(t2):
            for k in range(8):
                P("sync", (lambda t2=t2, k=k: (lambda e: e.dma_start(
                    out=sb["xnnt"][:, 512 * k:512 * k + 512],
                    in_=dr["xnn"][T2 * t2:T2 * t2 + T2, 128 * k:128 * k + 128],
                    transpose=True)))(),
                  reads=("dr_xnn",), writes=("xnnt",))

        load_xnnt(0)
        for t2 in range(NT2):
            for jw in range(8):
                wb = "wbuf0" if jw % 2 == 0 else "wbuf1"
                for k in range(8):
                    P("sync", (lambda jw=jw, k=k, wb=wb: (lambda e: e.dma_start(
                        out=sb[wb][:, 512 * k:512 * k + 512],
                        in_=dr["wutT"][128 * k:128 * k + 128,
                                       512 * jw:512 * jw + 512])))(),
                      reads=("dr_wt",), writes=(wb,))
                for jj in range(4):
                    j = 4 * jw + jj
                    pb = "pbig0" if jj % 2 == 0 else "pbig1"
                    for k in range(8):
                        P("tensor", (lambda jj=jj, k=k, pb=pb, wb=wb: (lambda e: e.matmul(
                            ps[pb][:, :],
                            sb[wb][:, 512 * k + 128 * jj:512 * k + 128 * jj + 128],
                            sb["xnnt"][:, 512 * k:512 * k + 512],
                            start=(k == 0), stop=(k == 7))))(),
                          reads=(wb, "xnnt"), writes=(pb,))
                    if j > 0:
                        P("tensor", (lambda jp=j - 1: (lambda e: e.matmul(
                            ps["psc"][0:1, :], sb["onesbf"][:, :],
                            sb["sqb"][:, 512 * (jp % 2):512 * (jp % 2) + 512],
                            start=(jp == 0), stop=False)))(),
                          reads=("onesbf", f"sqb2{(j - 1) % 2}"), writes=("psc",))
                    P("scalar", (lambda j=j, pb=pb: (lambda e: e.activation(
                        sb["hfm"][:, 512 * j:512 * j + 512], ps[pb][:, :], AF.Gelu)))(),
                      reads=(pb,), writes=(f"hfm{j}",))
                    P("scalar", (lambda j=j: (lambda e: e.activation(
                        sb["sqb"][:, 512 * (j % 2):512 * (j % 2) + 512],
                        sb["hfm"][:, 512 * j:512 * j + 512], AF.Square)))(),
                      reads=(f"hfm{j}",), writes=(f"sqb2{j % 2}",))
            P("tensor", lambda e: e.matmul(
                ps["psc"][0:1, :], sb["onesbf"][:, :], sb["sqb"][:, 512:1024],
                start=False, stop=True),
              reads=("onesbf", "sqb21"), writes=("psc",))
            pbanks = ["pt1", "po", "po2", "psmall"]
            for pss in range(2):
                for grp in range(8):
                    for kk in range(4):
                        k = 4 * grp + kk
                        P("sync", (lambda pss=pss, k=k, wr=f"wdnr{kk}": (lambda e: e.dma_start(
                            out=sb[wr][:, :],
                            in_=dr["wdtT"][128 * k:128 * k + 128,
                                           512 * pss:512 * pss + 512])))(),
                          reads=("dr_wt",), writes=(f"wdnr{kk}",))
                    for kk in range(4):
                        k = 4 * grp + kk
                        wr = f"wdnr{kk}"
                        for s4 in range(4):
                            P("tensor", (lambda k=k, s4=s4, wr=wr, pb=pbanks[s4]: (lambda e: e.matmul(
                                ps[pb][:, :],
                                sb["hfm"][:, 512 * k + 128 * s4:512 * k + 128 * s4 + 128],
                                sb[wr][:, :],
                                start=(k == 0), stop=(k == 31))))(),
                              reads=(f"hfm{k}", wr), writes=(pbanks[s4],))
                if pss == 0:
                    if t2 + 1 < NT2:
                        load_xnnt(t2 + 1)
                    P("scalar", lambda e: e.activation(sb["zrow"][:, :], ps["psc"][0:1, :],
                                                       AF.Ln, scale=np.float32(1.0 / DFF)),
                      reads=("psc",), writes=("zrow",))
                    P("scalar", lambda e: e.activation(sb["zrow"][:, :], sb["zrow"][:, :],
                                                       AF.Exp, scale=np.float32(0.5)),
                      reads=("zrow",), writes=("zrow",))
                    P("scalar", lambda e: e.activation(sb["zrow"][:, :], sb["zrow"][:, :],
                                                       AF.Ln, bias=np.float32(EPS_TR)),
                      reads=("zrow",), writes=("zrow",))
                    P("scalar", lambda e: e.activation(sb["zrow"][:, :], sb["zrow"][:, :],
                                                       AF.Exp, scale=np.float32(-1.0)),
                      reads=("zrow",), writes=("zrow",))
                    for c in range(4):
                        P("tensor", (lambda c=c: (lambda e: e.matmul(
                            ps["psc"][:, c:c + 1],
                            sb["zrow"][:, 128 * c:128 * c + 128], sb["onesf11"][:, :],
                            start=True, stop=True)))(),
                          reads=("zrow", "onesf11"), writes=("psc",))
                    P("vector", lambda e: e.tensor_copy(sb["zc4"][:, :], ps["psc"][:, 0:4]),
                      reads=("psc",), writes=("zc4",), pad=500)
                for s4 in range(4):
                    P("vector", (lambda s4=s4, pss=pss, pb=pbanks[s4]: (lambda e: e.tensor_scalar(
                        sb["dsub"][:, 1024 * s4 + 512 * pss:1024 * s4 + 512 * pss + 512],
                        ps[pb][:, :], sb["zc4"][:, s4:s4 + 1], None, ALU.mult)))(),
                      reads=(pbanks[s4], "zc4"), writes=("dsub",))
            for s4 in range(4):
                ci = 4 * t2 + s4
                sv, xv, ov = f"ssb{s4 % 2}", f"xsb{s4 % 2}", f"osb{s4 % 2}"
                P("scalar", (lambda ci=ci, sv=sv: (lambda e: e.dma_start(
                    out=sb[sv][:, :], in_=dr["s"][128 * ci:128 * ci + 128, :])))(),
                  reads=("dr_s",), writes=(sv,), kind="dma")
                P("scalar", (lambda ci=ci, xv=xv: (lambda e: e.dma_start(
                    out=sb[xv][:, :], in_=x_h[128 * ci:128 * ci + 128, :])))(),
                  writes=(xv,), kind="dma")
                P("scalar", (lambda s4=s4: (lambda e: e.activation(
                    sb["tnh"][:, :], sb["dsub"][:, 1024 * s4:1024 * s4 + 1024],
                    AF.Tanh, scale=np.float32(0.5))))(),
                  reads=("dsub",), writes=("tnh",))
                P("vector", lambda e: e.tensor_scalar(sb["sgt"][:, :], sb["tnh"][:, :],
                                                      0.5, 0.5, ALU.mult, ALU.add),
                  reads=("tnh",), writes=("sgt",))
                P("vector", (lambda s4=s4: (lambda e: e.tensor_tensor(
                    sb["gt"][:, :], sb["sgt"][:, :],
                    sb["dsub"][:, 1024 * s4:1024 * s4 + 1024], ALU.mult)))(),
                  reads=("sgt", "dsub"), writes=("gt",))
                P("vector", (lambda sv=sv: (lambda e: e.tensor_tensor(
                    sb["sgt"][:, :], sb["gt"][:, :], sb[sv][:, :], ALU.mult)))(),
                  reads=("gt", sv), writes=("sgt",))
                P("vector", (lambda xv=xv, ov=ov: (lambda e: e.tensor_tensor(
                    sb[ov][:, :], sb["sgt"][:, :], sb[xv][:, :], ALU.add)))(),
                  reads=("sgt", xv), writes=(ov,))
                P("scalar", (lambda ci=ci, ov=ov: (lambda e: e.dma_start(
                    out=out_h[128 * ci:128 * ci + 128, :], in_=sb[ov][:, :])))(),
                  reads=(ov,), writes=("dr_out",), kind="dma")
    else:
        p.barrier()
        for ci in range(4 * NT):
            sv = f"ssb{ci % 2}"
            P("sync", (lambda ci=ci, sv=sv: (lambda e: e.dma_start(
                out=sb[sv][:, :], in_=dr["s"][128 * ci:128 * ci + 128, :])))(),
              reads=("dr_s",), writes=(sv,))
            P("sync", (lambda ci=ci, sv=sv: (lambda e: e.dma_start(
                out=out_h[128 * ci:128 * ci + 128, :], in_=sb[sv][:, :])))(),
              reads=(sv,), writes=("dr_out",))

    for nm in taps:
        th = nc.dram_tensor(f"tap_{nm}", list(sb[nm].shape), sb[nm].dtype,
                            kind="ExternalOutput").ap()
        P("sync", (lambda nm=nm, th=th: (lambda e: e.dma_start(out=th[:, :], in_=sb[nm][:, :])))(),
          reads=(nm,), writes=(f"tap_{nm}",))

    p.barrier()
    p.emit(nc, sb["padbuf"][:, :])
    nc._dbg_sb = sb
    return nc


def kernel(**inputs):
    x = np.ascontiguousarray(np.asarray(inputs["L0_attn_x"], dtype=np.float32))
    base = {
        "wq": np.ascontiguousarray(inputs["L0_attn_Wq"], dtype=np.float32),
        "wk": np.ascontiguousarray(inputs["L0_attn_Wk"], dtype=np.float32),
        "wv": np.ascontiguousarray(inputs["L0_attn_Wv"], dtype=np.float32),
        "wo": np.ascontiguousarray(inputs["L0_attn_Wo"], dtype=np.float32),
        "wg": np.ascontiguousarray(inputs["L0_attn_Wg"], dtype=np.float32),
        "wog": np.ascontiguousarray(inputs["L0_attn_Wog"], dtype=np.float32),
        "wd": np.ascontiguousarray(inputs["L0_attn_Wd"], dtype=np.float32),
        "bd": np.ascontiguousarray(np.asarray(inputs["L0_attn_bd"], dtype=np.float32).reshape(H, 1)),
        "normw": np.ascontiguousarray(np.asarray(inputs["L0_attn_norm_w"], dtype=np.float32).reshape(8, 128)),
        "wup": np.ascontiguousarray(inputs["L0_ffn_w_up"], dtype=np.float32),
        "wdown": np.ascontiguousarray(inputs["L0_ffn_w_down"], dtype=np.float32),
    }
    nc = build_program()
    in_maps = [dict(base, x=np.ascontiguousarray(x[i])) for i in range(8)]
    res = run_bass_kernel_spmd(nc, in_maps, core_ids=list(range(8)))
    out = np.stack([r["out"] for r in res.results], axis=0)
    return out.astype(np.float32)


# version 15 (final)
# speedup vs baseline: 1.0363x; 1.0142x over previous
import numpy as np
import ml_dtypes
import concourse.bass as bass
import concourse.mybir as mybir
from concourse.bass_utils import run_bass_kernel_spmd

F32 = mybir.dt.float32
BF16 = mybir.dt.bfloat16
AF = mybir.ActivationFunctionType
ALU = mybir.AluOpType
AX = mybir.AxisListType

SEQ, DM, H, DH, DFF = 4096, 1024, 16, 64, 4096
C = 128
TILE = 512
T2 = 512
EPS_LN = 1e-5
EPS_RMS = float(np.finfo(np.float32).eps)
EPS_TR = 1e-8
BIGNEG = -1e6

ENGS = ("sync", "gpsimd", "tensor", "vector", "scalar")
DMA_ENGS = ("sync", "gpsimd")
DMA_QUEUES = ("sync", "gpsimd", "scalar")


class Plan:

    def __init__(self):
        self.ops = []
        self.last_writer = {}
        self.readers = {}

    def op(self, eng, fn, reads=(), writes=(), kind=None, pad=0):
        idx = len(self.ops)
        if kind is None:
            kind = "dma" if eng in DMA_ENGS else "comp"
        waits = set()
        soft = set()
        for b in reads:
            for e2, w in self.last_writer.get(b, {}).items():
                (waits if e2 != eng else soft).add(w)
            self.readers.setdefault(b, {})[eng] = idx
        for b in writes:
            for e2, w in self.last_writer.get(b, {}).items():
                if w != idx:
                    (waits if e2 != eng else soft).add(w)
            for e2, r in self.readers.get(b, {}).items():
                if r != idx:
                    (waits if e2 != eng else soft).add(r)
            self.readers[b] = {eng: idx}
            self.last_writer[b] = {eng: idx}
        self.ops.append([eng, fn, waits, kind, soft, pad])
        return idx

    def barrier(self):
        lasts = {}
        for i, op_ in enumerate(self.ops):
            e = op_[0]
            lasts[e] = i
        fence = {}
        for e in ENGS:
            if e not in lasts:
                continue
            deps = {lasts[e2] for e2 in lasts if e2 != e}
            fence[e] = len(self.ops)
            self.ops.append([e, lambda eng: eng.nop(), deps, "comp", set(), 0])
        self.last_writer = {}
        self.readers = {}
        for e, i in fence.items():
            self.last_writer.setdefault("__barrier__", {})[e] = i
        return fence

    def emit(self, nc, padbuf):
        plan_ops = self.ops
        marked = set()
        for eng, fn, waits, kind, soft, pad in plan_ops:
            marked |= waits
        for i, (eng, fn, waits, kind, soft, pad) in enumerate(plan_ops):
            if kind == "dma":
                marked.add(i)
        semval = {}
        opval = {}
        for i, (eng, fn, waits, kind, soft, pad) in enumerate(plan_ops):
            if i in marked:
                key = (eng, kind)
                semval[key] = semval.get(key, 0) + (16 if kind == "dma" else 1)
                opval[i] = semval[key]
        sems = {}
        for (eng, kind) in semval:
            sems[(eng, kind)] = nc.semaphore(f"sem_{eng}_{kind}").__enter__()
        dma_prefix = {e: [] for e in DMA_QUEUES}
        for i, (eng, fn, waits, kind, soft, pad) in enumerate(plan_ops):
            if kind == "dma" and i in marked:
                dma_prefix[eng].append((i, opval[i]))

        def dma_wait_val(eng, before_idx):
            import bisect
            lst = dma_prefix[eng]
            pos = bisect.bisect_left(lst, (before_idx, -1))
            return lst[pos - 1][1] if pos > 0 else 0

        thresholds = {e: set() for e in DMA_QUEUES}
        for i, (eng, fn, waits, kind, soft, pad) in enumerate(plan_ops):
            for w in waits:
                if plan_ops[w][3] == "dma" and plan_ops[w][0] != eng:
                    thresholds[plan_ops[w][0]].add(dma_wait_val(plan_ops[w][0], i))
        thr_sorted = {e: sorted(v) for e, v in thresholds.items()}

        def body_for(eng_name):
            def body(eng):
                import bisect
                waited = {}
                self_waited = 0
                cum = 0
                for i, (e, fn, waits, kind, soft, pad) in enumerate(plan_ops):
                    if e != eng_name:
                        continue
                    allw = sorted(waits) + sorted(w for w in soft if w in marked)
                    for w in allw:
                        we, wkind = plan_ops[w][0], plan_ops[w][3]
                        if wkind == "dma":
                            v = dma_wait_val(we, i) if we != eng_name else opval[w]
                            skey = (we, "dma")
                        else:
                            v = opval[w]
                            skey = (we, "comp")
                        if v == 0 or waited.get(skey, 0) >= v:
                            continue
                        eng.wait_ge(sems[skey], v)
                        waited[skey] = v
                    if kind == "dma" and i in marked:
                        lst = thr_sorted[eng_name]
                        pos = bisect.bisect_right(lst, cum)
                        tstar = lst[pos - 1] if pos > 0 else 0
                        if tstar > self_waited:
                            eng.wait_ge(sems[(eng_name, "dma")], tstar)
                            self_waited = tstar
                    inst = fn(eng)
                    if pad and eng_name == "vector":
                        inst = eng.tensor_copy(padbuf[:, 1:601:2], padbuf[:, 0:600:2])
                    elif pad and eng_name == "scalar":
                        inst = eng.activation(padbuf[:, 1:601:2], padbuf[:, 0:600:2],
                                              mybir.ActivationFunctionType.Copy)
                    if i in marked:
                        amt = 16 if kind == "dma" else 1
                        inst.then_inc(sems[(eng_name, kind)], amt)
                        if kind == "dma":
                            cum += amt
            return body

        blk = nc.Block().__enter__()
        for e in ENGS:
            if any(o[0] == e for o in plan_ops):
                getattr(blk, e)(body_for(e))
        blk.__exit__(None, None, None)


def _consts():
    bf = ml_dtypes.bfloat16
    id128 = np.eye(128, dtype=bf)
    idf32 = np.eye(128, dtype=np.float32)
    blk = np.where(np.arange(128)[:, None] > np.arange(128)[None, :], BIGNEG, 0.0)
    negbig = np.tile(blk, (1, 4)).astype(bf)
    eg = np.zeros((16, 1024), dtype=bf)
    for g in range(8):
        for pp in range(128):
            eg[2 * g + pp // 64, 128 * g + pp] = 1
    p2 = np.zeros((16, 128), dtype=np.float32)
    for h in range(16):
        for pp in range(128):
            if h % 2 == pp // 64:
                p2[h, pp] = 1
    dg = np.zeros((16, 8), dtype=np.float32)
    for h in range(16):
        dg[h, h // 2] = 1
    onesbf = np.ones((128, 1), dtype=bf)
    onesf11 = np.ones((1, 1), dtype=np.float32)
    onesf128 = np.ones((128, 1), dtype=np.float32)
    onesf1x = np.ones((1, 128), dtype=np.float32)
    cb = np.ones((16, 512), dtype=np.float32)
    cb[:, 0::128] = 0.0
    return dict(id128=id128, idf32=idf32, negbig=negbig, eg=eg, p2=p2, dg=dg,
                onesbf=onesbf, onesf11=onesf11, onesf128=onesf128,
                onesf1x=onesf1x, cb=cb)


def build_program(n_tiles=8, phase2=True, taps=(), scan_chunks=4, scan_mask=240767):
    nc = bass.Bass("TRN2", target_bir_lowering=False, detect_race_conditions=False)
    p = Plan()
    cst = _consts()
    NT = n_tiles
    NT2 = SEQ // T2 if phase2 else 0

    x_h = nc.dram_tensor("x", [SEQ, DM], F32, kind="ExternalInput").ap()
    w_h = {}
    for nm, shp in [("wq", [DM, DM]), ("wk", [DM, DM]), ("wv", [DM, DM]),
                    ("wo", [DM, DM]), ("wg", [DM, DM]), ("wog", [DM, DM]),
                    ("wd", [H, DM]), ("wup", [DFF, DM]), ("wdown", [DM, DFF])]:
        w_h[nm] = nc.dram_tensor(nm, shp, F32, kind="ExternalInput").ap()
    bd_h = nc.dram_tensor("bd", [H, 1], F32, kind="ExternalInput").ap()
    nw_h = nc.dram_tensor("normw", [8, 128], F32, kind="ExternalInput").ap()
    out_h = nc.dram_tensor("out", [SEQ, DM], F32, kind="ExternalOutput").ap()

    dr = {}
    for nm in ["wq", "wk", "wv", "wo", "wg", "wog"]:
        dr[nm] = nc.dram_tensor(f"{nm}_bf", [DM, DM], BF16, kind="Internal").ap()
        dr[f"{nm}T"] = nc.dram_tensor(f"{nm}_bfT", [DM, DM], BF16, kind="Internal").ap()
    dr["wd"] = nc.dram_tensor("wd_bf", [H, DM], BF16, kind="Internal").ap()
    dr["xn"] = nc.dram_tensor("xn_sc", [SEQ, DM], BF16, kind="Internal").ap()
    dr["xnn"] = nc.dram_tensor("xnn_sc", [SEQ, DM], BF16, kind="Internal").ap()
    dr["s"] = nc.dram_tensor("s_sc", [SEQ, DM], F32, kind="Internal").ap()
    dr["ast"] = nc.dram_tensor("ast_sc", [8 * 64, TILE], BF16, kind="Internal").ap()
    dr["wutT"] = nc.dram_tensor("wutT_sc", [DM, DFF], BF16, kind="Internal").ap()
    dr["wdtT"] = nc.dram_tensor("wdtT_sc", [DFF, DM], BF16, kind="Internal").ap()

    cst_h = {k: nc.inline_tensor(v, name=f"c_{k}").ap() for k, v in cst.items()}

    sb = {}

    def alloc(name, shape, dtype):
        sb[name] = nc.alloc_sbuf_tensor(name, shape, dtype)

    for k, v in cst.items():
        alloc(k, list(v.shape), BF16 if v.dtype == ml_dtypes.bfloat16 else F32)
    alloc("nwrows", [8, 128], F32)
    alloc("nwfm", [128, 8], F32)
    alloc("bdcol", [H, 1], F32)
    alloc("wbuf0", [128, 8192], BF16)
    alloc("wbuf1", [128, 8192], BF16)
    alloc("wdT", [128, 128], BF16)
    for nm in ["sums", "ssqc", "mu", "var", "rstd4", "c4", "rc4", "nb1", "nb2",
               "t4a", "t4b", "t4c"]:
        alloc(nm, [128, 4], F32)
    alloc("xntm1", [128, DM], BF16)
    for i in range(2):
        alloc(f"xnntm{i}", [128, DM], BF16)
    pool = nc.alloc_sbuf_tensor("pool", [128, 57344], mybir.dt.uint8)
    pool_base = nc.lookup_mloc(pool).addr

    def alloc_at(name, shape, dtype, off):
        sb[name] = nc.alloc_sbuf_tensor_at(name, shape, dtype, offset=pool_base + off)

    alloc("qfm", [128, 8 * 512], BF16)
    alloc("kgfm", [128, 8 * 512], BF16)
    alloc_at("vtm", [128, 4 * 1024], BF16, 0)
    alloc_at("sigg", [128, 8 * 512], BF16, 8192)
    alloc_at("sigog", [128, 8 * 512], BF16, 16384)
    alloc_at("y1", [128, 8 * 512], BF16, 24576)
    alloc_at("xnfm", [128, 8 * 512], BF16, 32768)
    alloc_at("repb", [128, 2048], BF16, 40960)
    alloc_at("khfm", [128, 8 * 128], BF16, 45056)
    alloc_at("khtm", [128, 8 * 128], BF16, 47104)
    alloc_at("dbuf", [128, 512], F32, 49152)
    alloc_at("pbuf", [128, 512], BF16, 51200)
    alloc_at("ahib", [16, 512], BF16, 52224)
    alloc_at("alob", [16, 512], BF16, 53248)
    alloc_at("nahib", [16, 512], BF16, 54272)
    alloc_at("nalob", [16, 512], BF16, 55296)
    alloc_at("hfm", [128, 32 * 512], BF16, 0)
    alloc_at("xnnt", [128, 8 * 512], BF16, 32768)
    alloc_at("dsub", [128, 4 * 1024], F32, 40960)
    pool2 = nc.alloc_sbuf_tensor("pool2", [128, 40960], mybir.dt.uint8)
    pool2_base = nc.lookup_mloc(pool2).addr

    def alloc_at2(name, shape, dtype, off):
        sb[name] = nc.alloc_sbuf_tensor_at(name, shape, dtype, offset=pool2_base + off)

    alloc_at2("sev0", [128, DM], F32, 0)
    alloc_at2("sev1", [128, DM], F32, 4096)
    alloc_at2("wfp0", [128, DM], F32, 8192)
    alloc_at2("wfp1", [128, DM], F32, 12288)
    alloc_at2("xb0", [128, DM], F32, 16384)
    alloc_at2("xb1", [128, DM], F32, 20480)
    alloc_at2("xb2", [128, DM], F32, 24576)
    alloc("xb3", [128, DM], F32)
    alloc_at2("junk", [128, DM], F32, 28672)
    alloc_at2("terna", [128, DM], BF16, 32768)
    alloc_at2("ternb", [128, DM], BF16, 34816)
    alloc_at2("ternw", [128, DM], BF16, 36864)
    alloc_at2("xntm0", [128, DM], BF16, 38912)
    alloc_at2("tnh", [128, 1024], F32, 0)
    alloc_at2("sgt", [128, 1024], F32, 4096)
    alloc_at2("gt", [128, 1024], F32, 8192)
    alloc_at2("ssb0", [128, DM], F32, 12288)
    alloc_at2("ssb1", [128, DM], F32, 16384)
    alloc_at2("xsb0", [128, DM], F32, 20480)
    alloc_at2("xsb1", [128, DM], F32, 24576)
    alloc_at2("osb0", [128, DM], F32, 28672)
    alloc_at2("osb1", [128, DM], F32, 32768)
    alloc_at2("wdnr0", [128, 512], BF16, 36864)
    alloc_at2("wdnr1", [128, 512], BF16, 37888)
    alloc_at2("wdnr2", [128, 512], BF16, 38912)
    alloc_at2("wdnr3", [128, 512], BF16, 39936)
    for i in range(2):
        alloc(f"tanhg{i}", [128, 512], BF16)
    alloc("zb", [16, 512], F32)
    alloc("spa", [16, 512], F32)
    alloc("spb", [16, 512], F32)
    alloc("gfm", [16, 512], F32)
    alloc("afm", [16, 512], F32)
    alloc("ahi32", [16, 512], F32)
    alloc("stgL", [4, 2048], BF16)
    alloc("stgR", [4, 2048], BF16)
    alloc("tmpw", [16, 512], F32)
    alloc("expaw", [16, 512], F32)
    alloc("emaw", [16, 512], F32)
    alloc("expbf", [16, 1024], BF16)
    alloc("eAdall", [16, 32], F32)
    alloc("eAcol", [128, 32], F32)
    alloc("qtil", [128, 8 * 128], BF16)
    alloc("sqb", [128, 1024], BF16)
    alloc("ssqrow", [1, 512], F32)
    alloc("S32", [128, 8 * 64], F32)
    alloc("Sbf", [128, 8 * 64], BF16)
    alloc("rstdrow", [1, 512], F32)
    alloc("rcol4", [128, 4], F32)
    alloc("acc_up", [128, 1], F32)
    alloc("acc_dn", [128, 1], F32)
    alloc("red1", [128, 1], F32)
    alloc("thrc", [128, 2], F32)
    alloc("nthrc", [128, 2], F32)
    alloc("thr11", [1, 2], F32)
    alloc("tern_t", [128, 512], BF16)
    alloc("zrow", [1, 512], F32)
    alloc("padbuf", [128, 608], BF16)
    alloc("zc4", [128, 4], F32)

    ps = {}
    for nm in ["pbig0", "pbig1", "psc", "pt1", "po", "po2", "psmall"]:
        ps[nm] = nc.alloc_psum_tensor(nm, [128, 512], F32)
    ps["ptr"] = nc.alloc_psum_tensor("ptr", [128, 512], BF16)

    P = p.op

    _f32_consts = [1.0 / DM, EPS_LN, EPS_TR, -1.0, 0.0, 0.5, 1.0, 0.5 / (DFF * DM)]
    _bf_consts = [-1.0, 0.5, 1.0, 0.0]
    alloc("cvf", [128, len(_f32_consts)], F32)
    alloc("cvb", [128, len(_bf_consts)], BF16)
    for i, v in enumerate(_f32_consts):
        P("vector", (lambda i=i, v=v: (lambda e: e.memset(sb["cvf"][:, i:i + 1], v)))(),
          writes=("cvf",))
        nc.const_aps.register(F32, v, sb["cvf"][:, i:i + 1])
    for i, v in enumerate(_bf_consts):
        P("vector", (lambda i=i, v=v: (lambda e: e.memset(sb["cvb"][:, i:i + 1], v)))(),
          writes=("cvb",))
        nc.const_aps.register(BF16, v, sb["cvb"][:, i:i + 1])

    for k in cst:
        P("sync", (lambda k=k: (lambda e: e.dma_start(out=sb[k][:, :], in_=cst_h[k][:, :])))(),
          writes=(k,))
    P("sync", lambda e: e.dma_start(out=sb["bdcol"][:, :], in_=bd_h[:, :]), writes=("bdcol",))
    P("sync", lambda e: e.dma_start(out=sb["nwrows"][:, :], in_=nw_h[:, :]), writes=("nwrows",))
    P("tensor", lambda e: e.transpose(ps["psmall"][:, 0:8], sb["nwrows"][:, :], sb["idf32"][0:8, 0:8]),
      reads=("nwrows", "idf32"), writes=("psmall",))
    P("vector", lambda e: e.tensor_copy(sb["nwfm"][:, :], ps["psmall"][:, 0:8]),
      reads=("psmall",), writes=("nwfm",))
    for nm in ["wq", "wk", "wv", "wo", "wg", "wog", "wd"]:
        P("gpsimd", (lambda nm=nm: (lambda e: e.dma_start(out=dr[nm][:, :], in_=w_h[nm][:, :])))(),
          writes=(f"dr_{nm}",))
    for k in range(8):
        P("sync", (lambda k=k: (lambda e: e.dma_start(
            out=sb["wdT"][:, 16 * k:16 * k + 16],
            in_=dr["wd"][:, 128 * k:128 * k + 128], transpose=True)))(),
          reads=("dr_wd",), writes=("wdT",))
    P("vector", lambda e: e.memset(sb["S32"][:, :], 0.0), writes=("S32",))
    P("vector", lambda e: e.memset(sb["Sbf"][:, :], 0.0), writes=("Sbf",))
    P("vector", lambda e: e.memset(sb["stgL"][:, :], 1.0), writes=("stgL",))
    P("vector", lambda e: e.memset(sb["stgR"][:, :], 1.0), writes=("stgR",))
    P("vector", lambda e: e.memset(sb["acc_up"][:, :], 0.0), writes=("acc_up",))
    P("vector", lambda e: e.memset(sb["acc_dn"][:, :], 0.0), writes=("acc_dn",))
    P("vector", lambda e: e.memset(sb["ssqrow"][:, :], 0.0), writes=("ssqrow",))
    P("vector", lambda e: e.memset(sb["padbuf"][:, :], 0.0), writes=("padbuf",))
    P("vector", lambda e: e.memset(sb["y1"][:, :], 0.0), writes=("y1",))
    p.barrier()

    for c in range(4):
        P("sync", (lambda c=c: (lambda e: e.dma_start(
            out=sb[f"xb{c}"][:, :], in_=x_h[128 * c:128 * c + 128, :])))(),
          writes=(f"xb{c}",))

    def pretranspose_weights():
        for wi, nm in enumerate(["wo", "wg", "wk", "wq", "wog", "wv"]):
            buf = "wbuf0" if wi % 2 == 0 else "wbuf1"
            for k in range(8):
                P("sync", (lambda nm=nm, buf=buf, k=k: (lambda e: e.dma_start(
                    out=sb[buf][:, 1024 * k:1024 * k + 1024],
                    in_=dr[nm][:, 128 * k:128 * k + 128], transpose=True)))(),
                  reads=(f"dr_{nm}",), writes=(buf,))
            for k in range(8):
                P("scalar", (lambda nm=nm, buf=buf, k=k: (lambda e: e.dma_start(
                    out=dr[f"{nm}T"][128 * k:128 * k + 128, :],
                    in_=sb[buf][:, 1024 * k:1024 * k + 1024])))(),
                  reads=(buf,), writes=(f"dr_{nm}T",), kind="dma")

    def stream_weight(nm, buf, direct=False):
        for k in range(8):
            if direct:
                P("sync", (lambda nm=nm, buf=buf, k=k: (lambda e: e.dma_start(
                    out=sb[buf][:, 1024 * k:1024 * k + 1024],
                    in_=dr[nm][:, 128 * k:128 * k + 128], transpose=True)))(),
                  reads=(f"dr_{nm}",), writes=(buf,))
            else:
                P("sync", (lambda nm=nm, buf=buf, k=k: (lambda e: e.dma_start(
                    out=sb[buf][:, 1024 * k:1024 * k + 1024],
                    in_=dr[f"{nm}T"][128 * k:128 * k + 128, :])))(),
                  reads=(f"dr_{nm}T",), writes=(buf,))

    def ln_chunk(t, c):
        ci = 4 * t + c
        xb = f"xb{ci % 4}"
        if t > 1:
            P("sync", (lambda ci=ci, xb=xb: (lambda e: e.dma_start(
                out=sb[xb][:, :], in_=x_h[128 * ci:128 * ci + 128, :])))(),
              writes=(xb,))
        P("vector", (lambda xb=xb, c=c: (lambda e: e.reduce_sum(
            sb["sums"][:, c:c + 1], sb[xb][:, :], axis=AX.X)))(),
          reads=(xb,), writes=("sums",))
        P("scalar", (lambda xb=xb: (lambda e: e.activation(
            sb["junk"][:, :], sb[xb][:, :], AF.Square)))(),
          reads=(xb,), writes=("junk",))
        P("vector", (lambda c=c: (lambda e: e.reduce_sum(
            sb["ssqc"][:, c:c + 1], sb["junk"][:, :], axis=AX.X)))(),
          reads=("junk",), writes=("ssqc",))

    def ln_stats():
        P("vector", lambda e: e.tensor_scalar(sb["mu"][:, :], sb["sums"][:, :],
                                              1.0 / DM, None, ALU.mult),
          reads=("sums",), writes=("mu",), pad=500)
        P("vector", lambda e: e.tensor_tensor(sb["t4a"][:, :], sb["mu"][:, :],
                                              sb["mu"][:, :], ALU.mult),
          reads=("mu",), writes=("t4a",), pad=500)
        P("vector", lambda e: e.scalar_tensor_tensor(
            sb["var"][:, :], sb["ssqc"][:, :], 1.0 / DM, sb["t4a"][:, :],
            ALU.mult, ALU.subtract),
          reads=("ssqc", "t4a"), writes=("var",), pad=500)
        P("scalar", lambda e: e.activation(sb["t4b"][:, :], sb["var"][:, :], AF.Ln,
                                           bias=np.float32(EPS_LN)),
          reads=("var",), writes=("t4b",), pad=500)
        P("scalar", lambda e: e.activation(sb["rstd4"][:, :], sb["t4b"][:, :], AF.Exp,
                                           scale=np.float32(-0.5)),
          reads=("t4b",), writes=("rstd4",), pad=500)
        P("scalar", lambda e: e.activation(sb["t4a"][:, :], sb["var"][:, :], AF.Ln),
          reads=("var",), writes=("t4a",), pad=500)
        P("vector", lambda e: e.tensor_tensor(sb["t4a"][:, :], sb["t4a"][:, :],
                                              sb["t4b"][:, :], ALU.subtract),
          reads=("t4a", "t4b"), writes=("t4a",), pad=500)
        P("scalar", lambda e: e.activation(sb["t4c"][:, :], sb["t4a"][:, :], AF.Exp,
                                           scale=np.float32(0.5)),
          reads=("t4a",), writes=("t4c",), pad=500)
        P("scalar", lambda e: e.activation(sb["t4a"][:, :], sb["t4c"][:, :], AF.Ln,
                                           bias=np.float32(EPS_TR)),
          reads=("t4c",), writes=("t4a",), pad=500)
        P("scalar", lambda e: e.activation(sb["c4"][:, :], sb["t4a"][:, :], AF.Exp,
                                           scale=np.float32(-1.0)),
          reads=("t4a",), writes=("c4",), pad=500)
        P("vector", lambda e: e.tensor_tensor(sb["rc4"][:, :], sb["rstd4"][:, :],
                                              sb["c4"][:, :], ALU.mult),
          reads=("rstd4", "c4"), writes=("rc4",), pad=500)

    def xn_chunk(t, c):
        ci = 4 * t + c
        xb = f"xb{ci % 4}"
        xn, xnn = f"xntm{c % 2}", f"xnntm{c % 2}"
        P("vector", (lambda xb=xb, xn=xn, c=c: (lambda e: e.tensor_scalar(
            sb[xn][:, :], sb[xb][:, :], sb["mu"][:, c:c + 1],
            sb["rstd4"][:, c:c + 1], ALU.subtract, ALU.mult)))(),
          reads=(xb, "mu", "rstd4"), writes=(xn,))
        P("vector", (lambda xb=xb, xnn=xnn, c=c: (lambda e: e.tensor_scalar(
            sb[xnn][:, :], sb[xb][:, :], sb["mu"][:, c:c + 1],
            sb["rc4"][:, c:c + 1], ALU.subtract, ALU.mult)))(),
          reads=(xb, "mu", "rc4"), writes=(xnn,))
        P("scalar", (lambda ci=ci, xnn=xnn: (lambda e: e.dma_start(
            out=dr["xnn"][128 * ci:128 * ci + 128, :], in_=sb[xnn][:, :])))(),
          reads=(xnn,), writes=("dr_xnn",), kind="dma")

    def load_xnfm(c):
        if True:
            xn = f"xntm{c % 2}"
            for k in range(8):
                slot = (8 * c + k) % 4
                P("tensor", (lambda xn=xn, k=k, slot=slot: (lambda e: e.transpose(
                    ps["ptr"][:, 128 * slot:128 * slot + 128],
                    sb[xn][:, 128 * k:128 * k + 128], sb["id128"][:, :])))(),
                  reads=(xn, "id128"), writes=("ptr",))
                P("vector", (lambda c=c, k=k, slot=slot: (lambda e: e.tensor_copy(
                    sb["xnfm"][:, 512 * k + 128 * c:512 * k + 128 * c + 128],
                    ps["ptr"][:, 128 * slot:128 * slot + 128])))(),
                  reads=("ptr",), writes=("xnfm",))

    def proj_fm(buf, evac):
        for g in range(8):
            pb = "pbig0" if g % 2 == 0 else "pbig1"
            for k in range(8):
                P("tensor", (lambda buf=buf, g=g, k=k, pb=pb: (lambda e: e.matmul(
                    ps[pb][:, :],
                    sb[buf][:, 1024 * k + 128 * g:1024 * k + 128 * g + 128],
                    sb["xnfm"][:, 512 * k:512 * k + 512],
                    start=(k == 0), stop=(k == 7))))(),
                  reads=(buf, "xnfm"), writes=(pb,))
            evac(g, pb)

    def proj_tm(buf, evac):
        for s4 in range(4):
            for fh in range(2):
                pb = "pbig0" if (2 * s4 + fh) % 2 == 0 else "pbig1"
                for k in range(8):
                    P("tensor", (lambda buf=buf, s4=s4, fh=fh, k=k, pb=pb: (lambda e: e.matmul(
                        ps[pb][:, :],
                        sb["xnfm"][:, 512 * k + 128 * s4:512 * k + 128 * s4 + 128],
                        sb[buf][:, 1024 * k + 512 * fh:1024 * k + 512 * fh + 512],
                        start=(k == 0), stop=(k == 7))))(),
                      reads=(buf, "xnfm"), writes=(pb,))
                evac(s4, fh, pb)

    def dproj_softplus(t):
        for k in range(8):
            P("tensor", (lambda k=k: (lambda e: e.matmul(
                ps["pbig0"][0:16, :], sb["wdT"][:, 16 * k:16 * k + 16],
                sb["xnfm"][:, 512 * k:512 * k + 512],
                start=(k == 0), stop=(k == 7))))(),
              reads=("wdT", "xnfm"), writes=("pbig0",))
        P("vector", lambda e: e.tensor_scalar(sb["zb"][:, :], ps["pbig0"][0:16, :],
                                              sb["bdcol"][:, :], None, ALU.add),
          reads=("pbig0", "bdcol"), writes=("zb",))
        P("scalar", lambda e: e.activation(sb["spa"][:, :], sb["zb"][:, :], AF.Abs),
          reads=("zb",), writes=("spa",))
        P("scalar", lambda e: e.activation(sb["spa"][:, :], sb["spa"][:, :], AF.Exp,
                                           scale=np.float32(-1.0)),
          reads=("spa",), writes=("spa",))
        P("scalar", lambda e: e.activation(sb["spa"][:, :], sb["spa"][:, :], AF.Ln,
                                           bias=np.float32(1.0)),
          reads=("spa",), writes=("spa",))
        P("vector", lambda e: e.tensor_scalar(sb["spb"][:, :], sb["zb"][:, :],
                                              0.0, None, ALU.max),
          reads=("zb",), writes=("spb",))
        P("vector", lambda e: e.scalar_tensor_tensor(
            sb["gfm"][:, :], sb["spb"][:, :], -1.0, sb["spa"][:, :],
            ALU.mult, ALU.subtract),
          reads=("spa", "spb"), writes=("gfm",))
        P("vector", lambda e: e.tensor_tensor_scan(
            sb["afm"][:, :], sb["cb"][:, :], sb["gfm"][:, :], 0.0,
            ALU.mult, ALU.add),
          reads=("cb", "gfm"), writes=("afm",))
        P("vector", lambda e: e.tensor_copy(sb["ahib"][:, :], sb["afm"][:, :]),
          reads=("afm",), writes=("ahib",))
        P("vector", lambda e: e.tensor_copy(sb["ahi32"][:, :], sb["ahib"][:, :]),
          reads=("ahib",), writes=("ahi32",))
        P("vector", lambda e: e.tensor_tensor(sb["alob"][:, :], sb["afm"][:, :],
                                              sb["ahi32"][:, :], ALU.subtract),
          reads=("afm", "ahi32"), writes=("alob",))
        P("vector", lambda e: e.tensor_scalar(sb["nahib"][:, :], sb["ahib"][:, :],
                                              -1.0, None, ALU.mult),
          reads=("ahib",), writes=("nahib",))
        P("vector", lambda e: e.tensor_scalar(sb["nalob"][:, :], sb["alob"][:, :],
                                              -1.0, None, ALU.mult),
          reads=("alob",), writes=("nalob",))
        for bi, nm in enumerate(["ahib", "alob", "nahib", "nalob"]):
            P("gpsimd", (lambda t=t, bi=bi, nm=nm: (lambda e: e.dma_start(
                out=dr["ast"][64 * t + 16 * bi:64 * t + 16 * bi + 16, :],
                in_=sb[nm][:, :])))(),
              reads=(nm,), writes=("dr_ast",))
        for c4i in range(4):
            P("vector", (lambda c4i=c4i: (lambda e: e.tensor_scalar(
                sb["tmpw"][:, 128 * c4i:128 * c4i + 128],
                sb["afm"][:, 128 * c4i:128 * c4i + 128], -1.0,
                sb["afm"][:, 128 * c4i + 127:128 * c4i + 128],
                ALU.mult, ALU.add)))(),
              reads=("afm",), writes=("tmpw",), pad=(500 if c4i == 3 else 0))
        P("scalar", lambda e: e.activation(sb["expaw"][:, :], sb["afm"][:, :], AF.Exp),
          reads=("afm",), writes=("expaw",))
        P("scalar", lambda e: e.activation(sb["emaw"][:, :], sb["tmpw"][:, :], AF.Exp),
          reads=("tmpw",), writes=("emaw",))
        ebv = sb["expbf"][:, :].rearrange("p (c t w) -> p c t w", c=4, t=2, w=128)
        P("vector", (lambda ebv=ebv: (lambda e: e.tensor_copy(
            ebv[:, :, 0, :], sb["expaw"][:, :].rearrange("p (c w) -> p c w", c=4))))(),
          reads=("expaw",), writes=("expbf",))
        P("vector", (lambda ebv=ebv: (lambda e: e.tensor_copy(
            ebv[:, :, 1, :], sb["emaw"][:, :].rearrange("p (c w) -> p c w", c=4))))(),
          reads=("emaw",), writes=("expbf",))
        for c4i in range(4):
            P("vector", (lambda c4i=c4i: (lambda e: e.tensor_scalar(
                sb["eAdall"][:, 8 * c4i:8 * c4i + 8], sb["dg"][:, :],
                sb["expaw"][:, 128 * c4i + 127:128 * c4i + 128], None, ALU.mult)))(),
              reads=("dg", "expaw"), writes=("eAdall",), pad=(500 if c4i == 3 else 0))
        for c4i in range(4):
            P("tensor", (lambda c4i=c4i: (lambda e: e.matmul(
                ps["psmall"][:, 8 * c4i:8 * c4i + 8], sb["p2"][:, :],
                sb["eAdall"][:, 8 * c4i:8 * c4i + 8], start=True, stop=True)))(),
              reads=("p2", "eAdall"), writes=("psmall",))
        P("vector", lambda e: e.tensor_copy(sb["eAcol"][:, :], ps["psmall"][:, 0:32]),
          reads=("psmall",), writes=("eAcol",))

    def scan_chunk(t, c, mask=240767):
        if not (mask & 1):
            return
        P("sync", (lambda t=t, c=c: (lambda e: e.dma_start(
            out=sb["stgL"][2:4, :].rearrange("s (h w) -> s h w", h=16),
            in_=dr["ast"][64 * t + 32:64 * t + 64, 128 * c:128 * c + 128]
                .rearrange("(s h) w -> s h w", s=2))))(),
          reads=("dr_ast",), writes=("stgL",))
        P("sync", (lambda t=t, c=c: (lambda e: e.dma_start(
            out=sb["stgR"][0:2, :].rearrange("s (h w) -> s h w", h=16),
            in_=dr["ast"][64 * t:64 * t + 32, 128 * c:128 * c + 128]
                .rearrange("(s h) w -> s h w", s=2))))(),
          reads=("dr_ast",), writes=("stgR",))
        for rnd in range(2 if (mask & 1024) else 0):
            for gg in range(4):
                g = 4 * rnd + gg
                pb = "pbig0" if gg < 2 else "pbig1"
                off = 256 * (gg % 2)
                P("tensor", (lambda g=g, pb=pb, off=off, c=c: (lambda e: e.matmul(
                    ps[pb][:, off:off + 256],
                    sb["eg"][:, 128 * g:128 * g + 128],
                    sb["expbf"][:, 256 * c:256 * c + 256],
                    start=True, stop=True)))(),
                  reads=("eg", "expbf"), writes=(pb,))
                P("scalar", (lambda g=g, pb=pb, off=off: (lambda e: e.activation(
                    sb["repb"][:, 256 * g:256 * g + 256], ps[pb][:, off:off + 256],
                    AF.Copy)))(),
                  reads=(pb,), writes=("repb",))
        for g in range(8 if (mask & 2048) else 0):
            P("vector", (lambda g=g, c=c: (lambda e: e.tensor_tensor(
                sb["qtil"][:, 128 * g:128 * g + 128],
                sb["qfm"][:, 512 * g + 128 * c:512 * g + 128 * c + 128],
                sb["repb"][:, 256 * g:256 * g + 128], ALU.mult)))(),
              reads=("qfm", "repb"), writes=("qtil",))
            P("vector", (lambda g=g, c=c: (lambda e: e.tensor_tensor(
                sb["khfm"][:, 128 * g:128 * g + 128],
                sb["kgfm"][:, 512 * g + 128 * c:512 * g + 128 * c + 128],
                sb["repb"][:, 256 * g + 128:256 * g + 256], ALU.mult)))(),
              reads=("kgfm", "repb"), writes=("khfm",), pad=(500 if g == 7 else 0))
        for g in range(8 if (mask & 2048) else 0):
            slot = g % 4
            P("tensor", (lambda g=g, slot=slot: (lambda e: e.transpose(
                ps["ptr"][:, 128 * slot:128 * slot + 128],
                sb["khfm"][:, 128 * g:128 * g + 128], sb["id128"][:, :])))(),
              reads=("khfm", "id128"), writes=("ptr",))
            P("vector", (lambda g=g, slot=slot: (lambda e: e.tensor_copy(
                sb["khtm"][:, 128 * g:128 * g + 128],
                ps["ptr"][:, 128 * slot:128 * slot + 128])))(),
              reads=("ptr",), writes=("khtm",))
        if not (mask & 4):
            return
        P("vector", lambda e: e.tensor_copy(sb["Sbf"][:, :], sb["S32"][:, :]),
          reads=("S32",), writes=("Sbf",))
        WAVES = [[0, 2, 4, 6], [1, 3, 5, 7], [8, 10, 12, 14], [9, 11, 13, 15]]
        for w, heads in enumerate(WAVES):
            for j, h in enumerate(heads):
                g, b = h // 2, h % 2
                if mask & 256:
                    b = 0
                if mask & 512:
                    g = 0
                P("tensor", (lambda g=g, b=b, j=j, c=c: (lambda e: e.matmul(
                    ps["psc"][:, 128 * j:128 * j + 128],
                    sb["kgfm"][64 * b:64 * b + 64, 512 * g + 128 * c:512 * g + 128 * c + 128],
                    sb["qfm"][64 * b:64 * b + 64, 512 * g + 128 * c:512 * g + 128 * c + 128],
                    start=True, stop=True)))(),
                  reads=("kgfm", "qfm"), writes=("psc",))
            if not (mask & 16):
                continue
            P("tensor", lambda e: e.matmul(ps["pt1"][:, :], sb["id128"][:, :],
                                           sb["negbig"][:, :], start=True, stop=False),
              reads=("id128", "negbig"), writes=("pt1",))
            for j, h in enumerate(heads):
                P("tensor", (lambda h=h, j=j, last=(j == 3): (lambda e: e.matmul(
                    ps["pt1"][:, 128 * j:128 * j + 128],
                    sb["stgL"][:, 128 * h:128 * h + 128],
                    sb["stgR"][:, 128 * h:128 * h + 128],
                    start=False, stop=last)))(),
                  reads=("stgL", "stgR"), writes=("pt1",))
            if not (mask & 32):
                continue
            P("scalar", lambda e: e.activation(sb["dbuf"][:, :], ps["pt1"][:, :], AF.Exp),
              reads=("pt1",), writes=("dbuf",))
            if not (mask & 64):
                continue
            P("vector", lambda e: e.tensor_tensor(sb["pbuf"][:, :], sb["dbuf"][:, :],
                                                  ps["psc"][:, :], ALU.mult),
              reads=("dbuf", "psc"), writes=("pbuf",))
            if not (mask & 8):
                continue
            def _tile(g):
                return ("po" if g % 2 == 0 else "po2",
                        128 * ((g // 2) % 2) + 256 * (g // 4))
            for pi in range(2):
                for j in (2 * pi, 2 * pi + 1):
                    h = heads[j]
                    g, b = h // 2, h % 2
                    pb, col = _tile(g)
                    P("tensor", (lambda h=h, j=j, b=b, c=c, pb=pb, col=col: (lambda e: e.matmul(
                        ps[pb][64 * b:64 * b + 64, col:col + 128],
                        sb["vtm"][:, 1024 * c + 64 * h:1024 * c + 64 * h + 64],
                        sb["pbuf"][:, 128 * j:128 * j + 128],
                        start=True, stop=False, tile_position=(0, 64 * b))))(),
                      reads=("vtm", "pbuf"), writes=(pb,))
                for j in (2 * pi, 2 * pi + 1):
                    h = heads[j]
                    g, b = h // 2, h % 2
                    pb, col = _tile(g)
                    P("tensor", (lambda g=g, b=b, pb=pb, col=col: (lambda e: e.matmul(
                        ps[pb][64 * b:64 * b + 64, col:col + 128],
                        sb["Sbf"][64 * b:64 * b + 64, 64 * g:64 * g + 64],
                        sb["qtil"][64 * b:64 * b + 64, 128 * g:128 * g + 128],
                        start=False, stop=True,
                        tile_position=(64 * b, 64 * b))))(),
                      reads=("Sbf", "qtil"), writes=(pb,))
            for j, h in enumerate(heads):
                if not (mask & 32768):
                    break
                g, b = h // 2, h % 2
                P("tensor", (lambda g=g, b=b, h=h, c=c: (lambda e: e.matmul(
                    ps["psmall"][64 * b:64 * b + 64, 128 + 64 * (g % 4):128 + 64 * (g % 4) + 64],
                    sb["khtm"][:, 128 * g + 64 * b:128 * g + 64 * b + 64],
                    sb["vtm"][:, 1024 * c + 64 * h:1024 * c + 64 * h + 64],
                    start=True, stop=True, tile_position=(0, 64 * b))))(),
                  reads=("khtm", "vtm"), writes=("psmall",))
            if w % 2 == 0:
                continue
            if not (mask & 8):
                continue
            base_g = 0 if w == 1 else 4
            for gi in range(4):
                if not (mask & 65536):
                    break
                g = base_g + gi
                pb = "po" if g % 2 == 0 else "po2"
                col = 128 * ((g // 2) % 2) + 256 * (g // 4)
                P("vector", (lambda g=g, gi=gi: (lambda e: e.scalar_tensor_tensor(
                    sb["S32"][:, 64 * g:64 * g + 64], sb["S32"][:, 64 * g:64 * g + 64],
                    sb["eAcol"][:, 8 * c + g:8 * c + g + 1],
                    ps["psmall"][:, 128 + 64 * gi:128 + 64 * gi + 64],
                    ALU.mult, ALU.add)))(),
                  reads=("S32", "eAcol", "psmall"), writes=("S32",))
                P("scalar", (lambda g=g, pb=pb, col=col: (lambda e: e.activation(
                    sb["sqb"][:, 128 * g:128 * g + 128],
                    ps[pb][:, col:col + 128], AF.Square)))(),
                  reads=(pb,), writes=("sqb", "po_rdlock"))
                P("vector", (lambda g=g, c=c, pb=pb, col=col: (lambda e: e.scalar_tensor_tensor(
                    sb["y1"][:, 512 * g + 128 * c:512 * g + 128 * c + 128],
                    ps[pb][:, col:col + 128],
                    sb["nwfm"][:, g:g + 1],
                    sb["sigog"][:, 512 * g + 128 * c:512 * g + 128 * c + 128],
                    ALU.mult, ALU.mult)))(),
                  reads=(pb, "nwfm", "sigog", "po_rdlock"), writes=("y1",))
        if not (mask & 131072):
            return
        for g in range(8):
            P("tensor", (lambda g=g: (lambda e: e.matmul(
                ps["psc"][0:1, 0:128], sb["onesbf"][:, :],
                sb["sqb"][:, 128 * g:128 * g + 128],
                start=(g == 0), stop=(g == 7))))(),
              reads=("onesbf", "sqb"), writes=("psc",))
        P("vector", (lambda c=c: (lambda e: e.tensor_copy(
            sb["ssqrow"][:, 128 * c:128 * c + 128], ps["psc"][0:1, 0:128])))(),
          reads=("psc",), writes=("ssqrow",))

    def attn_rstd_window():
        P("scalar", lambda e: e.activation(sb["rstdrow"][:, :], sb["ssqrow"][:, :],
                                           AF.Ln, scale=np.float32(1.0 / DM), bias=np.float32(EPS_RMS)),
          reads=("ssqrow",), writes=("rstdrow",))
        P("scalar", lambda e: e.activation(sb["rstdrow"][:, :], sb["rstdrow"][:, :],
                                           AF.Exp, scale=np.float32(-0.5)),
          reads=("rstdrow",), writes=("rstdrow",))
        for c in range(4):
            P("tensor", (lambda c=c: (lambda e: e.matmul(
                ps["psmall"][:, 32 + c:32 + c + 1],
                sb["rstdrow"][:, 128 * c:128 * c + 128], sb["onesf11"][:, :],
                start=True, stop=True)))(),
              reads=("rstdrow", "onesf11"), writes=("psmall",))
        P("vector", lambda e: e.tensor_copy(sb["rcol4"][:, :], ps["psmall"][:, 32:36]),
          reads=("psmall",), writes=("rcol4",))

    def wo_proj(tprev):
        stream_weight("wo", "wbuf1")
        for s4 in range(4):
            for fh in range(2):
                pb = "pbig0" if (2 * s4 + fh) % 2 == 0 else "pbig1"
                for k in range(8):
                    P("tensor", (lambda s4=s4, fh=fh, k=k, pb=pb: (lambda e: e.matmul(
                        ps[pb][:, :],
                        sb["y1"][:, 512 * k + 128 * s4:512 * k + 128 * s4 + 128],
                        sb["wbuf1"][:, 1024 * k + 512 * fh:1024 * k + 512 * fh + 512],
                        start=(k == 0), stop=(k == 7))))(),
                      reads=("y1", "wbuf1"), writes=(pb,))
                sv = f"sev{s4 % 2}"
                P("vector", (lambda s4=s4, fh=fh, pb=pb, sv=sv: (lambda e: e.tensor_scalar(
                    sb[sv][:, 512 * fh:512 * fh + 512], ps[pb][:, :],
                    sb["rcol4"][:, s4:s4 + 1], None, ALU.mult)))(),
                  reads=(pb, "rcol4"), writes=(sv,))
            ci = 4 * tprev + s4
            P("scalar", (lambda ci=ci, sv=f"sev{s4 % 2}": (lambda e: e.dma_start(
                out=dr["s"][128 * ci:128 * ci + 128, :], in_=sb[sv][:, :])))(),
              reads=(f"sev{s4 % 2}",), writes=("dr_s",), kind="dma")

    def alpha_tile(which, idx):
        wf = f"wfp{idx % 2}"
        if which == "up":
            src = w_h["wup"][128 * idx:128 * idx + 128, :]
            acc = "acc_up"
        else:
            r, q = idx // 4, idx % 4
            src = w_h["wdown"][128 * r:128 * r + 128, 1024 * q:1024 * q + 1024]
            acc = "acc_dn"
        P("sync", (lambda wf=wf, src=src: (lambda e: e.dma_start(out=sb[wf][:, :], in_=src)))(),
          writes=(wf,))
        P("vector", (lambda wf=wf: (lambda e: e.tensor_reduce(
            sb["red1"][:, :], sb[wf][:, :], axis=AX.X, op=ALU.add,
            apply_absolute_value=True)))(),
          reads=(wf,), writes=("red1",))
        P("vector", (lambda acc=acc: (lambda e: e.tensor_tensor(
            sb[acc][:, :], sb[acc][:, :], sb["red1"][:, :], ALU.add)))(),
          reads=("red1", acc), writes=(acc,))

    def thr_compute():
        for i, acc in enumerate(["acc_up", "acc_dn"]):
            P("tensor", (lambda i=i, acc=acc: (lambda e: e.matmul(
                ps["psmall"][0:1, 40 + i:41 + i], sb[acc][:, :], sb["onesf128"][:, :],
                start=True, stop=True)))(),
              reads=(acc, "onesf128"), writes=("psmall",))
        P("vector", lambda e: e.tensor_scalar(sb["thr11"][:, :], ps["psmall"][0:1, 40:42],
                                              0.5 / (DFF * DM), None, ALU.mult),
          reads=("psmall",), writes=("thr11",), pad=500)
        P("tensor", lambda e: e.matmul(ps["psmall"][:, 42:44], sb["onesf1x"][:, :],
                                       sb["thr11"][:, :], start=True, stop=True),
          reads=("onesf1x", "thr11"), writes=("psmall",))
        P("vector", lambda e: e.tensor_copy(sb["thrc"][:, :], ps["psmall"][:, 42:44]),
          reads=("psmall",), writes=("thrc",))
        P("vector", lambda e: e.tensor_scalar(sb["nthrc"][:, :], sb["thrc"][:, :],
                                              -1.0, None, ALU.mult),
          reads=("thrc",), writes=("nthrc",))

    def tern_tile(which, idx):
        wf = f"wfp{idx % 2}"
        col = 0 if which == "up" else 1
        if which == "up":
            src = w_h["wup"][128 * idx:128 * idx + 128, :]
        else:
            r, q = idx // 4, idx % 4
            src = w_h["wdown"][128 * r:128 * r + 128, 1024 * q:1024 * q + 1024]
        P("sync", (lambda wf=wf, src=src: (lambda e: e.dma_start(out=sb[wf][:, :], in_=src)))(),
          writes=(wf,))
        P("vector", (lambda wf=wf, col=col: (lambda e: e.tensor_scalar(
            sb["terna"][:, :], sb[wf][:, :], sb["thrc"][:, col:col + 1], None,
            ALU.is_gt)))(),
          reads=(wf, "thrc"), writes=("terna",))
        P("vector", (lambda wf=wf, col=col: (lambda e: e.tensor_scalar(
            sb["ternb"][:, :], sb[wf][:, :], sb["nthrc"][:, col:col + 1], None,
            ALU.is_lt)))(),
          reads=(wf, "nthrc"), writes=("ternb",))
        P("vector", lambda e: e.tensor_tensor(sb["ternw"][:, :], sb["terna"][:, :],
                                              sb["ternb"][:, :], ALU.subtract),
          reads=("terna", "ternb"), writes=("ternw",))
        for i in range(8):
            slot = i % 4
            P("tensor", (lambda i=i, slot=slot: (lambda e: e.transpose(
                ps["ptr"][:, 128 * slot:128 * slot + 128],
                sb["ternw"][:, 128 * i:128 * i + 128], sb["id128"][:, :])))(),
              reads=("ternw", "id128"), writes=("ptr",))
            P("vector", (lambda slot=slot: (lambda e: e.tensor_copy(
                sb["tern_t"][:, 128 * slot:128 * slot + 128],
                ps["ptr"][:, 128 * slot:128 * slot + 128])))(),
              reads=("ptr",), writes=(f"tt{slot}",))
            if i % 4 != 3:
                continue
            h = i // 4
            if which == "up":
                dst = dr["wutT"][512 * h:512 * h + 512,
                                 128 * idx:128 * idx + 128]
            else:
                r, q = idx // 4, idx % 4
                dst = dr["wdtT"][1024 * q + 512 * h:1024 * q + 512 * h + 512,
                                 128 * r:128 * r + 128]
            P("gpsimd", (lambda dst=dst: (lambda e: e.dma_start(
                out=dst.rearrange("(i p) c -> p i c", i=4),
                in_=sb["tern_t"][:, :].rearrange("p (i c) -> p i c", i=4))))(),
              reads=("tt0", "tt1", "tt2", "tt3"), writes=("dr_wt",))

    for t in range(NT):
        if t > 0:
            attn_rstd_window()
            wo_proj(t - 1)
        for c in range(4):
            ln_chunk(t, c)
        ln_stats()
        for c in range(4):
            xn_chunk(t, c)
            load_xnfm(c)
        dproj_softplus(t)
        stream_weight("wg", "wbuf0", direct=(t == 0))

        def evac_g(g, pb):
            tg = f"tanhg{g % 2}"
            P("scalar", (lambda pb=pb, tg=tg: (lambda e: e.activation(
                sb[tg][:, :], ps[pb][:, :], AF.Tanh, scale=np.float32(0.5))))(),
              reads=(pb,), writes=(tg,))
            P("vector", (lambda g=g, tg=tg: (lambda e: e.tensor_scalar(
                sb["sigg"][:, 512 * g:512 * g + 512], sb[tg][:, :],
                0.5, 0.5, ALU.mult, ALU.add)))(),
              reads=(tg,), writes=("sigg",))
        proj_fm("wbuf0", evac_g)
        stream_weight("wk", "wbuf1", direct=(t == 0))

        def evac_k(g, pb):
            P("vector", (lambda g=g, pb=pb: (lambda e: e.tensor_tensor(
                sb["kgfm"][:, 512 * g:512 * g + 512], ps[pb][:, :],
                sb["sigg"][:, 512 * g:512 * g + 512], ALU.mult)))(),
              reads=(pb, "sigg"), writes=("kgfm",))
        proj_fm("wbuf1", evac_k)
        stream_weight("wq", "wbuf0", direct=(t == 0))

        def evac_q(g, pb):
            P("vector", (lambda g=g, pb=pb: (lambda e: e.tensor_copy(
                sb["qfm"][:, 512 * g:512 * g + 512], ps[pb][:, :])))(),
              reads=(pb,), writes=("qfm",))
        proj_fm("wbuf0", evac_q)
        stream_weight("wog", "wbuf1", direct=(t == 0))

        def evac_og(g, pb):
            tg = f"tanhg{g % 2}"
            P("scalar", (lambda pb=pb, tg=tg: (lambda e: e.activation(
                sb[tg][:, :], ps[pb][:, :], AF.Tanh, scale=np.float32(0.5))))(),
              reads=(pb,), writes=(tg,))
            P("vector", (lambda g=g, tg=tg: (lambda e: e.tensor_scalar(
                sb["sigog"][:, 512 * g:512 * g + 512], sb[tg][:, :],
                0.5, 0.5, ALU.mult, ALU.add)))(),
              reads=(tg,), writes=("sigog",))
        proj_fm("wbuf1", evac_og)
        stream_weight("wv", "wbuf0", direct=(t == 0))

        def evac_v(s4, fh, pb):
            P("vector", (lambda s4=s4, fh=fh, pb=pb: (lambda e: e.tensor_copy(
                sb["vtm"][:, 1024 * s4 + 512 * fh:1024 * s4 + 512 * fh + 512],
                ps[pb][:, :])))(),
              reads=(pb,), writes=("vtm",))
        proj_tm("wbuf0", evac_v)
        for c in range(scan_chunks):
            scan_chunk(t, c, scan_mask)
        if t == 0:
            for c in range(4):
                P("sync", (lambda c=c: (lambda e: e.dma_start(
                    out=sb[f"xb{c}"][:, :],
                    in_=x_h[128 * (4 + c):128 * (4 + c) + 128, :])))(),
                  writes=(f"xb{c}",))
            pretranspose_weights()
        if t < 2:
            for i in range(8):
                alpha_tile("up", 16 * t + 2 * i), alpha_tile("up", 16 * t + 2 * i + 1)
        elif t == 2:
            for i in range(16):
                alpha_tile("dn", i)
        elif t == 3:
            for i in range(16):
                alpha_tile("dn", 16 + i)
            thr_compute()
        else:
            for i in range(8):
                tern_tile("up", 8 * (t - 4) + i)
            for i in range(8):
                tern_tile("dn", 8 * (t - 4) + i)
    attn_rstd_window()
    wo_proj(NT - 1)

    if phase2:
        p.barrier()

        def load_xnnt(t2):
            for k in range(8):
                P("sync", (lambda t2=t2, k=k: (lambda e: e.dma_start(
                    out=sb["xnnt"][:, 512 * k:512 * k + 512],
                    in_=dr["xnn"][T2 * t2:T2 * t2 + T2, 128 * k:128 * k + 128],
                    transpose=True)))(),
                  reads=("dr_xnn",), writes=("xnnt",))

        load_xnnt(0)
        for t2 in range(NT2):
            for jw in range(8):
                wb = "wbuf0" if jw % 2 == 0 else "wbuf1"
                for k in range(8):
                    P("sync", (lambda jw=jw, k=k, wb=wb: (lambda e: e.dma_start(
                        out=sb[wb][:, 512 * k:512 * k + 512],
                        in_=dr["wutT"][128 * k:128 * k + 128,
                                       512 * jw:512 * jw + 512])))(),
                      reads=("dr_wt",), writes=(wb,))
                for jj in range(4):
                    j = 4 * jw + jj
                    pb = "pbig0" if jj % 2 == 0 else "pbig1"
                    for k in range(8):
                        P("tensor", (lambda jj=jj, k=k, pb=pb, wb=wb: (lambda e: e.matmul(
                            ps[pb][:, :],
                            sb[wb][:, 512 * k + 128 * jj:512 * k + 128 * jj + 128],
                            sb["xnnt"][:, 512 * k:512 * k + 512],
                            start=(k == 0), stop=(k == 7))))(),
                          reads=(wb, "xnnt"), writes=(pb,))
                    if j > 0:
                        P("tensor", (lambda jp=j - 1: (lambda e: e.matmul(
                            ps["psc"][0:1, :], sb["onesbf"][:, :],
                            sb["sqb"][:, 512 * (jp % 2):512 * (jp % 2) + 512],
                            start=(jp == 0), stop=False)))(),
                          reads=("onesbf", f"sqb2{(j - 1) % 2}"), writes=("psc",))
                    P("scalar", (lambda j=j, pb=pb: (lambda e: e.activation(
                        sb["hfm"][:, 512 * j:512 * j + 512], ps[pb][:, :], AF.Gelu)))(),
                      reads=(pb,), writes=(f"hfm{j}",))
                    P("scalar", (lambda j=j: (lambda e: e.activation(
                        sb["sqb"][:, 512 * (j % 2):512 * (j % 2) + 512],
                        sb["hfm"][:, 512 * j:512 * j + 512], AF.Square)))(),
                      reads=(f"hfm{j}",), writes=(f"sqb2{j % 2}",))
            P("tensor", lambda e: e.matmul(
                ps["psc"][0:1, :], sb["onesbf"][:, :], sb["sqb"][:, 512:1024],
                start=False, stop=True),
              reads=("onesbf", "sqb21"), writes=("psc",))
            pbanks = ["pt1", "po", "po2", "psmall"]
            for pss in range(2):
                for grp in range(8):
                    for kk in range(4):
                        k = 4 * grp + kk
                        P("sync", (lambda pss=pss, k=k, wr=f"wdnr{kk}": (lambda e: e.dma_start(
                            out=sb[wr][:, :],
                            in_=dr["wdtT"][128 * k:128 * k + 128,
                                           512 * pss:512 * pss + 512])))(),
                          reads=("dr_wt",), writes=(f"wdnr{kk}",))
                    for kk in range(4):
                        k = 4 * grp + kk
                        wr = f"wdnr{kk}"
                        for s4 in range(4):
                            P("tensor", (lambda k=k, s4=s4, wr=wr, pb=pbanks[s4]: (lambda e: e.matmul(
                                ps[pb][:, :],
                                sb["hfm"][:, 512 * k + 128 * s4:512 * k + 128 * s4 + 128],
                                sb[wr][:, :],
                                start=(k == 0), stop=(k == 31))))(),
                              reads=(f"hfm{k}", wr), writes=(pbanks[s4],))
                if pss == 0:
                    if t2 + 1 < NT2:
                        load_xnnt(t2 + 1)
                    P("scalar", lambda e: e.activation(sb["zrow"][:, :], ps["psc"][0:1, :],
                                                       AF.Ln, scale=np.float32(1.0 / DFF)),
                      reads=("psc",), writes=("zrow",))
                    P("scalar", lambda e: e.activation(sb["zrow"][:, :], sb["zrow"][:, :],
                                                       AF.Exp, scale=np.float32(0.5)),
                      reads=("zrow",), writes=("zrow",))
                    P("scalar", lambda e: e.activation(sb["zrow"][:, :], sb["zrow"][:, :],
                                                       AF.Ln, bias=np.float32(EPS_TR)),
                      reads=("zrow",), writes=("zrow",))
                    P("scalar", lambda e: e.activation(sb["zrow"][:, :], sb["zrow"][:, :],
                                                       AF.Exp, scale=np.float32(-1.0)),
                      reads=("zrow",), writes=("zrow",))
                    for c in range(4):
                        P("tensor", (lambda c=c: (lambda e: e.matmul(
                            ps["psc"][:, c:c + 1],
                            sb["zrow"][:, 128 * c:128 * c + 128], sb["onesf11"][:, :],
                            start=True, stop=True)))(),
                          reads=("zrow", "onesf11"), writes=("psc",))
                    P("vector", lambda e: e.tensor_copy(sb["zc4"][:, :], ps["psc"][:, 0:4]),
                      reads=("psc",), writes=("zc4",), pad=500)
                for s4 in range(4):
                    P("vector", (lambda s4=s4, pss=pss, pb=pbanks[s4]: (lambda e: e.tensor_scalar(
                        sb["dsub"][:, 1024 * s4 + 512 * pss:1024 * s4 + 512 * pss + 512],
                        ps[pb][:, :], sb["zc4"][:, s4:s4 + 1], None, ALU.mult)))(),
                      reads=(pbanks[s4], "zc4"), writes=("dsub",))
            for s4 in range(4):
                ci = 4 * t2 + s4
                sv, xv, ov = f"ssb{s4 % 2}", f"xsb{s4 % 2}", f"osb{s4 % 2}"
                P("scalar", (lambda ci=ci, sv=sv: (lambda e: e.dma_start(
                    out=sb[sv][:, :], in_=dr["s"][128 * ci:128 * ci + 128, :])))(),
                  reads=("dr_s",), writes=(sv,), kind="dma")
                P("scalar", (lambda ci=ci, xv=xv: (lambda e: e.dma_start(
                    out=sb[xv][:, :], in_=x_h[128 * ci:128 * ci + 128, :])))(),
                  writes=(xv,), kind="dma")
                P("scalar", (lambda s4=s4: (lambda e: e.activation(
                    sb["tnh"][:, :], sb["dsub"][:, 1024 * s4:1024 * s4 + 1024],
                    AF.Tanh, scale=np.float32(0.5))))(),
                  reads=("dsub",), writes=("tnh",))
                P("vector", lambda e: e.tensor_scalar(sb["sgt"][:, :], sb["tnh"][:, :],
                                                      0.5, 0.5, ALU.mult, ALU.add),
                  reads=("tnh",), writes=("sgt",))
                P("vector", (lambda s4=s4: (lambda e: e.tensor_tensor(
                    sb["gt"][:, :], sb["sgt"][:, :],
                    sb["dsub"][:, 1024 * s4:1024 * s4 + 1024], ALU.mult)))(),
                  reads=("sgt", "dsub"), writes=("gt",))
                P("vector", (lambda sv=sv: (lambda e: e.tensor_tensor(
                    sb["sgt"][:, :], sb["gt"][:, :], sb[sv][:, :], ALU.mult)))(),
                  reads=("gt", sv), writes=("sgt",))
                P("vector", (lambda xv=xv, ov=ov: (lambda e: e.tensor_tensor(
                    sb[ov][:, :], sb["sgt"][:, :], sb[xv][:, :], ALU.add)))(),
                  reads=("sgt", xv), writes=(ov,))
                P("scalar", (lambda ci=ci, ov=ov: (lambda e: e.dma_start(
                    out=out_h[128 * ci:128 * ci + 128, :], in_=sb[ov][:, :])))(),
                  reads=(ov,), writes=("dr_out",), kind="dma")
    else:
        p.barrier()
        for ci in range(4 * NT):
            sv = f"ssb{ci % 2}"
            P("sync", (lambda ci=ci, sv=sv: (lambda e: e.dma_start(
                out=sb[sv][:, :], in_=dr["s"][128 * ci:128 * ci + 128, :])))(),
              reads=("dr_s",), writes=(sv,))
            P("sync", (lambda ci=ci, sv=sv: (lambda e: e.dma_start(
                out=out_h[128 * ci:128 * ci + 128, :], in_=sb[sv][:, :])))(),
              reads=(sv,), writes=("dr_out",))

    for nm in taps:
        th = nc.dram_tensor(f"tap_{nm}", list(sb[nm].shape), sb[nm].dtype,
                            kind="ExternalOutput").ap()
        P("sync", (lambda nm=nm, th=th: (lambda e: e.dma_start(out=th[:, :], in_=sb[nm][:, :])))(),
          reads=(nm,), writes=(f"tap_{nm}",))

    p.barrier()
    p.emit(nc, sb["padbuf"][:, :])
    nc._dbg_sb = sb
    return nc


def kernel(**inputs):
    x = np.ascontiguousarray(np.asarray(inputs["L0_attn_x"], dtype=np.float32))
    base = {
        "wq": np.ascontiguousarray(inputs["L0_attn_Wq"], dtype=np.float32),
        "wk": np.ascontiguousarray(inputs["L0_attn_Wk"], dtype=np.float32),
        "wv": np.ascontiguousarray(inputs["L0_attn_Wv"], dtype=np.float32),
        "wo": np.ascontiguousarray(inputs["L0_attn_Wo"], dtype=np.float32),
        "wg": np.ascontiguousarray(inputs["L0_attn_Wg"], dtype=np.float32),
        "wog": np.ascontiguousarray(inputs["L0_attn_Wog"], dtype=np.float32),
        "wd": np.ascontiguousarray(inputs["L0_attn_Wd"], dtype=np.float32),
        "bd": np.ascontiguousarray(np.asarray(inputs["L0_attn_bd"], dtype=np.float32).reshape(H, 1)),
        "normw": np.ascontiguousarray(np.asarray(inputs["L0_attn_norm_w"], dtype=np.float32).reshape(8, 128)),
        "wup": np.ascontiguousarray(inputs["L0_ffn_w_up"], dtype=np.float32),
        "wdown": np.ascontiguousarray(inputs["L0_ffn_w_down"], dtype=np.float32),
    }
    nc = build_program()
    in_maps = [dict(base, x=np.ascontiguousarray(x[i])) for i in range(8)]
    res = run_bass_kernel_spmd(nc, in_maps, core_ids=list(range(8)))
    out = np.stack([r["out"] for r in res.results], axis=0)
    return out.astype(np.float32)


# version 16
# speedup vs baseline: 1.0480x; 1.0113x over previous
import numpy as np
import ml_dtypes
import concourse.bass as bass
import concourse.mybir as mybir
from concourse.bass_utils import run_bass_kernel_spmd

F32 = mybir.dt.float32
BF16 = mybir.dt.bfloat16
AF = mybir.ActivationFunctionType
ALU = mybir.AluOpType
AX = mybir.AxisListType

SEQ, DM, H, DH, DFF = 4096, 1024, 16, 64, 4096
C = 128
TILE = 512
T2 = 512
EPS_LN = 1e-5
EPS_RMS = float(np.finfo(np.float32).eps)
EPS_TR = 1e-8
BIGNEG = -1e6

ENGS = ("sync", "gpsimd", "tensor", "vector", "scalar")
DMA_ENGS = ("sync", "gpsimd")
DMA_QUEUES = ("sync", "gpsimd", "scalar")


class Plan:

    def __init__(self):
        self.ops = []
        self.last_writer = {}
        self.readers = {}

    def op(self, eng, fn, reads=(), writes=(), kind=None, pad=0):
        idx = len(self.ops)
        if kind is None:
            kind = "dma" if eng in DMA_ENGS else "comp"
        waits = set()
        soft = set()
        for b in reads:
            for e2, w in self.last_writer.get(b, {}).items():
                (waits if e2 != eng else soft).add(w)
            self.readers.setdefault(b, {})[eng] = idx
        for b in writes:
            for e2, w in self.last_writer.get(b, {}).items():
                if w != idx:
                    (waits if e2 != eng else soft).add(w)
            for e2, r in self.readers.get(b, {}).items():
                if r != idx:
                    (waits if e2 != eng else soft).add(r)
            self.readers[b] = {eng: idx}
            self.last_writer[b] = {eng: idx}
        self.ops.append([eng, fn, waits, kind, soft, pad])
        return idx

    def barrier(self):
        lasts = {}
        for i, op_ in enumerate(self.ops):
            e = op_[0]
            lasts[e] = i
        fence = {}
        for e in ENGS:
            if e not in lasts:
                continue
            deps = {lasts[e2] for e2 in lasts if e2 != e}
            fence[e] = len(self.ops)
            self.ops.append([e, lambda eng: eng.nop(), deps, "comp", set(), 0])
        self.last_writer = {}
        self.readers = {}
        for e, i in fence.items():
            self.last_writer.setdefault("__barrier__", {})[e] = i
        return fence

    def emit(self, nc, padbuf):
        plan_ops = self.ops
        marked = set()
        for eng, fn, waits, kind, soft, pad in plan_ops:
            marked |= waits
        for i, (eng, fn, waits, kind, soft, pad) in enumerate(plan_ops):
            if kind == "dma":
                marked.add(i)
        semval = {}
        opval = {}
        for i, (eng, fn, waits, kind, soft, pad) in enumerate(plan_ops):
            if i in marked:
                key = (eng, kind)
                semval[key] = semval.get(key, 0) + (16 if kind == "dma" else 1)
                opval[i] = semval[key]
        sems = {}
        for (eng, kind) in semval:
            sems[(eng, kind)] = nc.semaphore(f"sem_{eng}_{kind}").__enter__()
        dma_prefix = {e: [] for e in DMA_QUEUES}
        for i, (eng, fn, waits, kind, soft, pad) in enumerate(plan_ops):
            if kind == "dma" and i in marked:
                dma_prefix[eng].append((i, opval[i]))

        def dma_wait_val(eng, before_idx):
            import bisect
            lst = dma_prefix[eng]
            pos = bisect.bisect_left(lst, (before_idx, -1))
            return lst[pos - 1][1] if pos > 0 else 0

        thresholds = {e: set() for e in DMA_QUEUES}
        for i, (eng, fn, waits, kind, soft, pad) in enumerate(plan_ops):
            for w in waits:
                if plan_ops[w][3] == "dma" and plan_ops[w][0] != eng:
                    thresholds[plan_ops[w][0]].add(dma_wait_val(plan_ops[w][0], i))
        thr_sorted = {e: sorted(v) for e, v in thresholds.items()}

        def body_for(eng_name):
            def body(eng):
                import bisect
                waited = {}
                self_waited = 0
                cum = 0
                for i, (e, fn, waits, kind, soft, pad) in enumerate(plan_ops):
                    if e != eng_name:
                        continue
                    allw = sorted(waits) + sorted(w for w in soft if w in marked)
                    for w in allw:
                        we, wkind = plan_ops[w][0], plan_ops[w][3]
                        if wkind == "dma":
                            v = dma_wait_val(we, i) if we != eng_name else opval[w]
                            skey = (we, "dma")
                        else:
                            v = opval[w]
                            skey = (we, "comp")
                        if v == 0 or waited.get(skey, 0) >= v:
                            continue
                        eng.wait_ge(sems[skey], v)
                        waited[skey] = v
                    if kind == "dma" and i in marked:
                        lst = thr_sorted[eng_name]
                        pos = bisect.bisect_right(lst, cum)
                        tstar = lst[pos - 1] if pos > 0 else 0
                        if tstar > self_waited:
                            eng.wait_ge(sems[(eng_name, "dma")], tstar)
                            self_waited = tstar
                    inst = fn(eng)
                    if pad and eng_name == "vector":
                        inst = eng.tensor_copy(padbuf[:, 1:601:2], padbuf[:, 0:600:2])
                    elif pad and eng_name == "scalar":
                        inst = eng.activation(padbuf[:, 1:601:2], padbuf[:, 0:600:2],
                                              mybir.ActivationFunctionType.Copy)
                    if i in marked:
                        amt = 16 if kind == "dma" else 1
                        inst.then_inc(sems[(eng_name, kind)], amt)
                        if kind == "dma":
                            cum += amt
            return body

        blk = nc.Block().__enter__()
        for e in ENGS:
            if any(o[0] == e for o in plan_ops):
                getattr(blk, e)(body_for(e))
        blk.__exit__(None, None, None)


def _consts():
    bf = ml_dtypes.bfloat16
    id128 = np.eye(128, dtype=bf)
    idf32 = np.eye(128, dtype=np.float32)
    blk = np.where(np.arange(128)[:, None] > np.arange(128)[None, :], BIGNEG, 0.0)
    negbig = np.tile(blk, (1, 4)).astype(bf)
    eg = np.zeros((16, 1024), dtype=bf)
    for g in range(8):
        for pp in range(128):
            eg[2 * g + pp // 64, 128 * g + pp] = 1
    p2 = np.zeros((16, 128), dtype=np.float32)
    for h in range(16):
        for pp in range(128):
            if h % 2 == pp // 64:
                p2[h, pp] = 1
    dg = np.zeros((16, 8), dtype=np.float32)
    for h in range(16):
        dg[h, h // 2] = 1
    onesbf = np.ones((128, 1), dtype=bf)
    onesf11 = np.ones((1, 1), dtype=np.float32)
    onesf128 = np.ones((128, 1), dtype=np.float32)
    onesf1x = np.ones((1, 128), dtype=np.float32)
    cb = np.ones((16, 512), dtype=np.float32)
    cb[:, 0::128] = 0.0
    return dict(id128=id128, idf32=idf32, negbig=negbig, eg=eg, p2=p2, dg=dg,
                onesbf=onesbf, onesf11=onesf11, onesf128=onesf128,
                onesf1x=onesf1x, cb=cb)


def build_program(n_tiles=8, phase2=True, taps=(), scan_chunks=4, scan_mask=240767):
    nc = bass.Bass("TRN2", target_bir_lowering=False, detect_race_conditions=False)
    p = Plan()
    cst = _consts()
    NT = n_tiles
    NT2 = SEQ // T2 if phase2 else 0

    x_h = nc.dram_tensor("x", [SEQ, DM], F32, kind="ExternalInput").ap()
    w_h = {}
    for nm, shp in [("wq", [DM, DM]), ("wk", [DM, DM]), ("wv", [DM, DM]),
                    ("wo", [DM, DM]), ("wg", [DM, DM]), ("wog", [DM, DM]),
                    ("wd", [H, DM]), ("wup", [DFF, DM]), ("wdown", [DM, DFF])]:
        w_h[nm] = nc.dram_tensor(nm, shp, F32, kind="ExternalInput").ap()
    bd_h = nc.dram_tensor("bd", [H, 1], F32, kind="ExternalInput").ap()
    nw_h = nc.dram_tensor("normw", [8, 128], F32, kind="ExternalInput").ap()
    out_h = nc.dram_tensor("out", [SEQ, DM], F32, kind="ExternalOutput").ap()

    dr = {}
    for nm in ["wq", "wk", "wv", "wo", "wg", "wog"]:
        dr[nm] = nc.dram_tensor(f"{nm}_bf", [DM, DM], BF16, kind="Internal").ap()
        dr[f"{nm}T"] = nc.dram_tensor(f"{nm}_bfT", [DM, DM], BF16, kind="Internal").ap()
    dr["wd"] = nc.dram_tensor("wd_bf", [H, DM], BF16, kind="Internal").ap()
    dr["xn"] = nc.dram_tensor("xn_sc", [SEQ, DM], BF16, kind="Internal").ap()
    dr["xnn"] = nc.dram_tensor("xnn_sc", [SEQ, DM], BF16, kind="Internal").ap()
    dr["s"] = nc.dram_tensor("s_sc", [SEQ, DM], F32, kind="Internal").ap()
    dr["ast"] = nc.dram_tensor("ast_sc", [8 * 64, TILE], BF16, kind="Internal").ap()
    dr["wutT"] = nc.dram_tensor("wutT_sc", [DM, DFF], BF16, kind="Internal").ap()
    dr["wdtT"] = nc.dram_tensor("wdtT_sc", [DFF, DM], BF16, kind="Internal").ap()

    cst_h = {k: nc.inline_tensor(v, name=f"c_{k}").ap() for k, v in cst.items()}

    sb = {}

    def alloc(name, shape, dtype):
        sb[name] = nc.alloc_sbuf_tensor(name, shape, dtype)

    for k, v in cst.items():
        alloc(k, list(v.shape), BF16 if v.dtype == ml_dtypes.bfloat16 else F32)
    alloc("nwrows", [8, 128], F32)
    alloc("nwfm", [128, 8], F32)
    alloc("bdcol", [H, 1], F32)
    alloc("wbuf0", [128, 8192], BF16)
    alloc("wbuf1", [128, 8192], BF16)
    alloc("wdT", [128, 128], BF16)
    for nm in ["sums", "ssqc", "mu", "var", "rstd4", "c4", "rc4", "nb1", "nb2",
               "t4a", "t4b", "t4c"]:
        alloc(nm, [128, 4], F32)
    alloc("xntm1", [128, DM], BF16)
    for i in range(2):
        alloc(f"xnntm{i}", [128, DM], BF16)
    pool = nc.alloc_sbuf_tensor("pool", [128, 57344], mybir.dt.uint8)
    pool_base = nc.lookup_mloc(pool).addr

    def alloc_at(name, shape, dtype, off):
        sb[name] = nc.alloc_sbuf_tensor_at(name, shape, dtype, offset=pool_base + off)

    alloc("qfm", [128, 8 * 512], BF16)
    alloc("kgfm", [128, 8 * 512], BF16)
    alloc_at("vtm", [128, 4 * 1024], BF16, 0)
    alloc_at("sigg", [128, 8 * 512], BF16, 8192)
    alloc_at("sigog", [128, 8 * 512], BF16, 16384)
    alloc_at("y1", [128, 8 * 512], BF16, 24576)
    alloc_at("xnfm", [128, 8 * 512], BF16, 32768)
    alloc_at("repb", [128, 2048], BF16, 40960)
    alloc_at("khfm", [128, 8 * 128], BF16, 45056)
    alloc_at("khtm", [128, 8 * 128], BF16, 47104)
    alloc_at("dbuf", [128, 512], F32, 49152)
    alloc_at("pbuf", [128, 512], BF16, 51200)
    alloc_at("ahib", [16, 512], BF16, 52224)
    alloc_at("alob", [16, 512], BF16, 53248)
    alloc_at("nahib", [16, 512], BF16, 54272)
    alloc_at("nalob", [16, 512], BF16, 55296)
    alloc_at("hfm", [128, 32 * 512], BF16, 0)
    alloc_at("xnnt", [128, 8 * 512], BF16, 32768)
    alloc_at("dsub", [128, 4 * 1024], F32, 40960)
    pool2 = nc.alloc_sbuf_tensor("pool2", [128, 40960], mybir.dt.uint8)
    pool2_base = nc.lookup_mloc(pool2).addr

    def alloc_at2(name, shape, dtype, off):
        sb[name] = nc.alloc_sbuf_tensor_at(name, shape, dtype, offset=pool2_base + off)

    alloc_at2("sev0", [128, DM], F32, 0)
    alloc_at2("sev1", [128, DM], F32, 4096)
    alloc_at2("wfp0", [128, DM], F32, 8192)
    alloc_at2("wfp1", [128, DM], F32, 12288)
    alloc_at2("xb0", [128, DM], F32, 16384)
    alloc_at2("xb1", [128, DM], F32, 20480)
    alloc_at2("xb2", [128, DM], F32, 24576)
    alloc("xb3", [128, DM], F32)
    alloc_at2("junk", [128, DM], F32, 28672)
    alloc_at2("terna", [128, DM], BF16, 32768)
    alloc_at2("ternb", [128, DM], BF16, 34816)
    alloc_at2("ternw", [128, DM], BF16, 36864)
    alloc_at2("xntm0", [128, DM], BF16, 38912)
    alloc_at2("tnh", [128, 1024], F32, 0)
    alloc_at2("sgt", [128, 1024], F32, 4096)
    alloc_at2("gt", [128, 1024], F32, 8192)
    alloc_at2("ssb0", [128, DM], F32, 12288)
    alloc_at2("ssb1", [128, DM], F32, 16384)
    alloc_at2("xsb0", [128, DM], F32, 20480)
    alloc_at2("xsb1", [128, DM], F32, 24576)
    alloc_at2("osb0", [128, DM], F32, 28672)
    alloc_at2("osb1", [128, DM], F32, 32768)
    alloc_at2("wdnr0", [128, 512], BF16, 36864)
    alloc_at2("wdnr1", [128, 512], BF16, 37888)
    alloc_at2("wdnr2", [128, 512], BF16, 38912)
    alloc_at2("wdnr3", [128, 512], BF16, 39936)
    for i in range(2):
        alloc(f"tanhg{i}", [128, 512], BF16)
    alloc("zb", [16, 512], F32)
    alloc("spa", [16, 512], F32)
    alloc("spb", [16, 512], F32)
    alloc("gfm", [16, 512], F32)
    alloc("afm", [16, 512], F32)
    alloc("ahi32", [16, 512], F32)
    alloc("stgL", [4, 2048], BF16)
    alloc("stgR", [4, 2048], BF16)
    alloc("tmpw", [16, 512], F32)
    alloc("expaw", [16, 512], F32)
    alloc("emaw", [16, 512], F32)
    alloc("expbf", [16, 1024], BF16)
    alloc("eAdall", [16, 32], F32)
    alloc("eAcol", [128, 32], F32)
    alloc("qtil", [128, 8 * 128], BF16)
    alloc("sqb", [128, 1024], BF16)
    alloc("ssqrow", [1, 512], F32)
    alloc("S32", [128, 8 * 64], F32)
    alloc("Sbf", [128, 8 * 64], BF16)
    alloc("rstdrow", [1, 512], F32)
    alloc("rcol4", [128, 4], F32)
    alloc("acc_up", [128, 1], F32)
    alloc("acc_dn", [128, 1], F32)
    alloc("red1", [128, 1], F32)
    alloc("thrc", [128, 2], F32)
    alloc("nthrc", [128, 2], F32)
    alloc("thr11", [1, 2], F32)
    alloc("tern_t", [128, 512], BF16)
    alloc("zrow", [1, 512], F32)
    alloc("padbuf", [128, 608], BF16)
    alloc("zc4", [128, 4], F32)

    ps = {}
    for nm in ["pbig0", "pbig1", "psc", "pt1", "po", "po2", "psmall"]:
        ps[nm] = nc.alloc_psum_tensor(nm, [128, 512], F32)
    ps["ptr"] = nc.alloc_psum_tensor("ptr", [128, 512], BF16)

    P = p.op

    _f32_consts = [1.0 / DM, EPS_LN, EPS_TR, -1.0, 0.0, 0.5, 1.0, 0.5 / (DFF * DM)]
    _bf_consts = [-1.0, 0.5, 1.0, 0.0]
    alloc("cvf", [128, len(_f32_consts)], F32)
    alloc("cvb", [128, len(_bf_consts)], BF16)
    for i, v in enumerate(_f32_consts):
        P("vector", (lambda i=i, v=v: (lambda e: e.memset(sb["cvf"][:, i:i + 1], v)))(),
          writes=("cvf",))
        nc.const_aps.register(F32, v, sb["cvf"][:, i:i + 1])
    for i, v in enumerate(_bf_consts):
        P("vector", (lambda i=i, v=v: (lambda e: e.memset(sb["cvb"][:, i:i + 1], v)))(),
          writes=("cvb",))
        nc.const_aps.register(BF16, v, sb["cvb"][:, i:i + 1])

    for k in cst:
        P("sync", (lambda k=k: (lambda e: e.dma_start(out=sb[k][:, :], in_=cst_h[k][:, :])))(),
          writes=(k,))
    P("sync", lambda e: e.dma_start(out=sb["bdcol"][:, :], in_=bd_h[:, :]), writes=("bdcol",))
    P("sync", lambda e: e.dma_start(out=sb["nwrows"][:, :], in_=nw_h[:, :]), writes=("nwrows",))
    P("tensor", lambda e: e.transpose(ps["psmall"][:, 0:8], sb["nwrows"][:, :], sb["idf32"][0:8, 0:8]),
      reads=("nwrows", "idf32"), writes=("psmall",))
    P("vector", lambda e: e.tensor_copy(sb["nwfm"][:, :], ps["psmall"][:, 0:8]),
      reads=("psmall",), writes=("nwfm",))
    for nm in ["wq", "wk", "wv", "wo", "wg", "wog", "wd"]:
        P("gpsimd", (lambda nm=nm: (lambda e: e.dma_start(out=dr[nm][:, :], in_=w_h[nm][:, :])))(),
          writes=(f"dr_{nm}",))
    for k in range(8):
        P("sync", (lambda k=k: (lambda e: e.dma_start(
            out=sb["wdT"][:, 16 * k:16 * k + 16],
            in_=dr["wd"][:, 128 * k:128 * k + 128], transpose=True)))(),
          reads=("dr_wd",), writes=("wdT",))
    P("vector", lambda e: e.memset(sb["S32"][:, :], 0.0), writes=("S32",))
    P("vector", lambda e: e.memset(sb["Sbf"][:, :], 0.0), writes=("Sbf",))
    P("vector", lambda e: e.memset(sb["stgL"][:, :], 1.0), writes=("stgL",))
    P("vector", lambda e: e.memset(sb["stgR"][:, :], 1.0), writes=("stgR",))
    P("vector", lambda e: e.memset(sb["acc_up"][:, :], 0.0), writes=("acc_up",))
    P("vector", lambda e: e.memset(sb["acc_dn"][:, :], 0.0), writes=("acc_dn",))
    P("vector", lambda e: e.memset(sb["ssqrow"][:, :], 0.0), writes=("ssqrow",))
    P("vector", lambda e: e.memset(sb["padbuf"][:, :], 0.0), writes=("padbuf",))
    P("vector", lambda e: e.memset(sb["y1"][:, :], 0.0), writes=("y1",))
    p.barrier()

    for c in range(4):
        P("sync", (lambda c=c: (lambda e: e.dma_start(
            out=sb[f"xb{c}"][:, :], in_=x_h[128 * c:128 * c + 128, :])))(),
          writes=(f"xb{c}",))

    def pretranspose_weights():
        for wi, nm in enumerate(["wo", "wg", "wk", "wq", "wog", "wv"]):
            buf = "wbuf0" if wi % 2 == 0 else "wbuf1"
            for k in range(8):
                P("sync", (lambda nm=nm, buf=buf, k=k: (lambda e: e.dma_start(
                    out=sb[buf][:, 1024 * k:1024 * k + 1024],
                    in_=dr[nm][:, 128 * k:128 * k + 128], transpose=True)))(),
                  reads=(f"dr_{nm}",), writes=(buf,))
            for k in range(8):
                P("scalar", (lambda nm=nm, buf=buf, k=k: (lambda e: e.dma_start(
                    out=dr[f"{nm}T"][128 * k:128 * k + 128, :],
                    in_=sb[buf][:, 1024 * k:1024 * k + 1024])))(),
                  reads=(buf,), writes=(f"dr_{nm}T",), kind="dma")

    def stream_weight(nm, buf, direct=False):
        for k in range(8):
            if direct:
                P("sync", (lambda nm=nm, buf=buf, k=k: (lambda e: e.dma_start(
                    out=sb[buf][:, 1024 * k:1024 * k + 1024],
                    in_=dr[nm][:, 128 * k:128 * k + 128], transpose=True)))(),
                  reads=(f"dr_{nm}",), writes=(buf,))
            else:
                P("sync", (lambda nm=nm, buf=buf, k=k: (lambda e: e.dma_start(
                    out=sb[buf][:, 1024 * k:1024 * k + 1024],
                    in_=dr[f"{nm}T"][128 * k:128 * k + 128, :])))(),
                  reads=(f"dr_{nm}T",), writes=(buf,))

    def ln_chunk(t, c):
        ci = 4 * t + c
        xb = f"xb{ci % 4}"
        if t > 1:
            P("sync", (lambda ci=ci, xb=xb: (lambda e: e.dma_start(
                out=sb[xb][:, :], in_=x_h[128 * ci:128 * ci + 128, :])))(),
              writes=(xb,))
        P("vector", (lambda xb=xb, c=c: (lambda e: e.reduce_sum(
            sb["sums"][:, c:c + 1], sb[xb][:, :], axis=AX.X)))(),
          reads=(xb,), writes=("sums",))
        P("scalar", (lambda xb=xb: (lambda e: e.activation(
            sb["junk"][:, :], sb[xb][:, :], AF.Square)))(),
          reads=(xb,), writes=("junk",))
        P("vector", (lambda c=c: (lambda e: e.reduce_sum(
            sb["ssqc"][:, c:c + 1], sb["junk"][:, :], axis=AX.X)))(),
          reads=("junk",), writes=("ssqc",))

    def ln_stats():
        P("vector", lambda e: e.tensor_scalar(sb["mu"][:, :], sb["sums"][:, :],
                                              1.0 / DM, None, ALU.mult),
          reads=("sums",), writes=("mu",), pad=500)
        P("vector", lambda e: e.tensor_tensor(sb["t4a"][:, :], sb["mu"][:, :],
                                              sb["mu"][:, :], ALU.mult),
          reads=("mu",), writes=("t4a",), pad=500)
        P("vector", lambda e: e.scalar_tensor_tensor(
            sb["var"][:, :], sb["ssqc"][:, :], 1.0 / DM, sb["t4a"][:, :],
            ALU.mult, ALU.subtract),
          reads=("ssqc", "t4a"), writes=("var",), pad=500)
        P("scalar", lambda e: e.activation(sb["t4b"][:, :], sb["var"][:, :], AF.Ln,
                                           bias=np.float32(EPS_LN)),
          reads=("var",), writes=("t4b",), pad=500)
        P("scalar", lambda e: e.activation(sb["rstd4"][:, :], sb["t4b"][:, :], AF.Exp,
                                           scale=np.float32(-0.5)),
          reads=("t4b",), writes=("rstd4",), pad=500)
        P("scalar", lambda e: e.activation(sb["t4a"][:, :], sb["var"][:, :], AF.Ln),
          reads=("var",), writes=("t4a",), pad=500)
        P("vector", lambda e: e.tensor_tensor(sb["t4a"][:, :], sb["t4a"][:, :],
                                              sb["t4b"][:, :], ALU.subtract),
          reads=("t4a", "t4b"), writes=("t4a",), pad=500)
        P("scalar", lambda e: e.activation(sb["t4c"][:, :], sb["t4a"][:, :], AF.Exp,
                                           scale=np.float32(0.5)),
          reads=("t4a",), writes=("t4c",), pad=500)
        P("scalar", lambda e: e.activation(sb["t4a"][:, :], sb["t4c"][:, :], AF.Ln,
                                           bias=np.float32(EPS_TR)),
          reads=("t4c",), writes=("t4a",), pad=500)
        P("scalar", lambda e: e.activation(sb["c4"][:, :], sb["t4a"][:, :], AF.Exp,
                                           scale=np.float32(-1.0)),
          reads=("t4a",), writes=("c4",), pad=500)
        P("vector", lambda e: e.tensor_tensor(sb["rc4"][:, :], sb["rstd4"][:, :],
                                              sb["c4"][:, :], ALU.mult),
          reads=("rstd4", "c4"), writes=("rc4",), pad=500)

    def xn_chunk(t, c):
        ci = 4 * t + c
        xb = f"xb{ci % 4}"
        xn, xnn = f"xntm{c % 2}", f"xnntm{c % 2}"
        P("vector", (lambda xb=xb, xn=xn, c=c: (lambda e: e.tensor_scalar(
            sb[xn][:, :], sb[xb][:, :], sb["mu"][:, c:c + 1],
            sb["rstd4"][:, c:c + 1], ALU.subtract, ALU.mult)))(),
          reads=(xb, "mu", "rstd4"), writes=(xn,))
        P("vector", (lambda xb=xb, xnn=xnn, c=c: (lambda e: e.tensor_scalar(
            sb[xnn][:, :], sb[xb][:, :], sb["mu"][:, c:c + 1],
            sb["rc4"][:, c:c + 1], ALU.subtract, ALU.mult)))(),
          reads=(xb, "mu", "rc4"), writes=(xnn,))
        P("scalar", (lambda ci=ci, xnn=xnn: (lambda e: e.dma_start(
            out=dr["xnn"][128 * ci:128 * ci + 128, :], in_=sb[xnn][:, :])))(),
          reads=(xnn,), writes=("dr_xnn",), kind="dma")

    def load_xnfm(c):
        if True:
            xn = f"xntm{c % 2}"
            for k in range(8):
                slot = (8 * c + k) % 4
                P("tensor", (lambda xn=xn, k=k, slot=slot: (lambda e: e.transpose(
                    ps["ptr"][:, 128 * slot:128 * slot + 128],
                    sb[xn][:, 128 * k:128 * k + 128], sb["id128"][:, :])))(),
                  reads=(xn, "id128"), writes=("ptr",))
                P("vector", (lambda c=c, k=k, slot=slot: (lambda e: e.tensor_copy(
                    sb["xnfm"][:, 512 * k + 128 * c:512 * k + 128 * c + 128],
                    ps["ptr"][:, 128 * slot:128 * slot + 128])))(),
                  reads=("ptr",), writes=("xnfm",))

    def proj_fm(buf, evac):
        for g in range(8):
            pb = "pbig0" if g % 2 == 0 else "pbig1"
            for k in range(8):
                P("tensor", (lambda buf=buf, g=g, k=k, pb=pb: (lambda e: e.matmul(
                    ps[pb][:, :],
                    sb[buf][:, 1024 * k + 128 * g:1024 * k + 128 * g + 128],
                    sb["xnfm"][:, 512 * k:512 * k + 512],
                    start=(k == 0), stop=(k == 7))))(),
                  reads=(buf, "xnfm"), writes=(pb,))
            evac(g, pb)

    def proj_tm(buf, evac):
        for s4 in range(4):
            for fh in range(2):
                pb = "pbig0" if (2 * s4 + fh) % 2 == 0 else "pbig1"
                for k in range(8):
                    P("tensor", (lambda buf=buf, s4=s4, fh=fh, k=k, pb=pb: (lambda e: e.matmul(
                        ps[pb][:, :],
                        sb["xnfm"][:, 512 * k + 128 * s4:512 * k + 128 * s4 + 128],
                        sb[buf][:, 1024 * k + 512 * fh:1024 * k + 512 * fh + 512],
                        start=(k == 0), stop=(k == 7))))(),
                      reads=(buf, "xnfm"), writes=(pb,))
                evac(s4, fh, pb)

    def dproj_softplus(t):
        for k in range(8):
            P("tensor", (lambda k=k: (lambda e: e.matmul(
                ps["pbig0"][0:16, :], sb["wdT"][:, 16 * k:16 * k + 16],
                sb["xnfm"][:, 512 * k:512 * k + 512],
                start=(k == 0), stop=(k == 7))))(),
              reads=("wdT", "xnfm"), writes=("pbig0",))
        P("vector", lambda e: e.tensor_scalar(sb["zb"][:, :], ps["pbig0"][0:16, :],
                                              sb["bdcol"][:, :], None, ALU.add),
          reads=("pbig0", "bdcol"), writes=("zb",))
        P("scalar", lambda e: e.activation(sb["spa"][:, :], sb["zb"][:, :], AF.Abs),
          reads=("zb",), writes=("spa",))
        P("scalar", lambda e: e.activation(sb["spa"][:, :], sb["spa"][:, :], AF.Exp,
                                           scale=np.float32(-1.0)),
          reads=("spa",), writes=("spa",))
        P("scalar", lambda e: e.activation(sb["spa"][:, :], sb["spa"][:, :], AF.Ln,
                                           bias=np.float32(1.0)),
          reads=("spa",), writes=("spa",))
        P("vector", lambda e: e.tensor_scalar(sb["spb"][:, :], sb["zb"][:, :],
                                              0.0, None, ALU.max),
          reads=("zb",), writes=("spb",))
        P("vector", lambda e: e.scalar_tensor_tensor(
            sb["gfm"][:, :], sb["spb"][:, :], -1.0, sb["spa"][:, :],
            ALU.mult, ALU.subtract),
          reads=("spa", "spb"), writes=("gfm",))
        P("vector", lambda e: e.tensor_tensor_scan(
            sb["afm"][:, :], sb["cb"][:, :], sb["gfm"][:, :], 0.0,
            ALU.mult, ALU.add),
          reads=("cb", "gfm"), writes=("afm",))
        P("vector", lambda e: e.tensor_copy(sb["ahib"][:, :], sb["afm"][:, :]),
          reads=("afm",), writes=("ahib",))
        P("vector", lambda e: e.tensor_copy(sb["ahi32"][:, :], sb["ahib"][:, :]),
          reads=("ahib",), writes=("ahi32",))
        P("vector", lambda e: e.tensor_tensor(sb["alob"][:, :], sb["afm"][:, :],
                                              sb["ahi32"][:, :], ALU.subtract),
          reads=("afm", "ahi32"), writes=("alob",))
        P("vector", lambda e: e.tensor_scalar(sb["nahib"][:, :], sb["ahib"][:, :],
                                              -1.0, None, ALU.mult),
          reads=("ahib",), writes=("nahib",))
        P("vector", lambda e: e.tensor_scalar(sb["nalob"][:, :], sb["alob"][:, :],
                                              -1.0, None, ALU.mult),
          reads=("alob",), writes=("nalob",))
        for bi, nm in enumerate(["ahib", "alob", "nahib", "nalob"]):
            P("gpsimd", (lambda t=t, bi=bi, nm=nm: (lambda e: e.dma_start(
                out=dr["ast"][64 * t + 16 * bi:64 * t + 16 * bi + 16, :],
                in_=sb[nm][:, :])))(),
              reads=(nm,), writes=("dr_ast",))
        for c4i in range(4):
            P("vector", (lambda c4i=c4i: (lambda e: e.tensor_scalar(
                sb["tmpw"][:, 128 * c4i:128 * c4i + 128],
                sb["afm"][:, 128 * c4i:128 * c4i + 128], -1.0,
                sb["afm"][:, 128 * c4i + 127:128 * c4i + 128],
                ALU.mult, ALU.add)))(),
              reads=("afm",), writes=("tmpw",), pad=(500 if c4i == 3 else 0))
        P("scalar", lambda e: e.activation(sb["expaw"][:, :], sb["afm"][:, :], AF.Exp),
          reads=("afm",), writes=("expaw",))
        P("scalar", lambda e: e.activation(sb["emaw"][:, :], sb["tmpw"][:, :], AF.Exp),
          reads=("tmpw",), writes=("emaw",))
        ebv = sb["expbf"][:, :].rearrange("p (c t w) -> p c t w", c=4, t=2, w=128)
        P("vector", (lambda ebv=ebv: (lambda e: e.tensor_copy(
            ebv[:, :, 0, :], sb["expaw"][:, :].rearrange("p (c w) -> p c w", c=4))))(),
          reads=("expaw",), writes=("expbf",))
        P("vector", (lambda ebv=ebv: (lambda e: e.tensor_copy(
            ebv[:, :, 1, :], sb["emaw"][:, :].rearrange("p (c w) -> p c w", c=4))))(),
          reads=("emaw",), writes=("expbf",))
        for c4i in range(4):
            P("vector", (lambda c4i=c4i: (lambda e: e.tensor_scalar(
                sb["eAdall"][:, 8 * c4i:8 * c4i + 8], sb["dg"][:, :],
                sb["expaw"][:, 128 * c4i + 127:128 * c4i + 128], None, ALU.mult)))(),
              reads=("dg", "expaw"), writes=("eAdall",), pad=(500 if c4i == 3 else 0))
        for c4i in range(4):
            P("tensor", (lambda c4i=c4i: (lambda e: e.matmul(
                ps["psmall"][:, 8 * c4i:8 * c4i + 8], sb["p2"][:, :],
                sb["eAdall"][:, 8 * c4i:8 * c4i + 8], start=True, stop=True)))(),
              reads=("p2", "eAdall"), writes=("psmall",))
        P("vector", lambda e: e.tensor_copy(sb["eAcol"][:, :], ps["psmall"][:, 0:32]),
          reads=("psmall",), writes=("eAcol",))

    def scan_chunk(t, c, mask=240767):
        if not (mask & 1):
            return
        P("sync", (lambda t=t, c=c: (lambda e: e.dma_start(
            out=sb["stgL"][2:4, :].rearrange("s (h w) -> s h w", h=16),
            in_=dr["ast"][64 * t + 32:64 * t + 64, 128 * c:128 * c + 128]
                .rearrange("(s h) w -> s h w", s=2))))(),
          reads=("dr_ast",), writes=("stgL",))
        P("sync", (lambda t=t, c=c: (lambda e: e.dma_start(
            out=sb["stgR"][0:2, :].rearrange("s (h w) -> s h w", h=16),
            in_=dr["ast"][64 * t:64 * t + 32, 128 * c:128 * c + 128]
                .rearrange("(s h) w -> s h w", s=2))))(),
          reads=("dr_ast",), writes=("stgR",))
        for rnd in range(2 if (mask & 1024) else 0):
            for gg in range(4):
                g = 4 * rnd + gg
                pb = "pbig0" if gg < 2 else "pbig1"
                off = 256 * (gg % 2)
                P("tensor", (lambda g=g, pb=pb, off=off, c=c: (lambda e: e.matmul(
                    ps[pb][:, off:off + 256],
                    sb["eg"][:, 128 * g:128 * g + 128],
                    sb["expbf"][:, 256 * c:256 * c + 256],
                    start=True, stop=True)))(),
                  reads=("eg", "expbf"), writes=(pb,))
                P("scalar", (lambda g=g, pb=pb, off=off: (lambda e: e.activation(
                    sb["repb"][:, 256 * g:256 * g + 256], ps[pb][:, off:off + 256],
                    AF.Copy)))(),
                  reads=(pb,), writes=("repb",))
        for g in range(8 if (mask & 2048) else 0):
            P("vector", (lambda g=g, c=c: (lambda e: e.tensor_tensor(
                sb["qtil"][:, 128 * g:128 * g + 128],
                sb["qfm"][:, 512 * g + 128 * c:512 * g + 128 * c + 128],
                sb["repb"][:, 256 * g:256 * g + 128], ALU.mult)))(),
              reads=("qfm", "repb"), writes=("qtil",))
            P("vector", (lambda g=g, c=c: (lambda e: e.tensor_tensor(
                sb["khfm"][:, 128 * g:128 * g + 128],
                sb["kgfm"][:, 512 * g + 128 * c:512 * g + 128 * c + 128],
                sb["repb"][:, 256 * g + 128:256 * g + 256], ALU.mult)))(),
              reads=("kgfm", "repb"), writes=("khfm",), pad=(500 if g == 7 else 0))
        for g in range(8 if (mask & 2048) else 0):
            slot = g % 4
            P("tensor", (lambda g=g, slot=slot: (lambda e: e.transpose(
                ps["ptr"][:, 128 * slot:128 * slot + 128],
                sb["khfm"][:, 128 * g:128 * g + 128], sb["id128"][:, :])))(),
              reads=("khfm", "id128"), writes=("ptr",))
            P("vector", (lambda g=g, slot=slot: (lambda e: e.tensor_copy(
                sb["khtm"][:, 128 * g:128 * g + 128],
                ps["ptr"][:, 128 * slot:128 * slot + 128])))(),
              reads=("ptr",), writes=("khtm",))
        if not (mask & 4):
            return
        P("vector", lambda e: e.tensor_copy(sb["Sbf"][:, :], sb["S32"][:, :]),
          reads=("S32",), writes=("Sbf",))
        WAVES = [[0, 2, 4, 6], [1, 3, 5, 7], [8, 10, 12, 14], [9, 11, 13, 15]]
        for w, heads in enumerate(WAVES):
            for j, h in enumerate(heads):
                g, b = h // 2, h % 2
                if mask & 256:
                    b = 0
                if mask & 512:
                    g = 0
                P("tensor", (lambda g=g, b=b, j=j, c=c: (lambda e: e.matmul(
                    ps["psc"][:, 128 * j:128 * j + 128],
                    sb["kgfm"][64 * b:64 * b + 64, 512 * g + 128 * c:512 * g + 128 * c + 128],
                    sb["qfm"][64 * b:64 * b + 64, 512 * g + 128 * c:512 * g + 128 * c + 128],
                    start=True, stop=True)))(),
                  reads=("kgfm", "qfm"), writes=("psc",))
            if not (mask & 16):
                continue
            P("tensor", lambda e: e.matmul(ps["pt1"][:, :], sb["id128"][:, :],
                                           sb["negbig"][:, :], start=True, stop=False),
              reads=("id128", "negbig"), writes=("pt1",))
            for j, h in enumerate(heads):
                P("tensor", (lambda h=h, j=j, last=(j == 3): (lambda e: e.matmul(
                    ps["pt1"][:, 128 * j:128 * j + 128],
                    sb["stgL"][:, 128 * h:128 * h + 128],
                    sb["stgR"][:, 128 * h:128 * h + 128],
                    start=False, stop=last)))(),
                  reads=("stgL", "stgR"), writes=("pt1",))
            if not (mask & 32):
                continue
            P("scalar", lambda e: e.activation(sb["dbuf"][:, :], ps["pt1"][:, :], AF.Exp),
              reads=("pt1",), writes=("dbuf",))
            if not (mask & 64):
                continue
            P("vector", lambda e: e.tensor_tensor(sb["pbuf"][:, :], sb["dbuf"][:, :],
                                                  ps["psc"][:, :], ALU.mult),
              reads=("dbuf", "psc"), writes=("pbuf",))
            if not (mask & 8):
                continue
            def _tile(g):
                return ("po" if g % 2 == 0 else "po2",
                        128 * ((g // 2) % 2) + 256 * (g // 4))
            for pi in range(2):
                for j in (2 * pi, 2 * pi + 1):
                    h = heads[j]
                    g, b = h // 2, h % 2
                    pb, col = _tile(g)
                    P("tensor", (lambda h=h, j=j, b=b, c=c, pb=pb, col=col: (lambda e: e.matmul(
                        ps[pb][64 * b:64 * b + 64, col:col + 128],
                        sb["vtm"][:, 1024 * c + 64 * h:1024 * c + 64 * h + 64],
                        sb["pbuf"][:, 128 * j:128 * j + 128],
                        start=True, stop=False, tile_position=(0, 64 * b))))(),
                      reads=("vtm", "pbuf"), writes=(pb,))
                for j in (2 * pi, 2 * pi + 1):
                    h = heads[j]
                    g, b = h // 2, h % 2
                    pb, col = _tile(g)
                    P("tensor", (lambda g=g, b=b, pb=pb, col=col: (lambda e: e.matmul(
                        ps[pb][64 * b:64 * b + 64, col:col + 128],
                        sb["Sbf"][64 * b:64 * b + 64, 64 * g:64 * g + 64],
                        sb["qtil"][64 * b:64 * b + 64, 128 * g:128 * g + 128],
                        start=False, stop=True,
                        tile_position=(64 * b, 64 * b))))(),
                      reads=("Sbf", "qtil"), writes=(pb,))
            for j, h in enumerate(heads):
                if not (mask & 32768):
                    break
                g, b = h // 2, h % 2
                P("tensor", (lambda g=g, b=b, h=h, c=c: (lambda e: e.matmul(
                    ps["psmall"][64 * b:64 * b + 64, 128 + 64 * (g % 4):128 + 64 * (g % 4) + 64],
                    sb["khtm"][:, 128 * g + 64 * b:128 * g + 64 * b + 64],
                    sb["vtm"][:, 1024 * c + 64 * h:1024 * c + 64 * h + 64],
                    start=True, stop=True, tile_position=(0, 64 * b))))(),
                  reads=("khtm", "vtm"), writes=("psmall",))
            if w % 2 == 0:
                continue
            if not (mask & 8):
                continue
            base_g = 0 if w == 1 else 4
            for gi in range(4):
                if not (mask & 65536):
                    break
                g = base_g + gi
                pb = "po" if g % 2 == 0 else "po2"
                col = 128 * ((g // 2) % 2) + 256 * (g // 4)
                P("vector", (lambda g=g, gi=gi: (lambda e: e.scalar_tensor_tensor(
                    sb["S32"][:, 64 * g:64 * g + 64], sb["S32"][:, 64 * g:64 * g + 64],
                    sb["eAcol"][:, 8 * c + g:8 * c + g + 1],
                    ps["psmall"][:, 128 + 64 * gi:128 + 64 * gi + 64],
                    ALU.mult, ALU.add)))(),
                  reads=("S32", "eAcol", "psmall"), writes=("S32",))
                P("scalar", (lambda g=g, pb=pb, col=col: (lambda e: e.activation(
                    sb["sqb"][:, 128 * g:128 * g + 128],
                    ps[pb][:, col:col + 128], AF.Square)))(),
                  reads=(pb,), writes=("sqb", "po_rdlock"))
                P("vector", (lambda g=g, c=c, pb=pb, col=col: (lambda e: e.scalar_tensor_tensor(
                    sb["y1"][:, 512 * g + 128 * c:512 * g + 128 * c + 128],
                    ps[pb][:, col:col + 128],
                    sb["nwfm"][:, g:g + 1],
                    sb["sigog"][:, 512 * g + 128 * c:512 * g + 128 * c + 128],
                    ALU.mult, ALU.mult)))(),
                  reads=(pb, "nwfm", "sigog", "po_rdlock"), writes=("y1",))
        if not (mask & 131072):
            return
        for g in range(8):
            P("tensor", (lambda g=g: (lambda e: e.matmul(
                ps["psc"][0:1, 0:128], sb["onesbf"][:, :],
                sb["sqb"][:, 128 * g:128 * g + 128],
                start=(g == 0), stop=(g == 7))))(),
              reads=("onesbf", "sqb"), writes=("psc",))
        P("vector", (lambda c=c: (lambda e: e.tensor_copy(
            sb["ssqrow"][:, 128 * c:128 * c + 128], ps["psc"][0:1, 0:128])))(),
          reads=("psc",), writes=("ssqrow",))

    def attn_rstd_window():
        P("scalar", lambda e: e.activation(sb["rstdrow"][:, :], sb["ssqrow"][:, :],
                                           AF.Ln, scale=np.float32(1.0 / DM), bias=np.float32(EPS_RMS)),
          reads=("ssqrow",), writes=("rstdrow",))
        P("scalar", lambda e: e.activation(sb["rstdrow"][:, :], sb["rstdrow"][:, :],
                                           AF.Exp, scale=np.float32(-0.5)),
          reads=("rstdrow",), writes=("rstdrow",))
        for c in range(4):
            P("tensor", (lambda c=c: (lambda e: e.matmul(
                ps["psmall"][:, 32 + c:32 + c + 1],
                sb["rstdrow"][:, 128 * c:128 * c + 128], sb["onesf11"][:, :],
                start=True, stop=True)))(),
              reads=("rstdrow", "onesf11"), writes=("psmall",))
        P("vector", lambda e: e.tensor_copy(sb["rcol4"][:, :], ps["psmall"][:, 32:36]),
          reads=("psmall",), writes=("rcol4",))

    def wo_proj(tprev):
        stream_weight("wo", "wbuf1")
        for s4 in range(4):
            for fh in range(2):
                pb = "pbig0" if (2 * s4 + fh) % 2 == 0 else "pbig1"
                for k in range(8):
                    P("tensor", (lambda s4=s4, fh=fh, k=k, pb=pb: (lambda e: e.matmul(
                        ps[pb][:, :],
                        sb["y1"][:, 512 * k + 128 * s4:512 * k + 128 * s4 + 128],
                        sb["wbuf1"][:, 1024 * k + 512 * fh:1024 * k + 512 * fh + 512],
                        start=(k == 0), stop=(k == 7))))(),
                      reads=("y1", "wbuf1"), writes=(pb,))
                sv = f"sev{s4 % 2}"
                P("vector", (lambda s4=s4, fh=fh, pb=pb, sv=sv: (lambda e: e.tensor_scalar(
                    sb[sv][:, 512 * fh:512 * fh + 512], ps[pb][:, :],
                    sb["rcol4"][:, s4:s4 + 1], None, ALU.mult)))(),
                  reads=(pb, "rcol4"), writes=(sv,))
            ci = 4 * tprev + s4
            P("scalar", (lambda ci=ci, sv=f"sev{s4 % 2}": (lambda e: e.dma_start(
                out=dr["s"][128 * ci:128 * ci + 128, :], in_=sb[sv][:, :])))(),
              reads=(f"sev{s4 % 2}",), writes=("dr_s",), kind="dma")

    def alpha_tile(which, idx):
        wf = f"wfp{idx % 2}"
        if which == "up":
            src = w_h["wup"][128 * idx:128 * idx + 128, :]
            acc = "acc_up"
        else:
            r, q = idx // 4, idx % 4
            src = w_h["wdown"][128 * r:128 * r + 128, 1024 * q:1024 * q + 1024]
            acc = "acc_dn"
        P("sync", (lambda wf=wf, src=src: (lambda e: e.dma_start(out=sb[wf][:, :], in_=src)))(),
          writes=(wf,))
        P("vector", (lambda wf=wf: (lambda e: e.tensor_reduce(
            sb["red1"][:, :], sb[wf][:, :], axis=AX.X, op=ALU.add,
            apply_absolute_value=True)))(),
          reads=(wf,), writes=("red1",))
        P("vector", (lambda acc=acc: (lambda e: e.tensor_tensor(
            sb[acc][:, :], sb[acc][:, :], sb["red1"][:, :], ALU.add)))(),
          reads=("red1", acc), writes=(acc,))

    def thr_compute():
        for i, acc in enumerate(["acc_up", "acc_dn"]):
            P("tensor", (lambda i=i, acc=acc: (lambda e: e.matmul(
                ps["psmall"][0:1, 40 + i:41 + i], sb[acc][:, :], sb["onesf128"][:, :],
                start=True, stop=True)))(),
              reads=(acc, "onesf128"), writes=("psmall",))
        P("vector", lambda e: e.tensor_scalar(sb["thr11"][:, :], ps["psmall"][0:1, 40:42],
                                              0.5 / (DFF * DM), None, ALU.mult),
          reads=("psmall",), writes=("thr11",), pad=500)
        P("tensor", lambda e: e.matmul(ps["psmall"][:, 42:44], sb["onesf1x"][:, :],
                                       sb["thr11"][:, :], start=True, stop=True),
          reads=("onesf1x", "thr11"), writes=("psmall",))
        P("vector", lambda e: e.tensor_copy(sb["thrc"][:, :], ps["psmall"][:, 42:44]),
          reads=("psmall",), writes=("thrc",))
        P("vector", lambda e: e.tensor_scalar(sb["nthrc"][:, :], sb["thrc"][:, :],
                                              -1.0, None, ALU.mult),
          reads=("thrc",), writes=("nthrc",))

    def tern_tile(which, idx):
        wf = f"wfp{idx % 2}"
        col = 0 if which == "up" else 1
        if which == "up":
            src = w_h["wup"][128 * idx:128 * idx + 128, :]
        else:
            r, q = idx // 4, idx % 4
            src = w_h["wdown"][128 * r:128 * r + 128, 1024 * q:1024 * q + 1024]
        P("sync", (lambda wf=wf, src=src: (lambda e: e.dma_start(out=sb[wf][:, :], in_=src)))(),
          writes=(wf,))
        P("vector", (lambda wf=wf, col=col: (lambda e: e.tensor_scalar(
            sb["terna"][:, :], sb[wf][:, :], sb["thrc"][:, col:col + 1], None,
            ALU.is_gt)))(),
          reads=(wf, "thrc"), writes=("terna",))
        P("vector", (lambda wf=wf, col=col: (lambda e: e.tensor_scalar(
            sb["ternb"][:, :], sb[wf][:, :], sb["nthrc"][:, col:col + 1], None,
            ALU.is_lt)))(),
          reads=(wf, "nthrc"), writes=("ternb",))
        P("vector", lambda e: e.tensor_tensor(sb["ternw"][:, :], sb["terna"][:, :],
                                              sb["ternb"][:, :], ALU.subtract),
          reads=("terna", "ternb"), writes=("ternw",))
        for i in range(8):
            slot = i % 4
            P("tensor", (lambda i=i, slot=slot: (lambda e: e.transpose(
                ps["ptr"][:, 128 * slot:128 * slot + 128],
                sb["ternw"][:, 128 * i:128 * i + 128], sb["id128"][:, :])))(),
              reads=("ternw", "id128"), writes=("ptr",))
            P("vector", (lambda slot=slot: (lambda e: e.tensor_copy(
                sb["tern_t"][:, 128 * slot:128 * slot + 128],
                ps["ptr"][:, 128 * slot:128 * slot + 128])))(),
              reads=("ptr",), writes=(f"tt{slot}",))
            if i % 4 != 3:
                continue
            h = i // 4
            if which == "up":
                dst = dr["wutT"][512 * h:512 * h + 512,
                                 128 * idx:128 * idx + 128]
            else:
                r, q = idx // 4, idx % 4
                dst = dr["wdtT"][1024 * q + 512 * h:1024 * q + 512 * h + 512,
                                 128 * r:128 * r + 128]
            P("gpsimd", (lambda dst=dst: (lambda e: e.dma_start(
                out=dst.rearrange("(i p) c -> p i c", i=4),
                in_=sb["tern_t"][:, :].rearrange("p (i c) -> p i c", i=4))))(),
              reads=("tt0", "tt1", "tt2", "tt3"), writes=("dr_wt",))

    for t in range(NT):
        if t > 0:
            attn_rstd_window()
            wo_proj(t - 1)
        for c in range(4):
            ln_chunk(t, c)
        ln_stats()
        for c in range(4):
            xn_chunk(t, c)
            load_xnfm(c)
        dproj_softplus(t)
        stream_weight("wg", "wbuf0", direct=(t == 0))

        def evac_g(g, pb):
            tg = f"tanhg{g % 2}"
            P("scalar", (lambda pb=pb, tg=tg: (lambda e: e.activation(
                sb[tg][:, :], ps[pb][:, :], AF.Tanh, scale=np.float32(0.5))))(),
              reads=(pb,), writes=(tg,))
            P("vector", (lambda g=g, tg=tg: (lambda e: e.tensor_scalar(
                sb["sigg"][:, 512 * g:512 * g + 512], sb[tg][:, :],
                0.5, 0.5, ALU.mult, ALU.add)))(),
              reads=(tg,), writes=("sigg",))
        proj_fm("wbuf0", evac_g)
        stream_weight("wk", "wbuf1", direct=(t == 0))

        def evac_k(g, pb):
            P("vector", (lambda g=g, pb=pb: (lambda e: e.tensor_tensor(
                sb["kgfm"][:, 512 * g:512 * g + 512], ps[pb][:, :],
                sb["sigg"][:, 512 * g:512 * g + 512], ALU.mult)))(),
              reads=(pb, "sigg"), writes=("kgfm",))
        proj_fm("wbuf1", evac_k)
        stream_weight("wq", "wbuf0", direct=(t == 0))

        def evac_q(g, pb):
            P("vector", (lambda g=g, pb=pb: (lambda e: e.tensor_copy(
                sb["qfm"][:, 512 * g:512 * g + 512], ps[pb][:, :])))(),
              reads=(pb,), writes=("qfm",))
        proj_fm("wbuf0", evac_q)
        stream_weight("wog", "wbuf1", direct=(t == 0))

        def evac_og(g, pb):
            tg = f"tanhg{g % 2}"
            P("scalar", (lambda pb=pb, tg=tg: (lambda e: e.activation(
                sb[tg][:, :], ps[pb][:, :], AF.Tanh, scale=np.float32(0.5))))(),
              reads=(pb,), writes=(tg,))
            P("vector", (lambda g=g, tg=tg: (lambda e: e.tensor_scalar(
                sb["sigog"][:, 512 * g:512 * g + 512], sb[tg][:, :],
                0.5, 0.5, ALU.mult, ALU.add)))(),
              reads=(tg,), writes=("sigog",))
        proj_fm("wbuf1", evac_og)
        stream_weight("wv", "wbuf0", direct=(t == 0))

        def evac_v(s4, fh, pb):
            P("vector", (lambda s4=s4, fh=fh, pb=pb: (lambda e: e.tensor_copy(
                sb["vtm"][:, 1024 * s4 + 512 * fh:1024 * s4 + 512 * fh + 512],
                ps[pb][:, :])))(),
              reads=(pb,), writes=("vtm",))
        proj_tm("wbuf0", evac_v)
        for c in range(scan_chunks):
            scan_chunk(t, c, scan_mask)
        if t == 0:
            for c in range(4):
                P("sync", (lambda c=c: (lambda e: e.dma_start(
                    out=sb[f"xb{c}"][:, :],
                    in_=x_h[128 * (4 + c):128 * (4 + c) + 128, :])))(),
                  writes=(f"xb{c}",))
            pretranspose_weights()
        if t < 2:
            for i in range(8):
                alpha_tile("up", 16 * t + 2 * i), alpha_tile("up", 16 * t + 2 * i + 1)
        elif t == 2:
            for i in range(16):
                alpha_tile("dn", i)
        elif t == 3:
            for i in range(16):
                alpha_tile("dn", 16 + i)
            thr_compute()
        else:
            for i in range(8):
                tern_tile("up", 8 * (t - 4) + i)
            for i in range(8):
                tern_tile("dn", 8 * (t - 4) + i)
    attn_rstd_window()
    wo_proj(NT - 1)

    if phase2:
        p.barrier()

        def load_xnnt(t2):
            for k in range(8):
                P("sync", (lambda t2=t2, k=k: (lambda e: e.dma_start(
                    out=sb["xnnt"][:, 512 * k:512 * k + 512],
                    in_=dr["xnn"][T2 * t2:T2 * t2 + T2, 128 * k:128 * k + 128],
                    transpose=True)))(),
                  reads=("dr_xnn",), writes=("xnnt",))

        load_xnnt(0)
        for t2 in range(NT2):
            for jw in range(8):
                wb = "wbuf0" if jw % 2 == 0 else "wbuf1"
                for k in range(8):
                    P("sync", (lambda jw=jw, k=k, wb=wb: (lambda e: e.dma_start(
                        out=sb[wb][:, 512 * k:512 * k + 512],
                        in_=dr["wutT"][128 * k:128 * k + 128,
                                       512 * jw:512 * jw + 512])))(),
                      reads=("dr_wt",), writes=(wb,))
                for jj in range(4):
                    j = 4 * jw + jj
                    pb = "pbig0" if jj % 2 == 0 else "pbig1"
                    for k in range(8):
                        P("tensor", (lambda jj=jj, k=k, pb=pb, wb=wb: (lambda e: e.matmul(
                            ps[pb][:, :],
                            sb[wb][:, 512 * k + 128 * jj:512 * k + 128 * jj + 128],
                            sb["xnnt"][:, 512 * k:512 * k + 512],
                            start=(k == 0), stop=(k == 7))))(),
                          reads=(wb, "xnnt"), writes=(pb,))
                    if j > 0:
                        P("tensor", (lambda jp=j - 1: (lambda e: e.matmul(
                            ps["psc"][0:1, :], sb["onesbf"][:, :],
                            sb["sqb"][:, 512 * (jp % 2):512 * (jp % 2) + 512],
                            start=(jp == 0), stop=False)))(),
                          reads=("onesbf", f"sqb2{(j - 1) % 2}"), writes=("psc",))
                    P("scalar", (lambda j=j, pb=pb: (lambda e: e.activation(
                        sb["hfm"][:, 512 * j:512 * j + 512], ps[pb][:, :], AF.Gelu)))(),
                      reads=(pb,), writes=(f"hfm{j}",))
                    P("scalar", (lambda j=j: (lambda e: e.activation(
                        sb["sqb"][:, 512 * (j % 2):512 * (j % 2) + 512],
                        sb["hfm"][:, 512 * j:512 * j + 512], AF.Square)))(),
                      reads=(f"hfm{j}",), writes=(f"sqb2{j % 2}",))
            P("tensor", lambda e: e.matmul(
                ps["psc"][0:1, :], sb["onesbf"][:, :], sb["sqb"][:, 512:1024],
                start=False, stop=True),
              reads=("onesbf", "sqb21"), writes=("psc",))
            pbanks = ["pt1", "po", "po2", "psmall"]
            for pss in range(2):
                for grp in range(8):
                    for kk in range(4):
                        k = 4 * grp + kk
                        P("scalar", (lambda pss=pss, k=k, wr=f"wdnr{kk}": (lambda e: e.dma_start(
                            out=sb[wr][:, :],
                            in_=dr["wdtT"][128 * k:128 * k + 128,
                                           512 * pss:512 * pss + 512])))(),
                          reads=("dr_wt",), writes=(f"wdnr{kk}",), kind="dma")
                    for kk in range(4):
                        k = 4 * grp + kk
                        wr = f"wdnr{kk}"
                        for s4 in range(4):
                            P("tensor", (lambda k=k, s4=s4, wr=wr, pb=pbanks[s4]: (lambda e: e.matmul(
                                ps[pb][:, :],
                                sb["hfm"][:, 512 * k + 128 * s4:512 * k + 128 * s4 + 128],
                                sb[wr][:, :],
                                start=(k == 0), stop=(k == 31))))(),
                              reads=(f"hfm{k}", wr), writes=(pbanks[s4],))
                if pss == 0:
                    if t2 + 1 < NT2:
                        load_xnnt(t2 + 1)
                    P("scalar", lambda e: e.activation(sb["zrow"][:, :], ps["psc"][0:1, :],
                                                       AF.Ln, scale=np.float32(1.0 / DFF)),
                      reads=("psc",), writes=("zrow",))
                    P("scalar", lambda e: e.activation(sb["zrow"][:, :], sb["zrow"][:, :],
                                                       AF.Exp, scale=np.float32(0.5)),
                      reads=("zrow",), writes=("zrow",))
                    P("scalar", lambda e: e.activation(sb["zrow"][:, :], sb["zrow"][:, :],
                                                       AF.Ln, bias=np.float32(EPS_TR)),
                      reads=("zrow",), writes=("zrow",))
                    P("scalar", lambda e: e.activation(sb["zrow"][:, :], sb["zrow"][:, :],
                                                       AF.Exp, scale=np.float32(-1.0)),
                      reads=("zrow",), writes=("zrow",))
                    for c in range(4):
                        P("tensor", (lambda c=c: (lambda e: e.matmul(
                            ps["psc"][:, c:c + 1],
                            sb["zrow"][:, 128 * c:128 * c + 128], sb["onesf11"][:, :],
                            start=True, stop=True)))(),
                          reads=("zrow", "onesf11"), writes=("psc",))
                    P("vector", lambda e: e.tensor_copy(sb["zc4"][:, :], ps["psc"][:, 0:4]),
                      reads=("psc",), writes=("zc4",), pad=500)
                for s4 in range(4):
                    P("vector", (lambda s4=s4, pss=pss, pb=pbanks[s4]: (lambda e: e.tensor_scalar(
                        sb["dsub"][:, 1024 * s4 + 512 * pss:1024 * s4 + 512 * pss + 512],
                        ps[pb][:, :], sb["zc4"][:, s4:s4 + 1], None, ALU.mult)))(),
                      reads=(pbanks[s4], "zc4"), writes=("dsub",))
            for s4 in range(4):
                ci = 4 * t2 + s4
                sv, xv, ov = f"ssb{s4 % 2}", f"xsb{s4 % 2}", f"osb{s4 % 2}"
                P("scalar", (lambda ci=ci, sv=sv: (lambda e: e.dma_start(
                    out=sb[sv][:, :], in_=dr["s"][128 * ci:128 * ci + 128, :])))(),
                  reads=("dr_s",), writes=(sv,), kind="dma")
                P("scalar", (lambda ci=ci, xv=xv: (lambda e: e.dma_start(
                    out=sb[xv][:, :], in_=x_h[128 * ci:128 * ci + 128, :])))(),
                  writes=(xv,), kind="dma")
                P("scalar", (lambda s4=s4: (lambda e: e.activation(
                    sb["tnh"][:, :], sb["dsub"][:, 1024 * s4:1024 * s4 + 1024],
                    AF.Tanh, scale=np.float32(0.5))))(),
                  reads=("dsub",), writes=("tnh",))
                P("vector", lambda e: e.tensor_scalar(sb["sgt"][:, :], sb["tnh"][:, :],
                                                      0.5, 0.5, ALU.mult, ALU.add),
                  reads=("tnh",), writes=("sgt",))
                P("vector", (lambda s4=s4: (lambda e: e.tensor_tensor(
                    sb["gt"][:, :], sb["sgt"][:, :],
                    sb["dsub"][:, 1024 * s4:1024 * s4 + 1024], ALU.mult)))(),
                  reads=("sgt", "dsub"), writes=("gt",))
                P("vector", (lambda sv=sv: (lambda e: e.tensor_tensor(
                    sb["sgt"][:, :], sb["gt"][:, :], sb[sv][:, :], ALU.mult)))(),
                  reads=("gt", sv), writes=("sgt",))
                P("vector", (lambda xv=xv, ov=ov: (lambda e: e.tensor_tensor(
                    sb[ov][:, :], sb["sgt"][:, :], sb[xv][:, :], ALU.add)))(),
                  reads=("sgt", xv), writes=(ov,))
                P("scalar", (lambda ci=ci, ov=ov: (lambda e: e.dma_start(
                    out=out_h[128 * ci:128 * ci + 128, :], in_=sb[ov][:, :])))(),
                  reads=(ov,), writes=("dr_out",), kind="dma")
    else:
        p.barrier()
        for ci in range(4 * NT):
            sv = f"ssb{ci % 2}"
            P("sync", (lambda ci=ci, sv=sv: (lambda e: e.dma_start(
                out=sb[sv][:, :], in_=dr["s"][128 * ci:128 * ci + 128, :])))(),
              reads=("dr_s",), writes=(sv,))
            P("sync", (lambda ci=ci, sv=sv: (lambda e: e.dma_start(
                out=out_h[128 * ci:128 * ci + 128, :], in_=sb[sv][:, :])))(),
              reads=(sv,), writes=("dr_out",))

    for nm in taps:
        th = nc.dram_tensor(f"tap_{nm}", list(sb[nm].shape), sb[nm].dtype,
                            kind="ExternalOutput").ap()
        P("sync", (lambda nm=nm, th=th: (lambda e: e.dma_start(out=th[:, :], in_=sb[nm][:, :])))(),
          reads=(nm,), writes=(f"tap_{nm}",))

    p.barrier()
    p.emit(nc, sb["padbuf"][:, :])
    nc._dbg_sb = sb
    return nc


def kernel(**inputs):
    x = np.ascontiguousarray(np.asarray(inputs["L0_attn_x"], dtype=np.float32))
    base = {
        "wq": np.ascontiguousarray(inputs["L0_attn_Wq"], dtype=np.float32),
        "wk": np.ascontiguousarray(inputs["L0_attn_Wk"], dtype=np.float32),
        "wv": np.ascontiguousarray(inputs["L0_attn_Wv"], dtype=np.float32),
        "wo": np.ascontiguousarray(inputs["L0_attn_Wo"], dtype=np.float32),
        "wg": np.ascontiguousarray(inputs["L0_attn_Wg"], dtype=np.float32),
        "wog": np.ascontiguousarray(inputs["L0_attn_Wog"], dtype=np.float32),
        "wd": np.ascontiguousarray(inputs["L0_attn_Wd"], dtype=np.float32),
        "bd": np.ascontiguousarray(np.asarray(inputs["L0_attn_bd"], dtype=np.float32).reshape(H, 1)),
        "normw": np.ascontiguousarray(np.asarray(inputs["L0_attn_norm_w"], dtype=np.float32).reshape(8, 128)),
        "wup": np.ascontiguousarray(inputs["L0_ffn_w_up"], dtype=np.float32),
        "wdown": np.ascontiguousarray(inputs["L0_ffn_w_down"], dtype=np.float32),
    }
    nc = build_program()
    in_maps = [dict(base, x=np.ascontiguousarray(x[i])) for i in range(8)]
    res = run_bass_kernel_spmd(nc, in_maps, core_ids=list(range(8)))
    out = np.stack([r["out"] for r in res.results], axis=0)
    return out.astype(np.float32)
